# version 04
# baseline (speedup 1.0000x reference)
import numpy as np
import ml_dtypes
from contextlib import ExitStack

import concourse.bass as bass
import concourse.mybir as mybir
from concourse.bass_utils import run_bass_kernel_spmd

F32 = mybir.dt.float32
BF16 = mybir.dt.bfloat16
U8 = mybir.dt.uint8
AF = mybir.ActivationFunctionType
ALU = mybir.AluOpType
AX = mybir.AxisListType
BFNP = ml_dtypes.bfloat16

D = 2048
H = 16
NC = 16
BLK = 256
TOKS = 2048
NBLK = TOKS // BLK
BIG = 1.0e9
GRP = 4
REPLICA_PAIRS = [[0, 1], [2, 3], [4, 5], [6, 7]]


class Prog:
    ENGINES = ("sync", "tensor", "vector", "scalar", "gpsimd")

    def __init__(self, nc, stack):
        self.nc = nc
        self.stack = stack
        self.ops = {e: [] for e in self.ENGINES}
        self.semh = {}
        self.cnt = {}

    def sem(self, name):
        if name not in self.semh:
            self.semh[name] = self.stack.enter_context(self.nc.semaphore(name))
            self.cnt[name] = 0
        return self.semh[name]

    def op(self, engine, fn, inc=None, by=1):
        if inc is not None:
            h = self.sem(inc)
            self.cnt[inc] += by
            self.ops[engine].append(lambda e, fn=fn, h=h, by=by: fn(e).then_inc(h, by))
            return self.cnt[inc]
        self.ops[engine].append(fn)
        return None

    def wait(self, engine, sem, value=None):
        if value is None:
            value = self.cnt.get(sem, 0)
        assert value <= self.cnt.get(sem, 0), (engine, sem, value)
        if value <= 0:
            return
        h = self.sem(sem)
        self.ops[engine].append(lambda e, h=h, v=value: e.wait_ge(h, v))

    def dma(self, engine, out, in_, inc):
        return self.op(engine, lambda e, o=out, i=in_: e.dma_start(out=o, in_=i),
                       inc=inc, by=16)

    def replay(self):
        with self.nc.Block() as block:
            for en in self.ENGINES:
                opl = self.ops[en]

                def body(eng, opl=opl):
                    for f in opl:
                        f(eng)
                getattr(block, en)(body)


class Ker:
    def __init__(self, chunks, use_coll, dbg=False):
        self.chunks = chunks
        self.use_coll = use_coll
        self.dbg = dbg
        self.TA = TOKS * chunks

    def build(self):
        nc = bass.Bass("TRN2", target_bir_lowering=False, debug=False)
        self.nc = nc
        TA = self.TA
        din = lambda n, sh: nc.dram_tensor(n, sh, F32, kind="ExternalInput").ap()
        self.x_in = din("x", [D, TA])
        self.wq = din("wq", [H, 128, D])
        self.wk = din("wk", [H, 128, D])
        self.wv = din("wv", [NC, 128, D])
        self.wg = din("wg", [NC, 128, D])
        self.wo = din("wo", [NC, 128, D])
        self.wf = din("wf", [NC, 128, D])
        self.vecs = din("vecs", [128, 5 * NC])
        self.slope = din("slope", [1, H])
        self.mask01 = din("mask01", [128, 1])
        self.arow = din("arow", [128, BLK])
        self.krow = din("krow", [128, BLK])
        self.idxT = din("idxT", [2 * 128, BLK])
        self.identw = din("identw", [128, 128])

        self.out_d = nc.dram_tensor("out", [D, TA], F32, kind="ExternalOutput").ap()
        spk = "ExternalOutput" if self.dbg else "Internal"
        self.attn_sp = nc.dram_tensor("attn_sp", [D, TA], BF16, kind=spk).ap()
        self.qdec_sp = nc.dram_tensor("qdec_sp", [D, TA], BF16, kind=spk).ap()
        self.gate_sp = nc.dram_tensor("gate_sp", [D, TA], BF16, kind=spk).ap()
        self.xn_sp = nc.dram_tensor("xn_sp", [D, TA], BF16, kind=spk).ap()
        self.x2_sp = nc.dram_tensor("x2_sp", [D, TA], F32, kind=spk).ap()
        if self.dbg:
            self.ln1_dp = nc.dram_tensor("ln1_dp", [128, NC * TOKS], BF16,
                                         kind="ExternalOutput").ap()
            self.vtm_dp = nc.dram_tensor("vtm_dp", [128, NC * D], BF16,
                                         kind="ExternalOutput").ap()
            self.sw_dp = nc.dram_tensor("sw_dp", [128, 64], F32,
                                        kind="ExternalOutput").ap()
            self.kvp_dp = nc.dram_tensor("kvp_dp", [128, H * 128], BF16,
                                         kind="ExternalOutput").ap()
            self.atc_dp = nc.dram_tensor("atc_dp", [D, TOKS], BF16,
                                         kind="ExternalOutput").ap()
            self.bd_dp = nc.dram_tensor("bd_dp", [128, 144], F32,
                                        kind="ExternalOutput").ap()
            self.wres_dp = nc.dram_tensor("wres_dp", [128, NC * D], BF16,
                                          kind="ExternalOutput").ap()
        self.kvf_d = nc.dram_tensor("kvf_d", [H * 128, 128], F32).ap()
        self.kvg_d = nc.dram_tensor("kvg_d", [2 * H * 128, 128], F32).ap()
        self.scl_d = nc.dram_tensor("scl_d", [128, 1], F32).ap()

        ctx = ExitStack()
        with ctx:
            sb = lambda n, sh, d: ctx.enter_context(nc.sbuf_tensor(n, sh, d))
            psb = lambda n, sh, d=F32: ctx.enter_context(nc.psum_tensor(n, sh, d))
            arA = sb("arA", [128, 65536], U8)[:]
            arB = sb("arB", [128, 65536], U8)[:]
            arC = sb("arC", [128, 49152], U8)[:]
            arD = sb("arD", [128, 8192], U8)[:]
            A2, B2v, C2, D2v = (a.bitcast(BF16) for a in (arA, arB, arC, arD))
            A4, C4 = arA.bitcast(F32), arC.bitcast(F32)

            self.ln1 = A2[:, 0:NC * TOKS]
            self.xt = [A4[:, 0:4096], A4[:, 4096:8192]]
            self.x2t = [A4[:, 8192:12288], A4[:, 12288:16384]]
            self.x22 = [A4[:, 0:8192], A4[:, 8192:16384]]
            self.vtm = B2v[:, 0:NC * D]
            self.wres = B2v[:, 0:NC * D]
            self.xblk = [C4[:, 0:2048], C4[:, 2048:4096]]
            self.xbb = [C2[:, 8192:10240], C2[:, 10240:12288]]
            self.sqb = [C2[:, 12288:14336], C2[:, 14336:16384]]
            self.wvsl = [C2[:, 0:8192], C2[:, 8192:16384]]
            self.qraw = C2[:, 0:2048]
            self.qdech = C2[:, 2048:4096]
            self.kfm = C2[:, 4096:6144]
            self.kkd = C2[:, 6144:8192]
            self.attnh = C2[:, 8192:10240]
            self.qkT = C2[:, 10240:10752]
            self.kkdT = C2[:, 10752:11008]
            self.wqh = C2[:, 11264:13312]
            self.wkh = C2[:, 13312:15360]
            self.attnt = [C2[:, 0:4096], C2[:, 4096:8192]]
            self.gatet = [C2[:, 8192:12288], C2[:, 12288:16384]]
            self.qdect = C2[:, 16384:20480]
            self.sqt = C2[:, 20480:24576]
            self.xn2 = [C2[:, 0:8192], C2[:, 8192:16384]]
            self.wfst = [C4[:, 8192:10240], C4[:, 10240:12288]]
            self.qdecs = D2v[:, 0:GRP * BLK]
            self.kdpats = D2v[:, GRP * BLK:2 * GRP * BLK]
            self.diags = D2v[:, 2 * GRP * BLK:4 * GRP * BLK]
            D4 = arD.bitcast(F32)
            self.rowsA = D4[0:1, 0:512]
            self.rowsE = D4[0:1, 512:1536]
            self.gatesb = D2v[:, 0:2048]

            self.cvec = sb("cvec", [128, 5 * NC], F32)[:]
            self.gscale = sb("gscale", [128, NC], F32)[:]
            self.bprime = sb("bprime", [128, NC], F32)[:]
            self.lnbb = sb("lnbb", [128, NC], BF16)[:]
            self.nsl = sb("nsl", [128, H], F32)[:]
            self.bdcol = sb("bdcol", [128, H], F32)[:]
            self.bdpow = sb("bdpow", [128, NBLK * H], F32)[:]
            self.onesb = sb("onesb", [128, 1], BF16)[:]
            self.onesf = sb("onesf", [1, 128], F32)[:]
            self.identb = sb("identb", [128, 128], BF16)[:]
            self.maskc = sb("maskc", [128, 1], F32)[:]
            self.arow_s = sb("arow_s", [128, BLK], F32)[:]
            self.krow_s = sb("krow_s", [128, BLK], F32)[:]
            self.idxT_s = sb("idxT_s", [128, 2 * BLK], F32)[:]
            self.slope_s = sb("slope_s", [1, H], F32)[:]
            self.kvst = sb("kvst", [128, H * 128], F32)[:]
            self.kvbf = sb("kvbf", [128, 128], BF16)[:]
            self.kvprevb = sb("kvprevb", [128, H * 128], BF16)[:]
            self.u1t = sb("u1t", [128, BLK], F32)[:]
            self.qbt = sb("qbt", [128, 2 * BLK], BF16)[:]
            self.sclrow = sb("sclrow", [1, 256], F32)[:]
            self.sclc = sb("sclc", [128, 4], F32)[:]
            self.srow = sb("srow", [1, 8], F32)[:]

            self.cpool = sb("cpool", [128, 16], F32)[:]
            self.cvals = [0.0, 1.0, 1e-5, 1e-6, -0.5, 1.0 / D] + \
                [256.0 * j for j in range(1, NBLK)]
            for i, v in enumerate(self.cvals):
                nc.const_aps.register(F32, v, self.cpool[:, i:i + 1])

            self.pb = [psb(f"pb{i}", [128, 512]) for i in range(6)]
            self.pbt = psb("pbt", [128, 256], BF16)
            self.pkv = psb("pkv", [128, 128])

            self.sbuf_left = nc.sbuf_bytes_remaining

            P = Prog(nc, ctx)
            self.P = P
            self.emit()
            P.replay()
        return nc

    def emit(self):
        P = self.P
        self._kkdT_n = 0
        self._qkT_n = 0
        self.emit_consts()
        for c in range(self.chunks):
            self.emit_ln1(c)
            self.emit_vproj(c)
            if self.dbg and c == 0:
                self.emit_dumps()
            for h in range(H):
                self.emit_head(c, h)
            self.emit_gate(c)
        if self.use_coll:
            self.emit_collective()
        else:
            pass
        for c in range(self.chunks):
            self.emit_wout_load(c)
            for t in range(NBLK):
                self.emit_e1(c, t)
        for c in range(self.chunks):
            self.emit_signw(c)
            if self.dbg and c == 0:
                self.emit_sw_dumps()
            for t2 in range(TOKS // 512):
                self.emit_e2(c, t2)
        P.wait("gpsimd", "out_st0")
        P.wait("gpsimd", "out_st1")
        P.op("gpsimd", lambda e: e.nop())

    def emit_consts(self):
        P, s = self.P, self
        P.dma("sync", s.cvec, s.vecs, inc="c_ld")
        P.dma("sync", s.slope_s, s.slope, inc="c_ld")
        P.dma("sync", s.maskc, s.mask01, inc="c_ld")
        P.dma("sync", s.arow_s, s.arow, inc="c_ld")
        P.dma("sync", s.krow_s, s.krow, inc="c_ld")
        P.dma("sync", s.idxT_s.rearrange("p (a c) -> p a c", a=2),
              s.idxT.rearrange("(a p) c -> p a c", p=128), inc="c_ld")
        P.dma("gpsimd", s.identb, s.identw, inc="c_ldg")
        P.op("gpsimd", lambda e: e.memset(s.onesb, 1.0))
        P.op("gpsimd", lambda e: e.memset(s.onesf, 1.0))
        P.op("gpsimd", lambda e: e.memset(s.kvst, 0.0))
        P.op("gpsimd", lambda e: e.memset(s.kvbf, 0.0))
        P.op("gpsimd", lambda e: e.memset(s.kvprevb, 0.0))
        P.op("gpsimd", lambda e: e.memset(s.bdpow[:, 0:16], 1.0))
        for i, v in enumerate(s.cvals):
            P.op("gpsimd", lambda e, i=i, v=v: e.memset(s.cpool[:, i:i + 1], v))
        P.op("gpsimd", lambda e: e.engine_nop(), inc="c_ms")
        P.wait("tensor", "c_ld")
        P.wait("tensor", "c_ldg")
        P.wait("tensor", "c_ms")
        P.op("tensor", lambda e: e.matmul(s.pb[5][:, 0:16], s.onesf, s.slope_s,
                                          start=True, stop=True), inc="c_pe")
        P.wait("vector", "c_pe")
        P.op("vector", lambda e: e.tensor_scalar_mul(s.nsl, s.pb[5][:, 0:16], -1.0),
             inc="c_ve")
        P.wait("scalar", "c_ve")
        P.op("scalar", lambda e: e.activation(s.bdcol, s.nsl, AF.Exp, scale=256.0),
             inc="c_se")
        for j in range(1, NBLK):
            P.op("scalar", lambda e, j=j: e.activation(
                s.bdpow[:, j * 16:(j + 1) * 16], s.nsl, AF.Exp, scale=256.0 * j),
                inc="c_se")

    def emit_ln1(self, c):
        P, s = self.P, self
        SB = 128
        nsb = TOKS // SB
        for sem in ("e2_se", "e2_pe", "e2_ve", "e1_ve", "e1_pe", "b1_pe", "b1_ve",
                    "b1_se", "b2_pe", "b2_se", "b2_sp0", "b2_sp1", "out_st0",
                    "out_st1", "e1_sp0", "e1_sp1", "c_ve", "c_se"):
            P.wait("sync", sem)
            P.wait("gpsimd", sem)
            P.wait("tensor", sem)
            P.wait("vector", sem)
            P.wait("scalar", sem)
        done_ve = {}
        done_pe = {}
        rows_done = {}
        for j in range(nsb):
            buf = j % 2
            t0 = c * TOKS + j * SB
            if j >= 2:
                P.wait("sync", "a1_ve", done_ve[j - 2])
                P.wait("gpsimd", "a1_pe", done_pe[j - 2])
            src = s.x_in[:, t0:t0 + SB].rearrange("(n p) b -> p n b", p=128)
            P.dma("sync", s.xblk[buf].rearrange("p (n b) -> p n b", b=SB), src,
                  inc=f"a1_ldx{buf}")
            nx = P.cnt[f"a1_ldx{buf}"]
            P.dma("gpsimd", s.xbb[buf].rearrange("p (n b) -> p n b", b=SB), src,
                  inc=f"a1_ldxb{buf}")
            nxb = P.cnt[f"a1_ldxb{buf}"]
            P.wait("scalar", f"a1_ldxb{buf}", nxb)
            if j >= 2:
                P.wait("scalar", "a1_pe", done_pe[j - 2])
            P.op("scalar", lambda e, buf=buf: e.activation(
                s.sqb[buf], s.xbb[buf], AF.Square), inc="a1_se")
            nse = P.cnt["a1_se"]
            P.wait("tensor", f"a1_ldxb{buf}", nxb)
            P.wait("tensor", "a1_se", nse)
            if j >= 1:
                P.wait("tensor", "a1_rows", rows_done[j - 1])
            for n in range(NC):
                P.op("tensor", lambda e, n=n, buf=buf: e.matmul(
                    s.pb[4][0:1, 0:SB], s.onesb,
                    s.xbb[buf][:, n * SB:(n + 1) * SB],
                    start=(n == 0), stop=(n == NC - 1)))
            for n in range(NC):
                P.op("tensor", lambda e, n=n, buf=buf: e.matmul(
                    s.pb[4][0:1, SB:2 * SB], s.onesb,
                    s.sqb[buf][:, n * SB:(n + 1) * SB],
                    start=(n == 0), stop=(n == NC - 1)),
                    inc=("a1_pe" if n == NC - 1 else None))
            done_pe[j] = P.cnt["a1_pe"]
            P.wait("vector", "a1_pe", done_pe[j])
            P.op("vector", lambda e: e.tensor_scalar_mul(
                s.rowsA[:, 0:SB], s.pb[4][0:1, 0:SB], 1.0 / D))
            P.op("vector", lambda e: e.tensor_tensor(
                s.rowsE[:, 0:SB], s.rowsA[:, 0:SB], s.rowsA[:, 0:SB], ALU.mult))
            P.op("vector", lambda e: e.scalar_tensor_tensor(
                s.rowsA[:, SB:2 * SB], s.pb[4][0:1, SB:2 * SB], 1.0 / D,
                s.rowsE[:, 0:SB], ALU.mult, ALU.subtract), inc="a1_ve")
            P.wait("scalar", "a1_ve", P.cnt["a1_ve"])
            P.op("scalar", lambda e: e.activation(
                s.rowsA[:, SB:2 * SB], s.rowsA[:, SB:2 * SB], AF.Ln, bias=1e-5))
            P.op("scalar", lambda e: e.activation(
                s.rowsA[:, SB:2 * SB], s.rowsA[:, SB:2 * SB], AF.Exp, scale=-0.5),
                inc="a1_se")
            P.wait("vector", "a1_se", P.cnt["a1_se"])
            P.op("vector", lambda e: e.scalar_tensor_tensor(
                s.rowsA[:, 0:SB], s.rowsA[:, 0:SB], -1.0,
                s.rowsA[:, SB:2 * SB], ALU.mult, ALU.mult), inc="a1_rows")
            rows_done[j] = P.cnt["a1_rows"]
            P.wait("tensor", "a1_rows", rows_done[j])
            if j >= 1:
                P.wait("tensor", "a1_ve", done_ve[j - 1])
            P.op("tensor", lambda e: e.matmul(
                s.pb[5][:, 0:2 * SB], s.onesf, s.rowsA[:, 0:2 * SB],
                start=True, stop=True), inc="a1_bc")
            P.wait("vector", "a1_bc", P.cnt["a1_bc"])
            P.wait("vector", f"a1_ldx{buf}", nx)
            for n in range(NC):
                dst = s.ln1[:, n * TOKS + j * SB: n * TOKS + (j + 1) * SB]
                P.op("vector", lambda e, n=n, buf=buf, dst=dst: e.tensor_tensor(
                    dst, s.xblk[buf][:, n * SB:(n + 1) * SB],
                    s.pb[5][:, SB:2 * SB], ALU.mult))
                P.op("vector", lambda e, dst=dst: e.tensor_tensor(
                    dst, dst, s.pb[5][:, 0:SB], ALU.add),
                    inc=("a1_ve" if n == NC - 1 else None))
            done_ve[j] = P.cnt["a1_ve"]
        self.ln1_done = P.cnt["a1_ve"]

    def emit_vproj(self, c):
        P, s = self.P, self
        for sem in ("a1_pe", "a1_ve", "b1_pe", "b1_ve", "b1_se", "b2_pe",
                    "e2_pe"):
            P.wait("gpsimd", sem)
        evac_pe = {}
        for vf in range(4):
            buf = vf % 2
            if vf >= 2:
                P.wait("gpsimd", "b0_pe", evac_pe[vf - 2])
            P.dma("gpsimd", s.wvsl[buf].rearrange("p (k cc) -> p k cc", cc=512),
                  s.wv[:, :, vf * 512:(vf + 1) * 512].rearrange("k p cc -> p k cc"),
                  inc=f"b0_wld{buf}")
            nw = P.cnt[f"b0_wld{buf}"]
            P.wait("tensor", f"b0_wld{buf}", nw)
            if vf == 0:
                P.wait("tensor", "a1_ve", self.ln1_done)
                P.wait("tensor", "b1_pe")
                P.wait("tensor", "b2_pe")
                P.wait("tensor", "b2_se")
                P.wait("tensor", "e1_pe")
                P.wait("tensor", "e2_pe")
                P.wait("tensor", "e2_ve")
            for tc in range(NC):
                pbank = s.pb[tc % 2]
                if tc >= 2:
                    P.wait("tensor", "b0_se", P.cnt["b0_se"] - 1)
                for k in range(NC):
                    P.op("tensor", lambda e, k=k, tc=tc, buf=buf, pbank=pbank: e.matmul(
                        pbank[:, 0:512],
                        s.ln1[:, k * TOKS + tc * 128:k * TOKS + (tc + 1) * 128],
                        s.wvsl[buf][:, k * 512:(k + 1) * 512],
                        start=(k == 0), stop=(k == NC - 1)),
                        inc=("b0_pe" if k == NC - 1 else None))
                P.wait("scalar", "b0_pe", P.cnt["b0_pe"])
                P.op("scalar", lambda e, tc=tc, vf=vf, pbank=pbank: e.activation(
                    s.vtm[:, tc * D + vf * 512: tc * D + (vf + 1) * 512],
                    pbank[:, 0:512], AF.Silu), inc="b0_se")
            evac_pe[vf] = P.cnt["b0_pe"]
        self.v_done = P.cnt["b0_se"]

    def emit_decay_group(self, g):
        P, s = self.P, self
        P.wait("scalar", "b1_pe")
        P.wait("scalar", "b1_ve")
        P.wait("scalar", "a1_bc")
        P.wait("scalar", "a1_ve")
        P.wait("scalar", "b2_sp0")
        P.wait("scalar", "b2_sp1")
        for i in range(GRP):
            h = g * GRP + i
            sc = s.nsl[:, h:h + 1]
            P.op("scalar", lambda e, i=i, sc=sc: e.activation(
                s.qdecs[:, i * BLK:(i + 1) * BLK], s.arow_s, AF.Exp, scale=sc))
            P.op("scalar", lambda e, i=i, sc=sc: e.activation(
                s.kdpats[:, i * BLK:(i + 1) * BLK], s.krow_s, AF.Exp, scale=sc))
            P.op("scalar", lambda e, i=i, sc=sc: e.activation(
                s.diags[:, i * 2 * BLK:(i + 1) * 2 * BLK], s.idxT_s, AF.Exp,
                scale=sc), inc="b1_se")

    def emit_head(self, c, h):
        P, s = self.P, self
        if h % GRP == 0:
            self.emit_decay_group((h % H) // GRP)
        hg = h % GRP

        P.wait("gpsimd", "b1_pe")
        if h == 0:
            P.wait("gpsimd", "b0_pe")
            P.wait("gpsimd", "b0_se")
            P.wait("gpsimd", "a1_se")
            P.wait("gpsimd", "e1_se")
            P.wait("gpsimd", "e1_ve")
            P.wait("gpsimd", "e2_pe")
        P.dma("gpsimd", s.wqh, s.wq[h], inc="b1_wld")
        P.dma("gpsimd", s.wkh, s.wk[h], inc="b1_wld")
        nw = P.cnt["b1_wld"]

        P.wait("tensor", "b1_wld", nw)
        if h == 0:
            P.wait("tensor", "b0_se", self.v_done)
        evac = {}
        for proj in range(2):
            banks = (s.pb[0], s.pb[1]) if proj == 0 else (s.pb[2], s.pb[3])
            dstr = s.qraw if proj == 0 else s.kfm
            wt = s.wqh if proj == 0 else s.wkh
            for qt in range(4):
                bank = banks[qt % 2]
                P.wait("tensor", "b1_se", evac.get((proj, qt - 2),
                                                   self._b1_proj_guard(proj)))
                if qt == 0 and proj == 0:
                    P.wait("tensor", "b1_sp")
                for k in range(NC):
                    P.op("tensor", lambda e, k=k, qt=qt, wt=wt, bank=bank: e.matmul(
                        bank[:, 0:512], wt[:, k * 128:(k + 1) * 128],
                        s.ln1[:, k * TOKS + qt * 512:k * TOKS + (qt + 1) * 512],
                        start=(k == 0), stop=(k == NC - 1)),
                        inc=("b1_pe" if k == NC - 1 else None))
                npe = P.cnt["b1_pe"]
                P.wait("scalar", "b1_pe", npe)
                if qt == 0 and proj == 0:
                    P.wait("scalar", "b1_sp")
                    P.wait("scalar", "b1_ve")
                n_ev = P.op("scalar", lambda e, qt=qt, bank=bank, dstr=dstr:
                            e.activation(dstr[:, qt * 512:(qt + 1) * 512],
                                         bank[:, 0:512], AF.Silu), inc="b1_se")
                evac[(proj, qt)] = n_ev
        self._b1_q_evac = evac[(0, 3)]
        self._b1_k_evac = evac[(1, 3)]

        P.wait("vector", "b1_sp")
        P.wait("vector", "b1_se", self._b1_q_evac)
        rep = lambda ap: ap.unsqueeze(1).broadcast_to([128, NBLK, BLK])
        P.op("vector", lambda e: e.tensor_tensor(
            s.qdech.rearrange("p (r b) -> p r b", b=BLK),
            s.qraw.rearrange("p (r b) -> p r b", b=BLK),
            rep(s.qdecs[:, hg * BLK:(hg + 1) * BLK]), ALU.mult), inc="b1_ve")
        P.wait("vector", "b1_se", self._b1_k_evac)
        P.op("vector", lambda e: e.tensor_tensor(
            s.kkd.rearrange("p (r b) -> p r b", b=BLK),
            s.kfm.rearrange("p (r b) -> p r b", b=BLK),
            rep(s.kdpats[:, hg * BLK:(hg + 1) * BLK]), ALU.mult), inc="b1_ve")
        n_dec = P.cnt["b1_ve"]

        hs = h * 128
        for j in range(NBLK):
            t0 = j * BLK
            P.wait("scalar", "b1_ve")
            P.wait("scalar", "b1_pe")
            nbf = P.op("scalar", lambda e: e.copy(
                s.kvbf, s.kvst[:, hs:hs + 128]), inc="b1_se")
            P.wait("tensor", "b1_ve", n_dec)
            P.wait("tensor", "b1_ve", self._kkdT_n)
            P.op("tensor", lambda e, t0=t0: e.transpose(
                self.pbt[:][:, 0:128], s.kkd[:, t0:t0 + 128], s.identb))
            ntr = P.op("tensor", lambda e, t0=t0: e.transpose(
                self.pbt[:][:, 128:256], s.kkd[:, t0 + 128:t0 + 256], s.identb),
                inc="b1_pe")
            P.wait("tensor", "b1_ve", self._qkT_n)
            P.op("tensor", lambda e, t0=t0: e.matmul(
                s.pb[4][:, 0:256], s.kfm[:, t0:t0 + 128],
                s.qraw[:, t0:t0 + 256], start=True, stop=True))
            nqk = P.op("tensor", lambda e, t0=t0: e.matmul(
                s.pb[4][:, 256:512], s.kfm[:, t0 + 128:t0 + 256],
                s.qraw[:, t0:t0 + 256], start=True, stop=True), inc="b1_pe")
            P.wait("vector", "b1_pe", ntr)
            self._kkdT_n = P.op("vector", lambda e: e.tensor_copy(
                s.kkdT, self.pbt[:, 0:256]), inc="b1_ve")
            P.wait("vector", "b1_pe", nqk)
            P.op("vector", lambda e: e.tensor_tensor(
                s.qkT[:, 0:256], s.pb[4][:, 0:256],
                s.diags[:, hg * 2 * BLK:hg * 2 * BLK + 256], ALU.mult))
            self._qkT_n = P.op("vector", lambda e: e.tensor_tensor(
                s.qkT[:, 256:512], s.pb[4][:, 256:512],
                s.diags[:, hg * 2 * BLK + 256:hg * 2 * BLK + 512], ALU.mult),
                inc="b1_ve")
            P.wait("tensor", "b1_ve", self._qkT_n)
            P.wait("tensor", "b1_se", nbf)
            if True:
                P.wait("tensor", "b1_se")
            vt0 = (2 * j) * D + hs
            vt1 = (2 * j + 1) * D + hs
            P.op("tensor", lambda e, vt0=vt0: e.matmul(
                s.pb[5][:, 0:256], s.vtm[:, vt0:vt0 + 128], s.qkT[:, 0:256],
                start=True, stop=False))
            P.op("tensor", lambda e, vt1=vt1: e.matmul(
                s.pb[5][:, 0:256], s.vtm[:, vt1:vt1 + 128], s.qkT[:, 256:512],
                start=False, stop=False))
            nat = P.op("tensor", lambda e, t0=t0: e.matmul(
                s.pb[5][:, 0:256], s.kvbf, s.qdech[:, t0:t0 + 256],
                start=False, stop=True), inc="b1_pe")
            P.wait("tensor", "b1_ve", self._kkdT_n)
            P.op("tensor", lambda e, vt0=vt0: e.matmul(
                s.pkv[:], s.kkdT[:, 0:128], s.vtm[:, vt0:vt0 + 128],
                start=True, stop=False))
            nkv = P.op("tensor", lambda e, vt1=vt1: e.matmul(
                s.pkv[:], s.kkdT[:, 128:256], s.vtm[:, vt1:vt1 + 128],
                start=False, stop=True), inc="b1_pe")
            P.wait("scalar", "b1_pe", nat)
            P.op("scalar", lambda e, t0=t0: e.copy(
                s.attnh[:, t0:t0 + 256], s.pb[5][:, 0:256]), inc="b1_se")
            P.wait("vector", "b1_pe", nkv)
            P.op("vector", lambda e: e.scalar_tensor_tensor(
                s.kvst[:, hs:hs + 128], s.kvst[:, hs:hs + 128],
                s.bdcol[:, h:h + 1], s.pkv[:], ALU.mult, ALU.add), inc="b1_ve")
        self._kv_done = P.cnt["b1_ve"]

        P.wait("sync", "b1_se")
        P.dma("sync", s.attn_sp[h * 128:(h + 1) * 128, c * TOKS:(c + 1) * TOKS],
              s.attnh, inc="b1_sp")
        if self.use_coll:
            P.wait("sync", "b1_ve", n_dec)
            P.dma("sync", s.qdec_sp[h * 128:(h + 1) * 128, c * TOKS:(c + 1) * TOKS],
                  s.qdech, inc="b1_sp")

    def _b1_proj_guard(self, proj):
        return self.P.cnt.get("b1_se", 0)

    def emit_gate(self, c):
        P, s = self.P, self
        P.wait("gpsimd", "b1_pe")
        P.wait("gpsimd", "b1_ve")
        for k in range(NC):
            P.dma("gpsimd", s.wres[:, k * D:(k + 1) * D], s.wg[k], inc="b2_wld")
        nw = P.cnt["b2_wld"]
        P.wait("tensor", "b2_wld", nw)
        P.wait("tensor", "b1_se")
        P.wait("tensor", "b1_ve")
        P.wait("scalar", "b1_ve")
        P.wait("scalar", "b1_pe")
        evacs = []
        for th in range(2):
            for f in range(NC):
                fi = th * NC + f
                b0, b1 = (s.pb[0], s.pb[1]) if f % 2 == 0 else (s.pb[2], s.pb[3])
                if fi >= 2:
                    P.wait("tensor", "b2_se", evacs[fi - 2])
                for k in range(NC):
                    P.op("tensor", lambda e, k=k, f=f, th=th, b0=b0: e.matmul(
                        b0[:, 0:512], s.wres[:, k * D + f * 128:k * D + (f + 1) * 128],
                        s.ln1[:, k * TOKS + th * 1024:k * TOKS + th * 1024 + 512],
                        start=(k == 0), stop=(k == NC - 1)))
                    P.op("tensor", lambda e, k=k, f=f, th=th, b1=b1: e.matmul(
                        b1[:, 0:512], s.wres[:, k * D + f * 128:k * D + (f + 1) * 128],
                        s.ln1[:, k * TOKS + th * 1024 + 512:k * TOKS + th * 1024 + 1024],
                        start=(k == 0), stop=(k == NC - 1)),
                        inc=("b2_pe" if k == NC - 1 else None))
                npe = P.cnt["b2_pe"]
                gb = s.gatesb[:, (f % 2) * 1024:(f % 2) * 1024 + 1024]
                P.wait("scalar", "b2_pe", npe)
                P.wait("scalar", f"b2_sp{f % 2}",
                       P.cnt.get(f"b2_sp{f % 2}", 0))
                P.op("scalar", lambda e, gb=gb, b0=b0: e.activation(
                    gb[:, 0:512], b0[:, 0:512], AF.Sigmoid))
                nse = P.op("scalar", lambda e, gb=gb, b1=b1: e.activation(
                    gb[:, 512:1024], b1[:, 0:512], AF.Sigmoid), inc="b2_se")
                evacs.append(nse)
                P.wait("sync", "b2_se", nse)
                P.dma("sync",
                      s.gate_sp[f * 128:(f + 1) * 128,
                                c * TOKS + th * 1024:c * TOKS + th * 1024 + 1024],
                      gb, inc=f"b2_sp{f % 2}")

    def emit_collective(self):
        P, s = self.P, self
        P.wait("gpsimd", "b1_ve")
        P.dma("gpsimd", s.kvf_d.rearrange("(h p) d -> p h d", p=128),
              s.kvst.rearrange("p (h d) -> p h d", d=128), inc="kv_sp")
        P.wait("gpsimd", "kv_sp")
        P.op("gpsimd", lambda e: e.collective_compute(
            "AllGather", ALU.bypass, replica_groups=REPLICA_PAIRS,
            ins=[s.kvf_d], outs=[s.kvg_d]), inc="cc_s")
        P.wait("gpsimd", "cc_s")
        P.dma("gpsimd", s.kvprevb.rearrange("p (h d) -> p h d", d=128),
              s.kvg_d[0:H * 128, :].rearrange("(h p) d -> p h d", p=128),
              inc="kv_ld")
        P.wait("vector", "kv_ld")
        P.op("vector", lambda e: e.tensor_scalar_mul(s.kvprevb, s.kvprevb, s.maskc),
             inc="kv_ve")
        if self.dbg:
            P.wait("sync", "kv_ve")
            P.dma("sync", s.kvp_dp, s.kvprevb, inc="dbg_st")
            P.wait("sync", "c_ve")
            P.wait("sync", "c_se")
            P.dma("sync", s.bd_dp[:, 0:16], s.bdcol, inc="dbg_st")
            P.dma("sync", s.bd_dp[:, 16:144], s.bdpow, inc="dbg_st")

    def emit_wout_load(self, c):
        if c > 0:
            return
        P, s = self.P, self
        P.wait("gpsimd", "b2_pe")
        for k in range(NC):
            P.dma("gpsimd", s.wres[:, k * D:(k + 1) * D], s.wo[k], inc="wo_ld")
        self._wo_n = P.cnt["wo_ld"]

    def emit_e1(self, c, t):
        P, s = self.P, self
        buf = t % 2
        off = c * TOKS + t * BLK
        use_corr = self.use_coll

        if t == 0:
            for sem in ("b1_pe", "b1_ve", "b1_se", "b2_pe", "b2_se", "a1_ve",
                        "a1_pe", "b2_sp0", "b2_sp1", "b1_sp", "e1_ve", "e1_pe",
                        "e1_se", "e1_sp0", "e1_sp1"):
                P.wait("sync", sem)
                P.wait("tensor", sem)
                P.wait("vector", sem)
                P.wait("scalar", sem)
            self._e1_free = {}
        else:
            if t >= 2:
                P.wait("sync", "e1_ve", self._e1_free[t - 2])
                P.wait("sync", f"e1_sp{buf}", self._e1_sp.get(t - 2, 0))
        fm = lambda apdram: apdram[:, off:off + BLK].rearrange(
            "(n p) b -> p n b", p=128)
        P.dma("sync", s.xt[buf].rearrange("p (n b) -> p n b", b=BLK),
              fm(s.x_in), inc=f"e1_ldx{buf}")
        nldx = P.cnt[f"e1_ldx{buf}"]
        P.dma("sync", s.attnt[buf].rearrange("p (n b) -> p n b", b=BLK),
              fm(s.attn_sp), inc=f"e1_lda{buf}")
        nlda = P.cnt[f"e1_lda{buf}"]
        if t >= 2:
            P.wait("sync", "e1_se", self._sq2_se.get(t - 2, 0))
        P.dma("sync", s.gatet[buf].rearrange("p (n b) -> p n b", b=BLK),
              fm(s.gate_sp), inc=f"e1_ldg{buf}")
        nldg = P.cnt[f"e1_ldg{buf}"]
        if use_corr:
            if t >= 1:
                P.wait("sync", "e1_ve", self._corr_done.get(t - 1, 0))
            else:
                self._corr_done = {}
            P.dma("sync", s.qdect.rearrange("p (n b) -> p n b", b=BLK),
                  fm(s.qdec_sp), inc="e1_ldq")
            nldq = P.cnt["e1_ldq"]

        if use_corr:
            P.wait("vector", "e1_ldq", nldq)
            P.wait("vector", f"e1_lda{buf}", nlda)
            if self.use_coll:
                P.wait("vector", "kv_ve")
            for h in range(H):
                qb = s.qbt[:, (h % 2) * BLK:(h % 2) * BLK + BLK]
                if h >= 2:
                    P.wait("vector", "e1_pe", self._corr_mm[h - 2])
                else:
                    if h == 0:
                        self._corr_prev = getattr(self, "_corr_mm", {})
                        self._corr_mm = {}
                        self._corr_tt = {}
                        if t >= 1:
                            P.wait("tensor", "e1_ve", self._corr_done[t - 1])
                    P.wait("vector", "e1_pe", self._corr_prev.get(H - 2 + h, 0))
                P.op("vector", lambda e, h=h, qb=qb: e.tensor_scalar_mul(
                    qb, s.qdect[:, h * BLK:(h + 1) * BLK],
                    s.bdpow[:, t * 16 + h:t * 16 + h + 1]), inc="e1_ve")
                nqb = P.cnt["e1_ve"]
                pc = s.pb[h % 2]
                P.wait("tensor", "e1_ve", nqb)
                if h >= 2:
                    P.wait("tensor", "e1_ve", self._corr_tt[h - 2])
                if t == 0 and h < 2 and self.use_coll:
                    P.wait("tensor", "kv_ve")
                self._corr_mm[h] = P.op("tensor", lambda e, h=h, qb=qb, pc=pc:
                                        e.matmul(pc[:, 0:256],
                                                 s.kvprevb[:, h * 128:(h + 1) * 128],
                                                 qb, start=True, stop=True),
                                        inc="e1_pe")
                P.wait("vector", "e1_pe", self._corr_mm[h])
                self._corr_tt[h] = P.op("vector", lambda e, h=h, pc=pc: e.tensor_tensor(
                    s.attnt[buf][:, h * BLK:(h + 1) * BLK],
                    s.attnt[buf][:, h * BLK:(h + 1) * BLK],
                    pc[:, 0:256], ALU.add), inc="e1_ve")
            self._corr_done[t] = P.cnt["e1_ve"]
        ncorr = P.cnt.get("e1_ve", 0)

        P.wait("scalar", f"e1_lda{buf}", nlda)
        if use_corr:
            P.wait("scalar", "e1_ve", ncorr)
        if t >= 1:
            P.wait("scalar", "e1_pe", self._sq_cons.get(t - 1, 0))
        else:
            self._sq_cons = {}
        nsq = P.op("scalar", lambda e, buf=buf: e.activation(
            s.sqt, s.attnt[buf], AF.Square), inc="e1_se")
        P.wait("tensor", "e1_se", nsq)
        if t >= 1:
            P.wait("tensor", "e1_rows", self._e1_rows.get(t - 1, 0))
        else:
            self._e1_rows = {}
        for n in range(NC):
            P.op("tensor", lambda e, n=n: e.matmul(
                s.pb[4][0:1, 0:BLK], s.onesb, s.sqt[:, n * BLK:(n + 1) * BLK],
                start=(n == 0), stop=(n == NC - 1)),
                inc=("e1_pe" if n == NC - 1 else None))
        nssq = P.cnt["e1_pe"]
        P.wait("scalar", "e1_pe", nssq)
        P.op("scalar", lambda e: e.activation(
            s.rowsE[:, 0:BLK], s.pb[4][0:1, 0:BLK], AF.Ln, scale=1.0 / D, bias=1e-6))
        nrow1 = P.op("scalar", lambda e: e.activation(
            s.rowsE[:, 0:BLK], s.rowsE[:, 0:BLK], AF.Exp, scale=-0.5), inc="e1_se")
        P.wait("tensor", "e1_se", nrow1)
        if t >= 1:
            P.wait("tensor", "e1_ve", self._e1_free.get(t - 1, 0))
        else:
            self._z_done = {}
        nbc1 = P.op("tensor", lambda e: e.matmul(
            s.pb[5][:, 0:BLK], s.onesf, s.rowsE[:, 0:BLK], start=True, stop=True),
            inc="e1_pe")
        if self.dbg:
            P.wait("sync", "e1_ve", ncorr)
            P.dma("sync", self.atc_dp[:, t * BLK:(t + 1) * BLK]
                  .rearrange("(n p) b -> p n b", p=128),
                  s.attnt[buf].rearrange("p (n b) -> p n b", b=BLK), inc="dbg_at")
            P.wait("vector", "dbg_at")
        P.wait("vector", "e1_pe", nbc1)
        P.wait("vector", "e1_se", nsq)
        P.wait("vector", f"e1_ldg{buf}", nldg)
        for n in range(NC):
            P.op("vector", lambda e, n=n, buf=buf: e.scalar_tensor_tensor(
                s.attnt[buf][:, n * BLK:(n + 1) * BLK],
                s.attnt[buf][:, n * BLK:(n + 1) * BLK],
                s.cvec[:, n:n + 1], s.pb[5][:, 0:BLK], ALU.mult, ALU.mult))
        P.op("vector", lambda e, buf=buf: e.tensor_tensor(
            s.attnt[buf], s.attnt[buf], s.gatet[buf], ALU.mult), inc="e1_ve")
        self._z_done[t] = P.cnt["e1_ve"]
        nz = self._z_done[t]

        P.wait("tensor", "e1_ve", nz)
        P.wait("tensor", "wo_ld", self._wo_n)
        nf = {}
        for f in range(NC):
            pob = s.pb[2 + (f % 2)]
            if f >= 2:
                P.wait("tensor", "e1_ve", nf[f - 2][1])
            for k in range(NC):
                P.op("tensor", lambda e, k=k, f=f, buf=buf, pob=pob: e.matmul(
                    pob[:, 0:256], s.wres[:, k * D + f * 128:k * D + (f + 1) * 128],
                    s.attnt[buf][:, k * BLK:(k + 1) * BLK],
                    start=(k == 0), stop=(k == NC - 1)),
                    inc=("e1_pe" if k == NC - 1 else None))
            npe = P.cnt["e1_pe"]
            P.wait("vector", "e1_pe", npe)
            if f == 0:
                P.wait("vector", f"e1_ldx{buf}", nldx)
                if t >= 2:
                    P.wait("vector", f"e1_sp{buf}", self._e1_sp.get(t - 2, 0))
            nev = P.op("vector", lambda e, f=f, buf=buf, pob=pob: e.tensor_tensor(
                s.x2t[buf][:, f * BLK:(f + 1) * BLK],
                s.xt[buf][:, f * BLK:(f + 1) * BLK], pob[:, 0:256], ALU.add),
                inc="e1_ve")
            nf[f] = (npe, nev)
        nx2 = nf[NC - 1][1]

        P.wait("vector", "e1_ve", nx2)
        nx2b = P.op("vector", lambda e, buf=buf: e.tensor_copy(
            s.gatet[buf], s.x2t[buf]), inc="e1_ve")
        P.wait("scalar", "e1_ve", nx2b)
        nsq2 = P.op("scalar", lambda e, buf=buf: e.activation(
            s.sqt, s.gatet[buf], AF.Square), inc="e1_se")
        self._sq2_se = getattr(self, "_sq2_se", {})
        self._sq2_se[t] = nsq2
        P.wait("tensor", "e1_ve", nx2b)
        P.wait("tensor", "e1_se", nsq2)
        for n in range(NC):
            P.op("tensor", lambda e, n=n, buf=buf: e.matmul(
                s.pb[4][0:1, 0:BLK], s.onesb, s.gatet[buf][:, n * BLK:(n + 1) * BLK],
                start=(n == 0), stop=(n == NC - 1)))
        for n in range(NC):
            P.op("tensor", lambda e, n=n: e.matmul(
                s.pb[4][0:1, BLK:2 * BLK], s.onesb, s.sqt[:, n * BLK:(n + 1) * BLK],
                start=(n == 0), stop=(n == NC - 1)),
                inc=("e1_pe" if n == NC - 1 else None))
        nS = P.cnt["e1_pe"]
        self._sq_cons[t] = nS
        P.wait("vector", "e1_pe", nS)
        P.op("vector", lambda e: e.tensor_scalar_mul(
            s.rowsE[:, 256:512], s.pb[4][0:1, 0:BLK], 1.0 / D))
        P.op("vector", lambda e: e.tensor_tensor(
            s.rowsE[:, 0:256], s.rowsE[:, 256:512], s.rowsE[:, 256:512], ALU.mult))
        P.op("vector", lambda e: e.scalar_tensor_tensor(
            s.rowsE[:, 0:256], s.pb[4][0:1, BLK:2 * BLK], 1.0 / D,
            s.rowsE[:, 0:256], ALU.mult, ALU.subtract), inc="e1_ve")
        nvar = P.cnt["e1_ve"]
        P.wait("scalar", "e1_ve", nvar)
        P.op("scalar", lambda e: e.activation(
            s.rowsE[:, 512:768], s.rowsE[:, 0:256], AF.Ln, bias=1e-5))
        P.op("scalar", lambda e: e.activation(
            s.rowsE[:, 512:768], s.rowsE[:, 512:768], AF.Exp, scale=-0.5),
            inc="e1_se")
        nr = P.cnt["e1_se"]
        P.wait("vector", "e1_se", nr)
        P.op("vector", lambda e: e.tensor_tensor(
            s.rowsE[:, 768:1024], s.rowsE[:, 512:768], s.rowsE[:, 512:768],
            ALU.mult))
        P.op("vector", lambda e: e.tensor_tensor(
            s.rowsE[:, 768:1024], s.rowsE[:, 0:256], s.rowsE[:, 768:1024],
            ALU.mult), inc="e1_ve")
        nt_ = P.cnt["e1_ve"]
        P.wait("scalar", "e1_ve", nt_)
        P.op("scalar", lambda e: e.activation(
            s.rowsE[:, 768:1024], s.rowsE[:, 768:1024], AF.Ln, bias=1e-5))
        P.op("scalar", lambda e: e.activation(
            s.rowsE[:, 768:1024], s.rowsE[:, 768:1024], AF.Exp, scale=-0.5),
            inc="e1_se")
        nru = P.cnt["e1_se"]
        P.wait("vector", "e1_se", nru)
        P.op("vector", lambda e: e.tensor_tensor(
            s.rowsE[:, 512:768], s.rowsE[:, 512:768], s.rowsE[:, 768:1024],
            ALU.mult))
        P.op("vector", lambda e: e.scalar_tensor_tensor(
            s.rowsE[:, 768:1024], s.rowsE[:, 256:512], -1.0,
            s.rowsE[:, 512:768], ALU.mult, ALU.mult), inc="e1_rows")
        self._e1_rows[t] = P.cnt["e1_rows"]
        P.wait("tensor", "e1_rows", self._e1_rows[t])
        P.wait("tensor", "e1_ve", nz)
        nbc2 = P.op("tensor", lambda e: e.matmul(
            s.pb[5][:, 0:512], s.onesf, s.rowsE[:, 512:1024],
            start=True, stop=True), inc="e1_pe")
        P.wait("vector", "e1_pe", nbc2)
        for n in range(NC):
            P.op("vector", lambda e, n=n, buf=buf: e.scalar_tensor_tensor(
                s.u1t, s.x2t[buf][:, n * BLK:(n + 1) * BLK],
                s.cvec[:, 3 * NC + n:3 * NC + n + 1], s.pb[5][:, 0:256],
                ALU.mult, ALU.mult))
            P.op("vector", lambda e, n=n, buf=buf: e.scalar_tensor_tensor(
                s.attnt[buf][:, n * BLK:(n + 1) * BLK], s.pb[5][:, 256:512],
                s.cvec[:, 3 * NC + n:3 * NC + n + 1], s.u1t,
                ALU.mult, ALU.add))
            P.op("vector", lambda e, n=n, buf=buf: e.tensor_scalar_add(
                s.attnt[buf][:, n * BLK:(n + 1) * BLK],
                s.attnt[buf][:, n * BLK:(n + 1) * BLK],
                s.cvec[:, 4 * NC + n:4 * NC + n + 1]),
                inc=("e1_ve" if n == NC - 1 else None))
        nxn = P.cnt["e1_ve"]
        self._e1_free[t] = nxn

        P.wait("sync", "e1_ve", nxn)
        P.dma("sync", s.xn_sp[:, off:off + BLK].rearrange("(n p) b -> p n b", p=128),
              s.attnt[buf].rearrange("p (n b) -> p n b", b=BLK), inc=f"e1_sp{buf}")
        P.dma("sync", s.x2_sp[:, off:off + BLK].rearrange("(n p) b -> p n b", p=128),
              s.x2t[buf].rearrange("p (n b) -> p n b", b=BLK), inc=f"e1_sp{buf}")
        if not hasattr(self, "_e1_sp"):
            self._e1_sp = {}
        self._e1_sp[t] = P.cnt[f"e1_sp{buf}"]

    def emit_signw(self, c):
        P, s = self.P, self
        if c > 0:
            return
        P.wait("sync", "e1_pe")
        P.wait("sync", "e1_ve")
        P.wait("sync", "e1_se")
        P.wait("gpsimd", "e1_pe")
        nred = 0
        for k in range(NC):
            buf = k % 2
            if k >= 2:
                P.wait("sync", "sw_se", P.cnt["sw_se"] - 1)
                P.wait("sync", "sw_ve", P.cnt["sw_ve"] - 1)
            P.dma("sync", s.wfst[buf], s.wf[k], inc=f"sw_ld{buf}")
            nld = P.cnt[f"sw_ld{buf}"]
            P.wait("scalar", f"sw_ld{buf}", nld)
            P.op("scalar", lambda e, k=k, buf=buf: e.activation(
                s.wres[:, k * D:(k + 1) * D], s.wfst[buf], AF.Sign), inc="sw_se")
            P.wait("vector", f"sw_ld{buf}", nld)
            P.op("vector", lambda e, k=k, buf=buf: e.tensor_reduce(
                s.sclc[:, 2:3], s.wfst[buf], AX.X, ALU.max,
                apply_absolute_value=True))
            if k == 0:
                P.op("vector", lambda e: e.tensor_copy(s.sclc[:, 0:1], s.sclc[:, 2:3]),
                     inc="sw_ve")
            else:
                P.op("vector", lambda e: e.tensor_tensor(
                    s.sclc[:, 0:1], s.sclc[:, 0:1], s.sclc[:, 2:3], ALU.max),
                    inc="sw_ve")
            nred = P.cnt["sw_ve"]
        P.wait("sync", "sw_ve", nred)
        P.dma("sync", s.scl_d, s.sclc[:, 0:1], inc="sw_sp")
        P.wait("sync", "sw_sp")
        P.dma("sync", s.sclrow[:, 0:128], s.scl_d.rearrange("p o -> o p"),
              inc="sw_ld2")
        P.wait("vector", "sw_ld2")
        P.op("vector", lambda e: e.tensor_reduce(
            s.srow[:, 0:1], s.sclrow[:, 0:128], AX.X, ALU.max), inc="sw_ve")
        nscl = P.cnt["sw_ve"]
        P.wait("tensor", "sw_ve", nscl)
        P.wait("tensor", "e1_pe")
        nbc = P.op("tensor", lambda e: e.matmul(
            s.pb[0][:, 0:1], s.onesf, s.srow[:, 0:1], start=True, stop=True),
            inc="sw_pe")
        P.wait("scalar", "sw_pe", nbc)
        nsc = P.op("scalar", lambda e: e.copy(s.sclc[:, 3:4], s.pb[0][:, 0:1]),
                   inc="sw_se")
        P.wait("vector", "sw_se", nsc)
        P.op("vector", lambda e: e.tensor_scalar_mul(
            s.gscale, s.cvec[:, NC:2 * NC], s.sclc[:, 3:4]))
        P.op("vector", lambda e: e.tensor_copy(s.bprime, s.cvec[:, 2 * NC:3 * NC]),
             inc="sw_ve")
        self._sw_done = P.cnt["sw_ve"]
        self._sw_se = P.cnt["sw_se"]

    def emit_e2(self, c, t2):
        P, s = self.P, self
        T2 = 512
        buf = t2 % 2
        off = c * TOKS + t2 * T2
        if t2 == 0 and c == 0:
            P.wait("sync", "e1_sp0")
            P.wait("sync", "e1_sp1")
            P.wait("sync", "sw_se", self._sw_se)
            P.wait("sync", "e1_ve")
            P.wait("sync", "e1_se")
            self._e2_hist = []
        if len(self._e2_hist) >= 2:
            pe_g, se_g, st_g = self._e2_hist[-2]
            P.wait("sync", "e2_pe", pe_g)
            P.wait("sync", "e2_se", se_g)
            P.wait("sync", f"out_st{buf}", st_g)
        P.dma("sync", s.xn2[buf].rearrange("p (n b) -> p n b", b=T2),
              s.xn_sp[:, off:off + T2].rearrange("(n p) b -> p n b", p=128),
              inc=f"e2_ldn{buf}")
        nldn = P.cnt[f"e2_ldn{buf}"]
        P.dma("sync", s.x22[buf].rearrange("p (n b) -> p n b", b=T2),
              s.x2_sp[:, off:off + T2].rearrange("(n p) b -> p n b", p=128),
              inc=f"e2_ldx{buf}")
        nldx = P.cnt[f"e2_ldx{buf}"]
        P.wait("tensor", f"e2_ldn{buf}", nldn)
        if t2 == 0 and c == 0:
            P.wait("tensor", "sw_se", self._sw_se)
            P.wait("tensor", "sw_ve", self._sw_done)
            P.wait("tensor", "e1_ve")
            P.wait("tensor", "e1_pe")
        nfs = {}
        for f in range(NC):
            pob = s.pb[2 * (f % 2)], s.pb[2 * (f % 2) + 1]
            if f >= 2:
                P.wait("tensor", "e2_se", nfs[f - 2])
            for k in range(NC):
                P.op("tensor", lambda e, k=k, f=f, buf=buf, pob=pob: e.matmul(
                    pob[0][:, 0:512], s.wres[:, k * D + f * 128:k * D + (f + 1) * 128],
                    s.xn2[buf][:, k * T2:k * T2 + 512],
                    start=(k == 0), stop=(k == NC - 1)),
                    inc=("e2_pe" if k == NC - 1 else None))
            npe = P.cnt["e2_pe"]
            P.wait("vector", "e2_pe", npe)
            if f == 0:
                P.wait("vector", f"e2_ldx{buf}", nldx)
            nev = P.op("vector", lambda e, f=f, buf=buf, pob=pob:
                       e.scalar_tensor_tensor(
                           s.x22[buf][:, f * T2:(f + 1) * T2], pob[0][:, 0:512],
                           s.gscale[:, f:f + 1], s.x22[buf][:, f * T2:(f + 1) * T2],
                           ALU.mult, ALU.add), inc="e2_ve")
            P.wait("gpsimd", "e2_ve", nev)
            nse = P.op("gpsimd", lambda e, f=f, buf=buf: e.tensor_scalar_add(
                s.x22[buf][:, f * T2:(f + 1) * T2],
                s.x22[buf][:, f * T2:(f + 1) * T2],
                s.bprime[:, f:f + 1]), inc="e2_se")
            nfs[f] = nse
        P.wait("sync", "e2_se", P.cnt["e2_se"])
        P.dma("sync", s.out_d[:, off:off + T2].rearrange("(n p) b -> p n b", p=128),
              s.x22[buf].rearrange("p (n b) -> p n b", b=T2), inc=f"out_st{buf}")
        self._e2_hist.append((P.cnt["e2_pe"], P.cnt["e2_se"],
                              P.cnt[f"out_st{buf}"]))

    def emit_sw_dumps(self):
        P, s = self.P, self
        P.wait("vector", "sw_ve", self._sw_done)
        P.wait("sync", "sw_ve", self._sw_done)
        P.dma("sync", s.sw_dp[:, 0:16], s.gscale, inc="dbg_st")
        P.dma("sync", s.sw_dp[:, 16:32], s.bprime, inc="dbg_st")
        P.dma("sync", s.sw_dp[:, 32:36], s.sclc, inc="dbg_st")
        P.wait("sync", "sw_se", self._sw_se)
        P.dma("sync", s.wres_dp, s.wres, inc="dbg_st")
        P.wait("sync", "dbg_st")

    def emit_dumps(self):
        P, s = self.P, self
        P.wait("sync", "b0_se", self.v_done)
        P.dma("sync", s.ln1_dp, s.ln1, inc="dbg_st")
        P.dma("sync", s.vtm_dp, s.vtm, inc="dbg_st")
        P.wait("sync", "dbg_st")


def _stage_weights(L):
    wqkvT = np.ascontiguousarray(L["L0_attn_w_qkv"].T)
    out = {}
    wqh = np.empty((H, 128, D), np.float32)
    wkh = np.empty((H, 128, D), np.float32)
    for h in range(H):
        wqh[h] = wqkvT[:, h * 128:(h + 1) * 128].reshape(NC, 128, 128) \
            .transpose(1, 0, 2).reshape(128, D)
        wkh[h] = wqkvT[:, D + h * 128:D + (h + 1) * 128].reshape(NC, 128, 128) \
            .transpose(1, 0, 2).reshape(128, D)
    out["wq"] = wqh
    out["wk"] = wkh
    out["wv"] = np.ascontiguousarray(wqkvT[:, 2 * D:3 * D].reshape(NC, 128, D))
    out["wg"] = np.ascontiguousarray(L["L0_attn_w_gate"].T.reshape(NC, 128, D))
    out["wo"] = np.ascontiguousarray(L["L0_attn_w_out"].T.reshape(NC, 128, D))
    out["wf"] = np.ascontiguousarray(L["L0_ffn_weight"].T.reshape(NC, 128, D))
    fm = lambda v: np.ascontiguousarray(np.asarray(v, np.float32).reshape(NC, 128).T)
    out["vecs"] = np.ascontiguousarray(np.concatenate(
        [fm(L["L0_attn_norm_w"]), fm(L["L0_ffn_gamma"]), fm(L["L0_ffn_beta"]),
         fm(L["L0_ffn_ln_weight"]), fm(L["L0_ffn_ln_bias"])], axis=1))
    out["slope"] = np.ascontiguousarray(
        np.asarray(L["L0_attn_slope_rate"], np.float32).reshape(1, H))
    ar = np.arange(1, BLK + 1, dtype=np.float32)
    out["arow"] = np.ascontiguousarray(np.broadcast_to(ar, (128, BLK)))
    out["krow"] = np.ascontiguousarray(np.broadcast_to(BLK - ar, (128, BLK)))
    kp = np.arange(2 * 128).reshape(2 * 128, 1).astype(np.float32)
    qp = np.arange(BLK).reshape(1, BLK).astype(np.float32)
    idxT = qp - kp
    idxT[idxT < 0] = BIG
    out["idxT"] = np.ascontiguousarray(idxT.astype(np.float32))
    out["identw"] = np.eye(128, dtype=np.float32)
    return out


_CACHE = {}


def _get_nc(chunks, use_coll, dbg=False):
    key = (chunks, use_coll, dbg)
    if key not in _CACHE:
        k = Ker(chunks, use_coll, dbg)
        _CACHE[key] = (k.build(), k.sbuf_left)
    return _CACHE[key][0]


def kernel(**inputs):
    x = np.asarray(inputs["L0_attn_hidden_states"], np.float32)
    B, N = x.shape[0], x.shape[1]
    staged = _stage_weights(inputs)

    use_v1 = True
    if use_v1:
        nc = _get_nc(1, True)
        in_maps = []
        for c in range(8):
            b, half = c // 2, c % 2
            m = dict(staged)
            m["x"] = np.ascontiguousarray(x[b, half * TOKS:(half + 1) * TOKS, :].T)
            m["mask01"] = np.full((128, 1), float(half), np.float32)
            in_maps.append(m)
        res = run_bass_kernel_spmd(nc, in_maps, core_ids=list(range(8)))
        out = np.empty((B, N, D), np.float32)
        for c in range(8):
            b, half = c // 2, c % 2
            out[b, half * TOKS:(half + 1) * TOKS, :] = res.results[c]["out"].T
        return out
    else:
        nc = _get_nc(2, False)
        in_maps = []
        for c in range(8):
            b = c % 4
            m = dict(staged)
            m["x"] = np.ascontiguousarray(x[b].T)
            m["mask01"] = np.zeros((128, 1), np.float32)
            in_maps.append(m)
        res = run_bass_kernel_spmd(nc, in_maps, core_ids=list(range(8)))
        out = np.empty((B, N, D), np.float32)
        for b in range(4):
            out[b] = res.results[b]["out"].T
        return out


# version 06
# speedup vs baseline: 1.1152x; 1.1152x over previous
import numpy as np
import ml_dtypes
from contextlib import ExitStack

import concourse.bass as bass
import concourse.mybir as mybir
from concourse.bass_utils import run_bass_kernel_spmd

F32 = mybir.dt.float32
BF16 = mybir.dt.bfloat16
U8 = mybir.dt.uint8
AF = mybir.ActivationFunctionType
ALU = mybir.AluOpType
AX = mybir.AxisListType
BFNP = ml_dtypes.bfloat16

D = 2048
H = 16
NC = 16
BLK = 256
TOKS = 2048
NBLK = TOKS // BLK
BIG = 1.0e9
GRP = 4
REPLICA_PAIRS = [[0, 1], [2, 3], [4, 5], [6, 7]]


class Prog:
    ENGINES = ("sync", "tensor", "vector", "scalar", "gpsimd")

    def __init__(self, nc, stack):
        self.nc = nc
        self.stack = stack
        self.ops = {e: [] for e in self.ENGINES}
        self.semh = {}
        self.cnt = {}

    def sem(self, name):
        if name not in self.semh:
            self.semh[name] = self.stack.enter_context(self.nc.semaphore(name))
            self.cnt[name] = 0
        return self.semh[name]

    def op(self, engine, fn, inc=None, by=1):
        if inc is not None:
            h = self.sem(inc)
            self.cnt[inc] += by
            self.ops[engine].append(lambda e, fn=fn, h=h, by=by: fn(e).then_inc(h, by))
            return self.cnt[inc]
        self.ops[engine].append(fn)
        return None

    def wait(self, engine, sem, value=None):
        if value is None:
            value = self.cnt.get(sem, 0)
        assert value <= self.cnt.get(sem, 0), (engine, sem, value)
        if value <= 0:
            return
        h = self.sem(sem)
        self.ops[engine].append(lambda e, h=h, v=value: e.wait_ge(h, v))

    def dma(self, engine, out, in_, inc):
        return self.op(engine, lambda e, o=out, i=in_: e.dma_start(out=o, in_=i),
                       inc=inc, by=16)

    def replay(self):
        with self.nc.Block() as block:
            for en in self.ENGINES:
                opl = self.ops[en]

                def body(eng, opl=opl):
                    for f in opl:
                        f(eng)
                getattr(block, en)(body)


class Ker:
    def __init__(self, chunks, use_coll, dbg=False):
        self.chunks = chunks
        self.use_coll = use_coll
        self.dbg = dbg
        self.TA = TOKS * chunks

    def build(self):
        nc = bass.Bass("TRN2", target_bir_lowering=False, debug=False)
        self.nc = nc
        TA = self.TA
        din = lambda n, sh: nc.dram_tensor(n, sh, F32, kind="ExternalInput").ap()
        self.x_in = din("x", [D, TA])
        self.wq = din("wq", [H, 128, D])
        self.wk = din("wk", [H, 128, D])
        self.wv = din("wv", [NC, 128, D])
        self.wg = din("wg", [NC, 128, D])
        self.wo = din("wo", [NC, 128, D])
        self.wf = din("wf", [NC, 128, D])
        self.vecs = din("vecs", [128, 5 * NC])
        self.slope = din("slope", [1, H])
        self.mask01 = din("mask01", [128, 1])
        self.arow = din("arow", [128, BLK])
        self.krow = din("krow", [128, BLK])
        self.idxT = din("idxT", [2 * 128, BLK])
        self.identw = din("identw", [128, 128])

        self.out_d = nc.dram_tensor("out", [D, TA], F32, kind="ExternalOutput").ap()
        spk = "ExternalOutput" if self.dbg else "Internal"
        self.attn_sp = nc.dram_tensor("attn_sp", [D, TA], BF16, kind=spk).ap()
        self.qdec_sp = nc.dram_tensor("qdec_sp", [D, TA], BF16, kind=spk).ap()
        self.gate_sp = nc.dram_tensor("gate_sp", [D, TA], BF16, kind=spk).ap()
        self.xn_sp = nc.dram_tensor("xn_sp", [D, TA], BF16, kind=spk).ap()
        self.x2_sp = nc.dram_tensor("x2_sp", [D, TA], F32, kind=spk).ap()
        if self.dbg:
            self.ln1_dp = nc.dram_tensor("ln1_dp", [128, NC * TOKS], BF16,
                                         kind="ExternalOutput").ap()
            self.vtm_dp = nc.dram_tensor("vtm_dp", [128, NC * D], BF16,
                                         kind="ExternalOutput").ap()
            self.sw_dp = nc.dram_tensor("sw_dp", [128, 64], F32,
                                        kind="ExternalOutput").ap()
            self.kvp_dp = nc.dram_tensor("kvp_dp", [128, H * 128], BF16,
                                         kind="ExternalOutput").ap()
            self.atc_dp = nc.dram_tensor("atc_dp", [D, TOKS], BF16,
                                         kind="ExternalOutput").ap()
            self.bd_dp = nc.dram_tensor("bd_dp", [128, 144], F32,
                                        kind="ExternalOutput").ap()
            self.wres_dp = nc.dram_tensor("wres_dp", [128, NC * D], BF16,
                                          kind="ExternalOutput").ap()
        self.kvf_d = nc.dram_tensor("kvf_d", [H * 128, 128], F32).ap()
        self.kvg_d = nc.dram_tensor("kvg_d", [2 * H * 128, 128], F32).ap()
        self.scl_d = nc.dram_tensor("scl_d", [128, 1], F32).ap()

        ctx = ExitStack()
        with ctx:
            sb = lambda n, sh, d: ctx.enter_context(nc.sbuf_tensor(n, sh, d))
            psb = lambda n, sh, d=F32: ctx.enter_context(nc.psum_tensor(n, sh, d))
            arA = sb("arA", [128, 65536], U8)[:]
            arB = sb("arB", [128, 65536], U8)[:]
            arC = sb("arC", [128, 49152], U8)[:]
            arD = sb("arD", [128, 8192], U8)[:]
            A2, B2v, C2, D2v = (a.bitcast(BF16) for a in (arA, arB, arC, arD))
            A4, C4 = arA.bitcast(F32), arC.bitcast(F32)

            self.ln1 = A2[:, 0:NC * TOKS]
            self.xt = [A4[:, 0:4096], A4[:, 4096:8192]]
            self.x2t = [A4[:, 8192:12288], A4[:, 12288:16384]]
            self.x22 = [A4[:, 0:8192], A4[:, 8192:16384]]
            self.vtm = B2v[:, 0:NC * D]
            self.wres = B2v[:, 0:NC * D]
            self.xblk = [C4[:, 0:2048], C4[:, 2048:4096]]
            self.xbb = [C2[:, 8192:10240], C2[:, 10240:12288]]
            self.sqb = [C2[:, 12288:14336], C2[:, 14336:16384]]
            self.wvsl = [C2[:, 0:8192], C2[:, 8192:16384]]
            self.qraw = C2[:, 0:2048]
            self.qdech = C2[:, 2048:4096]
            self.kfm = C2[:, 4096:6144]
            self.kkd = C2[:, 6144:8192]
            self.attnh = C2[:, 8192:10240]
            self.qkT = C2[:, 10240:10752]
            self.kkdT = C2[:, 10752:11008]
            self.wqh = C2[:, 11264:13312]
            self.wkh = C2[:, 13312:15360]
            self.attnt = [C2[:, 0:4096], C2[:, 4096:8192]]
            self.gatet = [C2[:, 8192:12288], C2[:, 12288:16384]]
            self.qdect = C2[:, 16384:20480]
            self.sqt = C2[:, 20480:24576]
            self.xn2 = [C2[:, 0:8192], C2[:, 8192:16384]]
            self.wfst = [C4[:, 8192:10240], C4[:, 10240:12288]]
            self.qdecs = D2v[:, 0:GRP * BLK]
            self.kdpats = D2v[:, GRP * BLK:2 * GRP * BLK]
            self.diags = D2v[:, 2 * GRP * BLK:4 * GRP * BLK]
            D4 = arD.bitcast(F32)
            self.rowsA = D4[0:1, 0:512]
            self.rowsE = D4[0:1, 512:1536]
            self.gatesb = D2v[:, 0:2048]

            self.cvec = sb("cvec", [128, 5 * NC], F32)[:]
            self.gscale = sb("gscale", [128, NC], F32)[:]
            self.bprime = sb("bprime", [128, NC], F32)[:]
            self.lnbb = sb("lnbb", [128, NC], BF16)[:]
            self.nsl = sb("nsl", [128, H], F32)[:]
            self.bdcol = sb("bdcol", [128, H], F32)[:]
            self.bdpow = sb("bdpow", [128, NBLK * H], F32)[:]
            self.onesb = sb("onesb", [128, 1], BF16)[:]
            self.onesf = sb("onesf", [1, 128], F32)[:]
            self.identb = sb("identb", [128, 128], BF16)[:]
            self.identf = sb("identf", [128, 128], F32)[:]
            self.maskc = sb("maskc", [128, 1], F32)[:]
            self.arow_s = sb("arow_s", [128, BLK], F32)[:]
            self.krow_s = sb("krow_s", [128, BLK], F32)[:]
            self.idxT_s = sb("idxT_s", [128, 2 * BLK], F32)[:]
            self.slope_s = sb("slope_s", [1, H], F32)[:]
            self.kvst = sb("kvst", [128, H * 128], F32)[:]
            self.kvbf = sb("kvbf", [128, 128], BF16)[:]
            self.kvprevb = sb("kvprevb", [128, H * 128], BF16)[:]
            self.u1t = sb("u1t", [128, BLK], F32)[:]
            self.qbt = sb("qbt", [128, 2 * BLK], BF16)[:]
            self.sclrow = sb("sclrow", [1, 256], F32)[:]
            self.sclc = sb("sclc", [128, 20], F32)[:]
            self.srow = sb("srow", [1, 8], F32)[:]

            self.cpool = sb("cpool", [128, 16], F32)[:]
            self.cvals = [0.0, 1.0, 1e-5, 1e-6, -0.5, 1.0 / D,
                          1.0 + 1e-5, 1e-10] + \
                [256.0 * j for j in range(1, NBLK)]
            for i, v in enumerate(self.cvals):
                nc.const_aps.register(F32, v, self.cpool[:, i:i + 1])

            self.pb = [psb(f"pb{i}", [128, 512]) for i in range(6)]
            self.pbt = psb("pbt", [128, 256], BF16)
            self.pkv = psb("pkv", [128, 128])

            self.sbuf_left = nc.sbuf_bytes_remaining

            P = Prog(nc, ctx)
            self.P = P
            self.emit()
            P.replay()
        return nc

    def emit(self):
        P = self.P
        self._kkdT_n = 0
        self._qkT_n = 0
        self.emit_consts()
        for c in range(self.chunks):
            self.emit_ln1(c)
            self.emit_vproj(c)
            if self.dbg and c == 0:
                self.emit_dumps()
            for h in range(H):
                self.emit_head(c, h)
            self.emit_gate(c)
        if self.use_coll:
            self.emit_collective()
        else:
            pass
        for c in range(self.chunks):
            self.emit_wout_load(c)
            for t in range(NBLK):
                self.emit_e1(c, t)
        for c in range(self.chunks):
            self.emit_signw(c)
            if self.dbg and c == 0:
                self.emit_sw_dumps()
            for t2 in range(TOKS // 512):
                self.emit_e2(c, t2)
        P.wait("gpsimd", "out_st0")
        P.wait("gpsimd", "out_st1")
        P.op("gpsimd", lambda e: e.nop())

    def emit_consts(self):
        P, s = self.P, self
        P.dma("sync", s.cvec, s.vecs, inc="c_ld")
        P.dma("sync", s.slope_s, s.slope, inc="c_ld")
        P.dma("sync", s.maskc, s.mask01, inc="c_ld")
        P.dma("sync", s.arow_s, s.arow, inc="c_ld")
        P.dma("sync", s.krow_s, s.krow, inc="c_ld")
        P.dma("sync", s.idxT_s.rearrange("p (a c) -> p a c", a=2),
              s.idxT.rearrange("(a p) c -> p a c", p=128), inc="c_ld")
        P.dma("gpsimd", s.identb, s.identw, inc="c_ldg")
        P.dma("sync", s.identf, s.identw, inc="c_ld")
        P.op("gpsimd", lambda e: e.memset(s.onesb, 1.0))
        P.op("gpsimd", lambda e: e.memset(s.onesf, 1.0))
        P.op("gpsimd", lambda e: e.memset(s.kvst, 0.0))
        P.op("gpsimd", lambda e: e.memset(s.kvbf, 0.0))
        P.op("gpsimd", lambda e: e.memset(s.kvprevb, 0.0))
        P.op("gpsimd", lambda e: e.memset(s.bdpow[:, 0:16], 1.0))
        for i, v in enumerate(s.cvals):
            P.op("gpsimd", lambda e, i=i, v=v: e.memset(s.cpool[:, i:i + 1], v))
        P.op("gpsimd", lambda e: e.engine_nop(), inc="c_ms")
        P.wait("tensor", "c_ld")
        P.wait("tensor", "c_ldg")
        P.wait("tensor", "c_ms")
        P.op("tensor", lambda e: e.matmul(s.pb[5][:, 0:16], s.onesf, s.slope_s,
                                          start=True, stop=True), inc="c_pe")
        P.wait("vector", "c_pe")
        P.op("vector", lambda e: e.tensor_scalar_mul(s.nsl, s.pb[5][:, 0:16], -1.0),
             inc="c_ve")
        P.wait("scalar", "c_ve")
        P.op("scalar", lambda e: e.activation(s.bdcol, s.nsl, AF.Exp, scale=256.0),
             inc="c_se")
        for j in range(1, NBLK):
            P.op("scalar", lambda e, j=j: e.activation(
                s.bdpow[:, j * 16:(j + 1) * 16], s.nsl, AF.Exp, scale=256.0 * j),
                inc="c_se")

    def emit_ln1(self, c):
        P, s = self.P, self
        SB = 128
        nsb = TOKS // SB
        for sem in ("e2_se", "e2_pe", "e2_ve", "e1_ve", "e1_pe", "b1_pe", "b1_ve",
                    "b1_se", "b2_pe", "b2_se", "b2_sp0", "b2_sp1", "out_st0",
                    "out_st1", "e1_sp0", "e1_sp1", "c_ve", "c_se"):
            P.wait("sync", sem)
            P.wait("gpsimd", sem)
            P.wait("tensor", sem)
            P.wait("vector", sem)
            P.wait("scalar", sem)
        done_ve = {}
        done_pe = {}
        rows_done = {}
        for j in range(nsb):
            buf = j % 2
            t0 = c * TOKS + j * SB
            if j >= 2:
                P.wait("sync", "a1_ve", done_ve[j - 2])
                P.wait("gpsimd", "a1_pe", done_pe[j - 2])
            src = s.x_in[:, t0:t0 + SB].rearrange("(n p) b -> p n b", p=128)
            P.dma("sync", s.xblk[buf].rearrange("p (n b) -> p n b", b=SB), src,
                  inc=f"a1_ldx{buf}")
            nx = P.cnt[f"a1_ldx{buf}"]
            P.dma("gpsimd", s.xbb[buf].rearrange("p (n b) -> p n b", b=SB), src,
                  inc=f"a1_ldxb{buf}")
            nxb = P.cnt[f"a1_ldxb{buf}"]
            P.wait("scalar", f"a1_ldxb{buf}", nxb)
            if j >= 2:
                P.wait("scalar", "a1_pe", done_pe[j - 2])
            P.op("scalar", lambda e, buf=buf: e.activation(
                s.sqb[buf], s.xbb[buf], AF.Square), inc="a1_se")
            nse = P.cnt["a1_se"]
            P.wait("tensor", f"a1_ldxb{buf}", nxb)
            P.wait("tensor", "a1_se", nse)
            if j >= 1:
                P.wait("tensor", "a1_rows", rows_done[j - 1])
            for n in range(NC):
                P.op("tensor", lambda e, n=n, buf=buf: e.matmul(
                    s.pb[4][0:1, 0:SB], s.onesb,
                    s.xbb[buf][:, n * SB:(n + 1) * SB],
                    start=(n == 0), stop=(n == NC - 1)))
            for n in range(NC):
                P.op("tensor", lambda e, n=n, buf=buf: e.matmul(
                    s.pb[4][0:1, SB:2 * SB], s.onesb,
                    s.sqb[buf][:, n * SB:(n + 1) * SB],
                    start=(n == 0), stop=(n == NC - 1)),
                    inc=("a1_pe" if n == NC - 1 else None))
            done_pe[j] = P.cnt["a1_pe"]
            P.wait("vector", "a1_pe", done_pe[j])
            P.op("vector", lambda e: e.tensor_scalar_mul(
                s.rowsA[:, 0:SB], s.pb[4][0:1, 0:SB], 1.0 / D))
            P.op("vector", lambda e: e.tensor_tensor(
                s.rowsE[:, 0:SB], s.rowsA[:, 0:SB], s.rowsA[:, 0:SB], ALU.mult))
            P.op("vector", lambda e: e.scalar_tensor_tensor(
                s.rowsA[:, SB:2 * SB], s.pb[4][0:1, SB:2 * SB], 1.0 / D,
                s.rowsE[:, 0:SB], ALU.mult, ALU.subtract), inc="a1_ve")
            P.wait("scalar", "a1_ve", P.cnt["a1_ve"])
            P.op("scalar", lambda e: e.activation(
                s.rowsA[:, SB:2 * SB], s.rowsA[:, SB:2 * SB], AF.Ln, bias=1e-5))
            P.op("scalar", lambda e: e.activation(
                s.rowsA[:, SB:2 * SB], s.rowsA[:, SB:2 * SB], AF.Exp, scale=-0.5),
                inc="a1_se")
            P.wait("vector", "a1_se", P.cnt["a1_se"])
            P.op("vector", lambda e: e.scalar_tensor_tensor(
                s.rowsA[:, 0:SB], s.rowsA[:, 0:SB], -1.0,
                s.rowsA[:, SB:2 * SB], ALU.mult, ALU.mult), inc="a1_rows")
            rows_done[j] = P.cnt["a1_rows"]
            P.wait("tensor", "a1_rows", rows_done[j])
            if j >= 1:
                P.wait("tensor", "a1_ve", done_ve[j - 1])
            P.op("tensor", lambda e: e.matmul(
                s.pb[5][:, 0:2 * SB], s.onesf, s.rowsA[:, 0:2 * SB],
                start=True, stop=True), inc="a1_bc")
            P.wait("vector", "a1_bc", P.cnt["a1_bc"])
            P.wait("vector", f"a1_ldx{buf}", nx)
            for n in range(NC):
                dst = s.ln1[:, n * TOKS + j * SB: n * TOKS + (j + 1) * SB]
                P.op("vector", lambda e, n=n, buf=buf, dst=dst: e.tensor_tensor(
                    dst, s.xblk[buf][:, n * SB:(n + 1) * SB],
                    s.pb[5][:, SB:2 * SB], ALU.mult))
                P.op("vector", lambda e, dst=dst: e.tensor_tensor(
                    dst, dst, s.pb[5][:, 0:SB], ALU.add),
                    inc=("a1_ve" if n == NC - 1 else None))
            done_ve[j] = P.cnt["a1_ve"]
        self.ln1_done = P.cnt["a1_ve"]

    def emit_vproj(self, c):
        P, s = self.P, self
        for sem in ("a1_pe", "a1_ve", "b1_pe", "b1_ve", "b1_se", "b2_pe",
                    "e2_pe"):
            P.wait("gpsimd", sem)
        evac_pe = {}
        for vf in range(4):
            buf = vf % 2
            if vf >= 2:
                P.wait("gpsimd", "b0_pe", evac_pe[vf - 2])
            P.dma("gpsimd", s.wvsl[buf].rearrange("p (k cc) -> p k cc", cc=512),
                  s.wv[:, :, vf * 512:(vf + 1) * 512].rearrange("k p cc -> p k cc"),
                  inc=f"b0_wld{buf}")
            nw = P.cnt[f"b0_wld{buf}"]
            P.wait("tensor", f"b0_wld{buf}", nw)
            if vf == 0:
                P.wait("tensor", "a1_ve", self.ln1_done)
                P.wait("tensor", "b1_pe")
                P.wait("tensor", "b2_pe")
                P.wait("tensor", "b2_se")
                P.wait("tensor", "e1_pe")
                P.wait("tensor", "e2_pe")
                P.wait("tensor", "e2_ve")
            for tc in range(NC):
                pbank = s.pb[tc % 2]
                if tc >= 2:
                    P.wait("tensor", "b0_se", P.cnt["b0_se"] - 1)
                for k in range(NC):
                    P.op("tensor", lambda e, k=k, tc=tc, buf=buf, pbank=pbank: e.matmul(
                        pbank[:, 0:512],
                        s.ln1[:, k * TOKS + tc * 128:k * TOKS + (tc + 1) * 128],
                        s.wvsl[buf][:, k * 512:(k + 1) * 512],
                        start=(k == 0), stop=(k == NC - 1)),
                        inc=("b0_pe" if k == NC - 1 else None))
                P.wait("scalar", "b0_pe", P.cnt["b0_pe"])
                P.op("scalar", lambda e, tc=tc, vf=vf, pbank=pbank: e.activation(
                    s.vtm[:, tc * D + vf * 512: tc * D + (vf + 1) * 512],
                    pbank[:, 0:512], AF.Silu), inc="b0_se")
            evac_pe[vf] = P.cnt["b0_pe"]
        self.v_done = P.cnt["b0_se"]

    def emit_decay_group(self, g):
        P, s = self.P, self
        P.wait("scalar", "b1_pe")
        P.wait("scalar", "b1_ve")
        P.wait("scalar", "a1_bc")
        P.wait("scalar", "a1_ve")
        P.wait("scalar", "b2_sp0")
        P.wait("scalar", "b2_sp1")
        for i in range(GRP):
            h = g * GRP + i
            sc = s.nsl[:, h:h + 1]
            P.op("scalar", lambda e, i=i, sc=sc: e.activation(
                s.qdecs[:, i * BLK:(i + 1) * BLK], s.arow_s, AF.Exp, scale=sc))
            P.op("scalar", lambda e, i=i, sc=sc: e.activation(
                s.kdpats[:, i * BLK:(i + 1) * BLK], s.krow_s, AF.Exp, scale=sc))
            P.op("scalar", lambda e, i=i, sc=sc: e.activation(
                s.diags[:, i * 2 * BLK:(i + 1) * 2 * BLK], s.idxT_s, AF.Exp,
                scale=sc), inc="b1_se")

    def emit_head(self, c, h):
        P, s = self.P, self
        if h % GRP == 0:
            self.emit_decay_group((h % H) // GRP)
        hg = h % GRP

        P.wait("gpsimd", "b1_pe")
        if h == 0:
            P.wait("gpsimd", "b0_pe")
            P.wait("gpsimd", "b0_se")
            P.wait("gpsimd", "a1_se")
            P.wait("gpsimd", "e1_se")
            P.wait("gpsimd", "e1_ve")
            P.wait("gpsimd", "e2_pe")
        P.dma("gpsimd", s.wqh, s.wq[h], inc="b1_wld")
        P.dma("gpsimd", s.wkh, s.wk[h], inc="b1_wld")
        nw = P.cnt["b1_wld"]

        P.wait("tensor", "b1_wld", nw)
        if h == 0:
            P.wait("tensor", "b0_se", self.v_done)
        evac = {}
        for proj in range(2):
            banks = (s.pb[0], s.pb[1]) if proj == 0 else (s.pb[2], s.pb[3])
            dstr = s.qraw if proj == 0 else s.kfm
            wt = s.wqh if proj == 0 else s.wkh
            for qt in range(4):
                bank = banks[qt % 2]
                P.wait("tensor", "b1_se", evac.get((proj, qt - 2),
                                                   self._b1_proj_guard(proj)))
                if qt == 0 and proj == 0:
                    P.wait("tensor", "b1_sp")
                for k in range(NC):
                    P.op("tensor", lambda e, k=k, qt=qt, wt=wt, bank=bank: e.matmul(
                        bank[:, 0:512], wt[:, k * 128:(k + 1) * 128],
                        s.ln1[:, k * TOKS + qt * 512:k * TOKS + (qt + 1) * 512],
                        start=(k == 0), stop=(k == NC - 1)),
                        inc=("b1_pe" if k == NC - 1 else None))
                npe = P.cnt["b1_pe"]
                P.wait("scalar", "b1_pe", npe)
                if qt == 0 and proj == 0:
                    P.wait("scalar", "b1_sp")
                    P.wait("scalar", "b1_ve")
                n_ev = P.op("scalar", lambda e, qt=qt, bank=bank, dstr=dstr:
                            e.activation(dstr[:, qt * 512:(qt + 1) * 512],
                                         bank[:, 0:512], AF.Silu), inc="b1_se")
                evac[(proj, qt)] = n_ev
        self._b1_q_evac = evac[(0, 3)]
        self._b1_k_evac = evac[(1, 3)]

        P.wait("vector", "b1_sp")
        P.wait("vector", "b1_se", self._b1_q_evac)
        rep = lambda ap: ap.unsqueeze(1).broadcast_to([128, NBLK, BLK])
        P.op("vector", lambda e: e.tensor_tensor(
            s.qdech.rearrange("p (r b) -> p r b", b=BLK),
            s.qraw.rearrange("p (r b) -> p r b", b=BLK),
            rep(s.qdecs[:, hg * BLK:(hg + 1) * BLK]), ALU.mult), inc="b1_ve")
        P.wait("vector", "b1_se", self._b1_k_evac)
        P.op("vector", lambda e: e.tensor_tensor(
            s.kkd.rearrange("p (r b) -> p r b", b=BLK),
            s.kfm.rearrange("p (r b) -> p r b", b=BLK),
            rep(s.kdpats[:, hg * BLK:(hg + 1) * BLK]), ALU.mult), inc="b1_ve")
        n_dec = P.cnt["b1_ve"]

        hs = h * 128
        for j in range(NBLK):
            t0 = j * BLK
            P.wait("scalar", "b1_ve")
            P.wait("scalar", "b1_pe")
            nbf = P.op("scalar", lambda e: e.copy(
                s.kvbf, s.kvst[:, hs:hs + 128]), inc="b1_se")
            P.wait("tensor", "b1_ve", n_dec)
            P.wait("tensor", "b1_ve", self._kkdT_n)
            P.op("tensor", lambda e, t0=t0: e.transpose(
                self.pbt[:][:, 0:128], s.kkd[:, t0:t0 + 128], s.identb))
            ntr = P.op("tensor", lambda e, t0=t0: e.transpose(
                self.pbt[:][:, 128:256], s.kkd[:, t0 + 128:t0 + 256], s.identb),
                inc="b1_pe")
            P.wait("tensor", "b1_ve", self._qkT_n)
            P.op("tensor", lambda e, t0=t0: e.matmul(
                s.pb[4][:, 0:256], s.kfm[:, t0:t0 + 128],
                s.qraw[:, t0:t0 + 256], start=True, stop=True))
            nqk = P.op("tensor", lambda e, t0=t0: e.matmul(
                s.pb[4][:, 256:512], s.kfm[:, t0 + 128:t0 + 256],
                s.qraw[:, t0:t0 + 256], start=True, stop=True), inc="b1_pe")
            P.wait("vector", "b1_pe", ntr)
            self._kkdT_n = P.op("vector", lambda e: e.tensor_copy(
                s.kkdT, self.pbt[:, 0:256]), inc="b1_ve")
            P.wait("vector", "b1_pe", nqk)
            P.op("vector", lambda e: e.tensor_tensor(
                s.qkT[:, 0:256], s.pb[4][:, 0:256],
                s.diags[:, hg * 2 * BLK:hg * 2 * BLK + 256], ALU.mult))
            self._qkT_n = P.op("vector", lambda e: e.tensor_tensor(
                s.qkT[:, 256:512], s.pb[4][:, 256:512],
                s.diags[:, hg * 2 * BLK + 256:hg * 2 * BLK + 512], ALU.mult),
                inc="b1_ve")
            P.wait("tensor", "b1_ve", self._qkT_n)
            P.wait("tensor", "b1_se", nbf)
            if True:
                P.wait("tensor", "b1_se")
            vt0 = (2 * j) * D + hs
            vt1 = (2 * j + 1) * D + hs
            P.op("tensor", lambda e, vt0=vt0: e.matmul(
                s.pb[5][:, 0:256], s.vtm[:, vt0:vt0 + 128], s.qkT[:, 0:256],
                start=True, stop=False))
            P.op("tensor", lambda e, vt1=vt1: e.matmul(
                s.pb[5][:, 0:256], s.vtm[:, vt1:vt1 + 128], s.qkT[:, 256:512],
                start=False, stop=False))
            nat = P.op("tensor", lambda e, t0=t0: e.matmul(
                s.pb[5][:, 0:256], s.kvbf, s.qdech[:, t0:t0 + 256],
                start=False, stop=True), inc="b1_pe")
            P.wait("tensor", "b1_ve", self._kkdT_n)
            P.op("tensor", lambda e, vt0=vt0: e.matmul(
                s.pkv[:], s.kkdT[:, 0:128], s.vtm[:, vt0:vt0 + 128],
                start=True, stop=False))
            nkv = P.op("tensor", lambda e, vt1=vt1: e.matmul(
                s.pkv[:], s.kkdT[:, 128:256], s.vtm[:, vt1:vt1 + 128],
                start=False, stop=True), inc="b1_pe")
            P.wait("scalar", "b1_pe", nat)
            P.op("scalar", lambda e, t0=t0: e.copy(
                s.attnh[:, t0:t0 + 256], s.pb[5][:, 0:256]), inc="b1_se")
            P.wait("vector", "b1_pe", nkv)
            P.op("vector", lambda e: e.scalar_tensor_tensor(
                s.kvst[:, hs:hs + 128], s.kvst[:, hs:hs + 128],
                s.bdcol[:, h:h + 1], s.pkv[:], ALU.mult, ALU.add), inc="b1_ve")
        self._kv_done = P.cnt["b1_ve"]

        P.wait("sync", "b1_se")
        P.dma("sync", s.attn_sp[h * 128:(h + 1) * 128, c * TOKS:(c + 1) * TOKS],
              s.attnh, inc="b1_sp")
        if self.use_coll:
            P.wait("sync", "b1_ve", n_dec)
            P.dma("sync", s.qdec_sp[h * 128:(h + 1) * 128, c * TOKS:(c + 1) * TOKS],
                  s.qdech, inc="b1_sp")

    def _b1_proj_guard(self, proj):
        return self.P.cnt.get("b1_se", 0)

    def emit_gate(self, c):
        P, s = self.P, self
        P.wait("gpsimd", "b1_pe")
        P.wait("gpsimd", "b1_ve")
        for k in range(NC):
            P.dma("gpsimd", s.wres[:, k * D:(k + 1) * D], s.wg[k], inc="b2_wld")
        nw = P.cnt["b2_wld"]
        P.wait("tensor", "b2_wld", nw)
        P.wait("tensor", "b1_se")
        P.wait("tensor", "b1_ve")
        P.wait("scalar", "b1_ve")
        P.wait("scalar", "b1_pe")
        evacs = []
        for th in range(2):
            for f in range(NC):
                fi = th * NC + f
                b0, b1 = (s.pb[0], s.pb[1]) if f % 2 == 0 else (s.pb[2], s.pb[3])
                if fi >= 2:
                    P.wait("tensor", "b2_se", evacs[fi - 2])
                for k in range(NC):
                    P.op("tensor", lambda e, k=k, f=f, th=th, b0=b0: e.matmul(
                        b0[:, 0:512], s.wres[:, k * D + f * 128:k * D + (f + 1) * 128],
                        s.ln1[:, k * TOKS + th * 1024:k * TOKS + th * 1024 + 512],
                        start=(k == 0), stop=(k == NC - 1)))
                    P.op("tensor", lambda e, k=k, f=f, th=th, b1=b1: e.matmul(
                        b1[:, 0:512], s.wres[:, k * D + f * 128:k * D + (f + 1) * 128],
                        s.ln1[:, k * TOKS + th * 1024 + 512:k * TOKS + th * 1024 + 1024],
                        start=(k == 0), stop=(k == NC - 1)),
                        inc=("b2_pe" if k == NC - 1 else None))
                npe = P.cnt["b2_pe"]
                gb = s.gatesb[:, (f % 2) * 1024:(f % 2) * 1024 + 1024]
                P.wait("scalar", "b2_pe", npe)
                P.wait("scalar", f"b2_sp{f % 2}",
                       P.cnt.get(f"b2_sp{f % 2}", 0))
                P.op("scalar", lambda e, gb=gb, b0=b0: e.activation(
                    gb[:, 0:512], b0[:, 0:512], AF.Sigmoid))
                nse = P.op("scalar", lambda e, gb=gb, b1=b1: e.activation(
                    gb[:, 512:1024], b1[:, 0:512], AF.Sigmoid), inc="b2_se")
                evacs.append(nse)
                P.wait("vector", "b2_se", nse)
                nve = P.op("vector", lambda e, gb=gb, f=f: e.tensor_scalar_mul(
                    gb, gb, s.cvec[:, f:f + 1]), inc="b2_ve")
                P.wait("sync", "b2_ve", nve)
                P.wait("sync", "b2_se", nse)
                P.dma("sync",
                      s.gate_sp[f * 128:(f + 1) * 128,
                                c * TOKS + th * 1024:c * TOKS + th * 1024 + 1024],
                      gb, inc=f"b2_sp{f % 2}")

    def emit_collective(self):
        P, s = self.P, self
        P.wait("gpsimd", "b1_ve")
        P.dma("gpsimd", s.kvf_d.rearrange("(h p) d -> p h d", p=128),
              s.kvst.rearrange("p (h d) -> p h d", d=128), inc="kv_sp")
        P.wait("gpsimd", "kv_sp")
        P.op("gpsimd", lambda e: e.collective_compute(
            "AllGather", ALU.bypass, replica_groups=REPLICA_PAIRS,
            ins=[s.kvf_d], outs=[s.kvg_d]), inc="cc_s")
        P.wait("gpsimd", "cc_s")
        P.dma("gpsimd", s.kvprevb.rearrange("p (h d) -> p h d", d=128),
              s.kvg_d[0:H * 128, :].rearrange("(h p) d -> p h d", p=128),
              inc="kv_ld")
        P.wait("vector", "kv_ld")
        P.op("vector", lambda e: e.tensor_scalar_mul(s.kvprevb, s.kvprevb, s.maskc),
             inc="kv_ve")
        if self.dbg:
            P.wait("sync", "kv_ve")
            P.dma("sync", s.kvp_dp, s.kvprevb, inc="dbg_st")
            P.wait("sync", "c_ve")
            P.wait("sync", "c_se")
            P.dma("sync", s.bd_dp[:, 0:16], s.bdcol, inc="dbg_st")
            P.dma("sync", s.bd_dp[:, 16:144], s.bdpow, inc="dbg_st")

    def emit_wout_load(self, c):
        if c > 0:
            return
        P, s = self.P, self
        P.wait("gpsimd", "b2_pe")
        for k in range(NC):
            P.dma("gpsimd", s.wres[:, k * D:(k + 1) * D], s.wo[k], inc="wo_ld")
        self._wo_n = P.cnt["wo_ld"]

    def emit_e1(self, c, t):
        P, s = self.P, self
        buf = t % 2
        off = c * TOKS + t * BLK
        use_corr = self.use_coll

        if t == 0:
            for sem in ("b1_pe", "b1_ve", "b1_se", "b2_pe", "b2_se", "a1_ve",
                        "a1_pe", "b2_sp0", "b2_sp1", "b1_sp", "e1_ve", "e1_pe",
                        "e1_se", "e1_sp0", "e1_sp1"):
                P.wait("sync", sem)
                P.wait("tensor", sem)
                P.wait("vector", sem)
                P.wait("scalar", sem)
            self._e1_free = {}
        else:
            if t >= 2:
                P.wait("sync", "e1_ve", self._e1_free[t - 2])
                P.wait("sync", f"e1_sp{buf}", self._e1_sp.get(t - 2, 0))
        fm = lambda apdram: apdram[:, off:off + BLK].rearrange(
            "(n p) b -> p n b", p=128)
        P.dma("sync", s.xt[buf].rearrange("p (n b) -> p n b", b=BLK),
              fm(s.x_in), inc=f"e1_ldx{buf}")
        nldx = P.cnt[f"e1_ldx{buf}"]
        P.dma("sync", s.attnt[buf].rearrange("p (n b) -> p n b", b=BLK),
              fm(s.attn_sp), inc=f"e1_lda{buf}")
        nlda = P.cnt[f"e1_lda{buf}"]
        if t >= 2:
            P.wait("sync", "e1_se", self._sq2_se.get(t - 2, 0))
        P.dma("sync", s.gatet[buf].rearrange("p (n b) -> p n b", b=BLK),
              fm(s.gate_sp), inc=f"e1_ldg{buf}")
        nldg = P.cnt[f"e1_ldg{buf}"]
        if use_corr:
            if t >= 1:
                P.wait("sync", "e1_ve", self._corr_done.get(t - 1, 0))
            else:
                self._corr_done = {}
            P.dma("sync", s.qdect.rearrange("p (n b) -> p n b", b=BLK),
                  fm(s.qdec_sp), inc="e1_ldq")
            nldq = P.cnt["e1_ldq"]

        if use_corr:
            P.wait("vector", f"e1_lda{buf}", nlda)
            if t == 0:
                self._corr_tt = {}
                self._corr_mm = {}
            P.wait("tensor", "e1_ldq", nldq)
            if t == 0 and self.use_coll:
                P.wait("tensor", "kv_ve")
            if t >= 1:
                P.wait("tensor", "e1_ve", self._kvscale_n)
            for h in range(H):
                if h >= 1:
                    P.wait("tensor", "e1_ve", self._corr_tt[h - 1])
                self._corr_mm[h] = P.op("tensor", lambda e, h=h: e.matmul(
                    s.pb[0][:, 0:256], s.kvprevb[:, h * 128:(h + 1) * 128],
                    s.qdect[:, h * BLK:(h + 1) * BLK],
                    start=True, stop=True), inc="e1_pe")
                P.wait("vector", "e1_pe", self._corr_mm[h])
                self._corr_tt[h] = P.op("vector", lambda e, h=h: e.tensor_tensor(
                    s.attnt[buf][:, h * BLK:(h + 1) * BLK],
                    s.attnt[buf][:, h * BLK:(h + 1) * BLK],
                    s.pb[0][:, 0:256], ALU.add), inc="e1_ve")
            self._corr_done[t] = P.cnt["e1_ve"]
            if t < NBLK - 1:
                P.wait("vector", "e1_pe", self._corr_mm[H - 1])
                for h in range(H):
                    P.op("vector", lambda e, h=h: e.tensor_scalar_mul(
                        s.kvprevb[:, h * 128:(h + 1) * 128],
                        s.kvprevb[:, h * 128:(h + 1) * 128],
                        s.bdcol[:, h:h + 1]),
                        inc=("e1_ve" if h == H - 1 else None))
                self._kvscale_n = P.cnt["e1_ve"]
        ncorr = P.cnt.get("e1_ve", 0)

        P.wait("scalar", f"e1_lda{buf}", nlda)
        if use_corr:
            P.wait("scalar", "e1_ve", ncorr)
        if t >= 1:
            P.wait("scalar", "e1_pe", self._sq_cons.get(t - 1, 0))
        else:
            self._sq_cons = {}
        nsq = P.op("scalar", lambda e, buf=buf: e.activation(
            s.sqt, s.attnt[buf], AF.Square), inc="e1_se")
        P.wait("tensor", "e1_se", nsq)
        if t >= 1:
            P.wait("tensor", "e1_ve", self._e1_rows.get(t - 1, 0))
        else:
            self._e1_rows = {}
        for n in range(NC):
            P.op("tensor", lambda e, n=n: e.matmul(
                s.pb[4][0:1, 0:BLK], s.onesb, s.sqt[:, n * BLK:(n + 1) * BLK],
                start=(n == 0), stop=(n == NC - 1)),
                inc=("e1_pe" if n == NC - 1 else None))
        nssq = P.cnt["e1_pe"]
        P.wait("scalar", "e1_pe", nssq)
        P.op("scalar", lambda e: e.activation(
            s.rowsE[:, 0:BLK], s.pb[4][0:1, 0:BLK], AF.Ln, scale=1.0 / D, bias=1e-6))
        nrow1 = P.op("scalar", lambda e: e.activation(
            s.rowsE[:, 0:BLK], s.rowsE[:, 0:BLK], AF.Exp, scale=-0.5), inc="e1_se")
        P.wait("tensor", "e1_se", nrow1)
        if t >= 1:
            P.wait("tensor", "e1_ve", self._z_done.get(t - 1, 0))
        else:
            self._z_done = {}
        nbc1 = P.op("tensor", lambda e: e.matmul(
            s.pb[1][:, 0:BLK], s.onesf, s.rowsE[:, 0:BLK], start=True, stop=True),
            inc="e1_pe")
        if self.dbg:
            P.wait("sync", "e1_ve", ncorr)
            P.dma("sync", self.atc_dp[:, t * BLK:(t + 1) * BLK]
                  .rearrange("(n p) b -> p n b", p=128),
                  s.attnt[buf].rearrange("p (n b) -> p n b", b=BLK), inc="dbg_at")
            P.wait("vector", "dbg_at")
        P.wait("vector", "e1_pe", nbc1)
        P.wait("vector", "e1_se", nsq)
        P.wait("vector", f"e1_ldg{buf}", nldg)
        rrep = s.pb[1][:, 0:BLK].unsqueeze(1).broadcast_to([128, NC, BLK])
        P.op("vector", lambda e, buf=buf, rrep=rrep: e.tensor_tensor(
            s.attnt[buf].rearrange("p (n b) -> p n b", b=BLK),
            s.attnt[buf].rearrange("p (n b) -> p n b", b=BLK), rrep, ALU.mult))
        P.op("vector", lambda e, buf=buf: e.tensor_tensor(
            s.attnt[buf], s.attnt[buf], s.gatet[buf], ALU.mult), inc="e1_ve")
        self._z_done[t] = P.cnt["e1_ve"]
        nz = self._z_done[t]

        P.wait("tensor", "e1_ve", nz)
        P.wait("tensor", "wo_ld", self._wo_n)
        nf = {}
        for f in range(NC):
            pob = s.pb[2 + (f % 2)]
            if f >= 2:
                P.wait("tensor", "e1_ve", nf[f - 2][1])
            for k in range(NC):
                P.op("tensor", lambda e, k=k, f=f, buf=buf, pob=pob: e.matmul(
                    pob[:, 0:256], s.wres[:, k * D + f * 128:k * D + (f + 1) * 128],
                    s.attnt[buf][:, k * BLK:(k + 1) * BLK],
                    start=(k == 0), stop=(k == NC - 1)),
                    inc=("e1_pe" if k == NC - 1 else None))
            npe = P.cnt["e1_pe"]
            P.wait("vector", "e1_pe", npe)
            if f == 0:
                P.wait("vector", f"e1_ldx{buf}", nldx)
                if t >= 2:
                    P.wait("vector", f"e1_sp{buf}", self._e1_sp.get(t - 2, 0))
            nev = P.op("vector", lambda e, f=f, buf=buf, pob=pob: e.tensor_tensor(
                s.x2t[buf][:, f * BLK:(f + 1) * BLK],
                s.xt[buf][:, f * BLK:(f + 1) * BLK], pob[:, 0:256], ALU.add),
                inc="e1_ve")
            nf[f] = (npe, nev)
        nx2 = nf[NC - 1][1]

        P.wait("vector", "e1_ve", nx2)
        nx2b = P.op("vector", lambda e, buf=buf: e.tensor_copy(
            s.gatet[buf], s.x2t[buf]), inc="e1_ve")
        P.wait("scalar", "e1_ve", nx2b)
        nsq2 = P.op("scalar", lambda e, buf=buf: e.activation(
            s.sqt, s.gatet[buf], AF.Square), inc="e1_se")
        self._sq2_se = getattr(self, "_sq2_se", {})
        self._sq2_se[t] = nsq2
        P.wait("tensor", "e1_ve", nx2b)
        P.wait("tensor", "e1_se", nsq2)
        for n in range(NC):
            P.op("tensor", lambda e, n=n, buf=buf: e.matmul(
                s.pb[4][0:1, 0:BLK], s.onesb, s.gatet[buf][:, n * BLK:(n + 1) * BLK],
                start=(n == 0), stop=(n == NC - 1)))
        for n in range(NC):
            P.op("tensor", lambda e, n=n: e.matmul(
                s.pb[4][0:1, BLK:2 * BLK], s.onesb, s.sqt[:, n * BLK:(n + 1) * BLK],
                start=(n == 0), stop=(n == NC - 1)),
                inc=("e1_pe" if n == NC - 1 else None))
        nS = P.cnt["e1_pe"]
        self._sq_cons[t] = nS
        P.wait("vector", "e1_pe", nS)
        P.op("vector", lambda e: e.tensor_scalar_mul(
            s.rowsE[:, 256:512], s.pb[4][0:1, 0:BLK], 1.0 / D))
        P.op("vector", lambda e: e.tensor_tensor(
            s.rowsE[:, 0:256], s.rowsE[:, 256:512], s.rowsE[:, 256:512], ALU.mult))
        P.op("vector", lambda e: e.scalar_tensor_tensor(
            s.rowsE[:, 0:256], s.pb[4][0:1, BLK:2 * BLK], 1.0 / D,
            s.rowsE[:, 0:256], ALU.mult, ALU.subtract), inc="e1_ve")
        nvar = P.cnt["e1_ve"]
        self._e1_rows = getattr(self, "_e1_rows", {})
        P.wait("scalar", "e1_ve", nvar)
        P.op("scalar", lambda e: e.activation(
            s.rowsE[:, 512:768], s.rowsE[:, 0:256], AF.Ln,
            scale=1.0 + 1e-5, bias=1e-10))
        P.op("scalar", lambda e: e.activation(
            s.rowsE[:, 512:768], s.rowsE[:, 512:768], AF.Exp, scale=-0.5),
            inc="e1_se")
        nr = P.cnt["e1_se"]
        self._e1_rows[t] = nvar
        P.wait("tensor", "e1_se", nr)
        P.wait("tensor", "e1_ve", nvar)
        P.wait("tensor", "e1_ve", self._e1_free.get(t - 1, 0))
        nbc2 = P.op("tensor", lambda e: e.matmul(
            s.pb[5][:, 0:512], s.onesf, s.rowsE[:, 256:768],
            start=True, stop=True), inc="e1_pe")
        P.wait("vector", "e1_pe", nbc2)
        mrep = s.pb[5][:, 0:256].unsqueeze(1).broadcast_to([128, NC, BLK])
        rrep2 = s.pb[5][:, 256:512].unsqueeze(1).broadcast_to([128, NC, BLK])
        P.op("vector", lambda e, buf=buf, mrep=mrep: e.tensor_tensor(
            s.xt[buf].rearrange("p (n b) -> p n b", b=BLK),
            s.x2t[buf].rearrange("p (n b) -> p n b", b=BLK), mrep, ALU.subtract))
        P.op("vector", lambda e, buf=buf, rrep2=rrep2: e.tensor_tensor(
            s.xt[buf].rearrange("p (n b) -> p n b", b=BLK),
            s.xt[buf].rearrange("p (n b) -> p n b", b=BLK), rrep2, ALU.mult))
        for n in range(NC):
            P.op("vector", lambda e, n=n, buf=buf: e.tensor_scalar(
                s.attnt[buf][:, n * BLK:(n + 1) * BLK],
                s.xt[buf][:, n * BLK:(n + 1) * BLK],
                s.cvec[:, 3 * NC + n:3 * NC + n + 1],
                s.cvec[:, 4 * NC + n:4 * NC + n + 1],
                ALU.mult, ALU.add),
                inc=("e1_ve" if n == NC - 1 else None))
        nxn = P.cnt["e1_ve"]
        self._e1_free[t] = nxn

        P.wait("sync", "e1_ve", nxn)
        P.dma("sync", s.xn_sp[:, off:off + BLK].rearrange("(n p) b -> p n b", p=128),
              s.attnt[buf].rearrange("p (n b) -> p n b", b=BLK), inc=f"e1_sp{buf}")
        P.dma("sync", s.x2_sp[:, off:off + BLK].rearrange("(n p) b -> p n b", p=128),
              s.x2t[buf].rearrange("p (n b) -> p n b", b=BLK), inc=f"e1_sp{buf}")
        if not hasattr(self, "_e1_sp"):
            self._e1_sp = {}
        self._e1_sp[t] = P.cnt[f"e1_sp{buf}"]

    def emit_signw(self, c):
        P, s = self.P, self
        if c > 0:
            return
        P.wait("sync", "e1_pe")
        P.wait("sync", "e1_ve")
        P.wait("sync", "e1_se")
        P.wait("gpsimd", "e1_pe")
        nred = 0
        for k in range(NC):
            buf = k % 2
            if k >= 2:
                P.wait("sync", "sw_se", P.cnt["sw_se"] - 1)
                P.wait("sync", "sw_ve", P.cnt["sw_ve"] - 1)
            P.dma("sync", s.wfst[buf], s.wf[k], inc=f"sw_ld{buf}")
            nld = P.cnt[f"sw_ld{buf}"]
            P.wait("scalar", f"sw_ld{buf}", nld)
            P.op("scalar", lambda e, k=k, buf=buf: e.activation(
                s.wres[:, k * D:(k + 1) * D], s.wfst[buf], AF.Sign), inc="sw_se")
            P.wait("vector", f"sw_ld{buf}", nld)
            P.op("vector", lambda e, k=k, buf=buf: e.tensor_reduce(
                s.sclc[:, k:k + 1], s.wfst[buf], AX.X, ALU.max,
                apply_absolute_value=True), inc="sw_ve")
            nred = P.cnt["sw_ve"]
        P.wait("scalar", "sw_ve", nred)
        nhop = P.op("scalar", lambda e: e.copy(s.srow[:, 1:2], s.srow[:, 0:1]),
                    inc="sw_se")
        P.wait("vector", "sw_se", nhop)
        ng = P.op("vector", lambda e: e.tensor_reduce(
            s.sclc[:, 16:17], s.sclc[:, 0:16], AX.X, ALU.max), inc="sw_ve")
        P.wait("tensor", "sw_ve", ng)
        P.wait("tensor", "e1_pe")
        ntr2 = P.op("tensor", lambda e: e.transpose(
            s.pb[1][0:1, 0:128], s.sclc[:, 16:17], s.identf), inc="sw_pe")
        P.wait("vector", "sw_pe", ntr2)
        P.op("vector", lambda e: e.tensor_reduce(
            s.srow[:, 0:1], s.pb[1][0:1, 0:128], AX.X, ALU.max), inc="sw_ve")
        nscl = P.cnt["sw_ve"]
        P.wait("tensor", "sw_ve", nscl)
        P.wait("tensor", "e1_pe")
        nbc = P.op("tensor", lambda e: e.matmul(
            s.pb[0][:, 0:1], s.onesf, s.srow[:, 0:1], start=True, stop=True),
            inc="sw_pe")
        P.wait("scalar", "sw_pe", nbc)
        nsc = P.op("scalar", lambda e: e.copy(s.sclc[:, 17:18], s.pb[0][:, 0:1]),
                   inc="sw_se")
        P.wait("vector", "sw_se", nsc)
        P.op("vector", lambda e: e.tensor_scalar_mul(
            s.gscale, s.cvec[:, NC:2 * NC], s.sclc[:, 17:18]))
        P.op("vector", lambda e: e.tensor_copy(s.bprime, s.cvec[:, 2 * NC:3 * NC]),
             inc="sw_ve")
        self._sw_done = P.cnt["sw_ve"]
        self._sw_se = P.cnt["sw_se"]

    def emit_e2(self, c, t2):
        P, s = self.P, self
        T2 = 512
        buf = t2 % 2
        off = c * TOKS + t2 * T2
        if t2 == 0 and c == 0:
            P.wait("sync", "e1_sp0")
            P.wait("sync", "e1_sp1")
            P.wait("sync", "sw_se", self._sw_se)
            P.wait("sync", "e1_ve")
            P.wait("sync", "e1_se")
            self._e2_hist = []
        if len(self._e2_hist) >= 2:
            pe_g, se_g, st_g = self._e2_hist[-2]
            P.wait("sync", "e2_pe", pe_g)
            P.wait("sync", "e2_se", se_g)
            P.wait("sync", f"out_st{buf}", st_g)
        P.dma("sync", s.xn2[buf].rearrange("p (n b) -> p n b", b=T2),
              s.xn_sp[:, off:off + T2].rearrange("(n p) b -> p n b", p=128),
              inc=f"e2_ldn{buf}")
        nldn = P.cnt[f"e2_ldn{buf}"]
        P.dma("sync", s.x22[buf].rearrange("p (n b) -> p n b", b=T2),
              s.x2_sp[:, off:off + T2].rearrange("(n p) b -> p n b", p=128),
              inc=f"e2_ldx{buf}")
        nldx = P.cnt[f"e2_ldx{buf}"]
        P.wait("tensor", f"e2_ldn{buf}", nldn)
        if t2 == 0 and c == 0:
            P.wait("tensor", "sw_se", self._sw_se)
            P.wait("tensor", "sw_ve", self._sw_done)
            P.wait("tensor", "e1_ve")
            P.wait("tensor", "e1_pe")
        nfs = {}
        for f in range(NC):
            pob = s.pb[2 * (f % 2)], s.pb[2 * (f % 2) + 1]
            if f >= 2:
                P.wait("tensor", "e2_se", nfs[f - 2])
            for k in range(NC):
                P.op("tensor", lambda e, k=k, f=f, buf=buf, pob=pob: e.matmul(
                    pob[0][:, 0:512], s.wres[:, k * D + f * 128:k * D + (f + 1) * 128],
                    s.xn2[buf][:, k * T2:k * T2 + 512],
                    start=(k == 0), stop=(k == NC - 1)),
                    inc=("e2_pe" if k == NC - 1 else None))
            npe = P.cnt["e2_pe"]
            P.wait("vector", "e2_pe", npe)
            if f == 0:
                P.wait("vector", f"e2_ldx{buf}", nldx)
            nev = P.op("vector", lambda e, f=f, buf=buf, pob=pob:
                       e.scalar_tensor_tensor(
                           s.x22[buf][:, f * T2:(f + 1) * T2], pob[0][:, 0:512],
                           s.gscale[:, f:f + 1], s.x22[buf][:, f * T2:(f + 1) * T2],
                           ALU.mult, ALU.add), inc="e2_ve")
            nse = P.op("vector", lambda e, f=f, buf=buf: e.tensor_scalar_add(
                s.x22[buf][:, f * T2:(f + 1) * T2],
                s.x22[buf][:, f * T2:(f + 1) * T2],
                s.bprime[:, f:f + 1]), inc="e2_se")
            nfs[f] = nse
        P.wait("sync", "e2_se", P.cnt["e2_se"])
        P.dma("sync", s.out_d[:, off:off + T2].rearrange("(n p) b -> p n b", p=128),
              s.x22[buf].rearrange("p (n b) -> p n b", b=T2), inc=f"out_st{buf}")
        self._e2_hist.append((P.cnt["e2_pe"], P.cnt["e2_se"],
                              P.cnt[f"out_st{buf}"]))

    def emit_sw_dumps(self):
        P, s = self.P, self
        P.wait("vector", "sw_ve", self._sw_done)
        P.wait("sync", "sw_ve", self._sw_done)
        P.dma("sync", s.sw_dp[:, 0:16], s.gscale, inc="dbg_st")
        P.dma("sync", s.sw_dp[:, 16:32], s.bprime, inc="dbg_st")
        P.dma("sync", s.sw_dp[:, 32:36], s.sclc, inc="dbg_st")
        P.wait("sync", "sw_se", self._sw_se)
        P.dma("sync", s.wres_dp, s.wres, inc="dbg_st")
        P.wait("sync", "dbg_st")

    def emit_dumps(self):
        P, s = self.P, self
        P.wait("sync", "b0_se", self.v_done)
        P.dma("sync", s.ln1_dp, s.ln1, inc="dbg_st")
        P.dma("sync", s.vtm_dp, s.vtm, inc="dbg_st")
        P.wait("sync", "dbg_st")


def _stage_weights(L):
    wqkvT = np.ascontiguousarray(L["L0_attn_w_qkv"].T)
    out = {}
    wqh = np.empty((H, 128, D), np.float32)
    wkh = np.empty((H, 128, D), np.float32)
    for h in range(H):
        wqh[h] = wqkvT[:, h * 128:(h + 1) * 128].reshape(NC, 128, 128) \
            .transpose(1, 0, 2).reshape(128, D)
        wkh[h] = wqkvT[:, D + h * 128:D + (h + 1) * 128].reshape(NC, 128, 128) \
            .transpose(1, 0, 2).reshape(128, D)
    out["wq"] = wqh
    out["wk"] = wkh
    out["wv"] = np.ascontiguousarray(wqkvT[:, 2 * D:3 * D].reshape(NC, 128, D))
    out["wg"] = np.ascontiguousarray(L["L0_attn_w_gate"].T.reshape(NC, 128, D))
    out["wo"] = np.ascontiguousarray(L["L0_attn_w_out"].T.reshape(NC, 128, D))
    out["wf"] = np.ascontiguousarray(L["L0_ffn_weight"].T.reshape(NC, 128, D))
    fm = lambda v: np.ascontiguousarray(np.asarray(v, np.float32).reshape(NC, 128).T)
    out["vecs"] = np.ascontiguousarray(np.concatenate(
        [fm(L["L0_attn_norm_w"]), fm(L["L0_ffn_gamma"]), fm(L["L0_ffn_beta"]),
         fm(L["L0_ffn_ln_weight"]), fm(L["L0_ffn_ln_bias"])], axis=1))
    out["slope"] = np.ascontiguousarray(
        np.asarray(L["L0_attn_slope_rate"], np.float32).reshape(1, H))
    ar = np.arange(1, BLK + 1, dtype=np.float32)
    out["arow"] = np.ascontiguousarray(np.broadcast_to(ar, (128, BLK)))
    out["krow"] = np.ascontiguousarray(np.broadcast_to(BLK - ar, (128, BLK)))
    kp = np.arange(2 * 128).reshape(2 * 128, 1).astype(np.float32)
    qp = np.arange(BLK).reshape(1, BLK).astype(np.float32)
    idxT = qp - kp
    idxT[idxT < 0] = BIG
    out["idxT"] = np.ascontiguousarray(idxT.astype(np.float32))
    out["identw"] = np.eye(128, dtype=np.float32)
    return out


_CACHE = {}


def _get_nc(chunks, use_coll, dbg=False):
    key = (chunks, use_coll, dbg)
    if key not in _CACHE:
        k = Ker(chunks, use_coll, dbg)
        _CACHE[key] = (k.build(), k.sbuf_left)
    return _CACHE[key][0]


def kernel(**inputs):
    x = np.asarray(inputs["L0_attn_hidden_states"], np.float32)
    B, N = x.shape[0], x.shape[1]
    staged = _stage_weights(inputs)

    use_v1 = True
    if use_v1:
        nc = _get_nc(1, True)
        in_maps = []
        for c in range(8):
            b, half = c // 2, c % 2
            m = dict(staged)
            m["x"] = np.ascontiguousarray(x[b, half * TOKS:(half + 1) * TOKS, :].T)
            m["mask01"] = np.full((128, 1), float(half), np.float32)
            in_maps.append(m)
        res = run_bass_kernel_spmd(nc, in_maps, core_ids=list(range(8)))
        out = np.empty((B, N, D), np.float32)
        for c in range(8):
            b, half = c // 2, c % 2
            out[b, half * TOKS:(half + 1) * TOKS, :] = res.results[c]["out"].T
        return out
    else:
        nc = _get_nc(2, False)
        in_maps = []
        for c in range(8):
            b = c % 4
            m = dict(staged)
            m["x"] = np.ascontiguousarray(x[b].T)
            m["mask01"] = np.zeros((128, 1), np.float32)
            in_maps.append(m)
        res = run_bass_kernel_spmd(nc, in_maps, core_ids=list(range(8)))
        out = np.empty((B, N, D), np.float32)
        for b in range(4):
            out[b] = res.results[b]["out"].T
        return out


# version 07
# speedup vs baseline: 1.2899x; 1.1567x over previous
import numpy as np
import ml_dtypes
from contextlib import ExitStack

import concourse.bass as bass
import concourse.mybir as mybir
from concourse.bass_utils import run_bass_kernel_spmd

F32 = mybir.dt.float32
BF16 = mybir.dt.bfloat16
U8 = mybir.dt.uint8
AF = mybir.ActivationFunctionType
ALU = mybir.AluOpType
AX = mybir.AxisListType
BFNP = ml_dtypes.bfloat16

D = 2048
H = 16
NC = 16
BLK = 256
TOKS = 2048
NBLK = TOKS // BLK
BIG = 1.0e9
GRP = 4
REPLICA_PAIRS = [[0, 1], [2, 3], [4, 5], [6, 7]]


class Prog:
    ENGINES = ("sync", "tensor", "vector", "scalar", "gpsimd")

    def __init__(self, nc, stack):
        self.nc = nc
        self.stack = stack
        self.ops = {e: [] for e in self.ENGINES}
        self.semh = {}
        self.cnt = {}

    def sem(self, name):
        if name not in self.semh:
            self.semh[name] = self.stack.enter_context(self.nc.semaphore(name))
            self.cnt[name] = 0
        return self.semh[name]

    def op(self, engine, fn, inc=None, by=1):
        if inc is not None:
            h = self.sem(inc)
            self.cnt[inc] += by
            self.ops[engine].append(lambda e, fn=fn, h=h, by=by: fn(e).then_inc(h, by))
            return self.cnt[inc]
        self.ops[engine].append(fn)
        return None

    def wait(self, engine, sem, value=None):
        if value is None:
            value = self.cnt.get(sem, 0)
        assert value <= self.cnt.get(sem, 0), (engine, sem, value)
        if value <= 0:
            return
        h = self.sem(sem)
        self.ops[engine].append(lambda e, h=h, v=value: e.wait_ge(h, v))

    def dma(self, engine, out, in_, inc):
        return self.op(engine, lambda e, o=out, i=in_: e.dma_start(out=o, in_=i),
                       inc=inc, by=16)

    def replay(self):
        with self.nc.Block() as block:
            for en in self.ENGINES:
                opl = self.ops[en]

                def body(eng, opl=opl):
                    for f in opl:
                        f(eng)
                getattr(block, en)(body)


class Ker:
    def __init__(self, chunks, use_coll, dbg=False):
        self.chunks = chunks
        self.use_coll = use_coll
        self.dbg = dbg
        self.TA = TOKS * chunks

    def build(self):
        nc = bass.Bass("TRN2", target_bir_lowering=False, debug=False)
        self.nc = nc
        TA = self.TA
        din = lambda n, sh: nc.dram_tensor(n, sh, F32, kind="ExternalInput").ap()
        self.x_in = din("x", [D, TA])
        self.wq = din("wq", [H, 128, D])
        self.wk = din("wk", [H, 128, D])
        self.wv = din("wv", [NC, 128, D])
        self.wg = din("wg", [NC, 128, D])
        self.wo = din("wo", [NC, 128, D])
        self.wf = din("wf", [NC, 128, D])
        self.vecs = din("vecs", [128, 5 * NC])
        self.slope = din("slope", [1, H])
        self.mask01 = din("mask01", [128, 1])
        self.arow = din("arow", [128, BLK])
        self.krow = din("krow", [128, BLK])
        self.idxT = din("idxT", [2 * 128, BLK])
        self.identw = din("identw", [128, 128])

        self.out_d = nc.dram_tensor("out", [D, TA], F32, kind="ExternalOutput").ap()
        spk = "ExternalOutput" if self.dbg else "Internal"
        self.attn_sp = nc.dram_tensor("attn_sp", [D, TA], BF16, kind=spk).ap()
        self.qdec_sp = nc.dram_tensor("qdec_sp", [D, TA], BF16, kind=spk).ap()
        self.gate_sp = nc.dram_tensor("gate_sp", [D, TA], BF16, kind=spk).ap()
        self.xn_sp = nc.dram_tensor("xn_sp", [D, TA], BF16, kind=spk).ap()
        self.x2_sp = nc.dram_tensor("x2_sp", [D, TA], F32, kind=spk).ap()
        if self.dbg:
            self.ln1_dp = nc.dram_tensor("ln1_dp", [128, NC * TOKS], BF16,
                                         kind="ExternalOutput").ap()
            self.vtm_dp = nc.dram_tensor("vtm_dp", [128, NC * D], BF16,
                                         kind="ExternalOutput").ap()
            self.sw_dp = nc.dram_tensor("sw_dp", [128, 64], F32,
                                        kind="ExternalOutput").ap()
            self.kvp_dp = nc.dram_tensor("kvp_dp", [128, H * 128], BF16,
                                         kind="ExternalOutput").ap()
            self.atc_dp = nc.dram_tensor("atc_dp", [D, TOKS], BF16,
                                         kind="ExternalOutput").ap()
            self.bd_dp = nc.dram_tensor("bd_dp", [128, 144], F32,
                                        kind="ExternalOutput").ap()
            self.wres_dp = nc.dram_tensor("wres_dp", [128, NC * D], BF16,
                                          kind="ExternalOutput").ap()
        self.kvf_d = nc.dram_tensor("kvf_d", [H * 128, 128], F32).ap()
        self.kvg_d = nc.dram_tensor("kvg_d", [2 * H * 128, 128], F32).ap()
        self.scl_d = nc.dram_tensor("scl_d", [128, 1], F32).ap()

        ctx = ExitStack()
        with ctx:
            sb = lambda n, sh, d: ctx.enter_context(nc.sbuf_tensor(n, sh, d))
            psb = lambda n, sh, d=F32: ctx.enter_context(nc.psum_tensor(n, sh, d))
            arA = sb("arA", [128, 65536], U8)[:]
            arB = sb("arB", [128, 65536], U8)[:]
            arC = sb("arC", [128, 49152], U8)[:]
            arD = sb("arD", [128, 8192], U8)[:]
            A2, B2v, C2, D2v = (a.bitcast(BF16) for a in (arA, arB, arC, arD))
            A4, C4 = arA.bitcast(F32), arC.bitcast(F32)

            self.ln1 = A2[:, 0:NC * TOKS]
            self.xt = [A4[:, 0:4096], A4[:, 4096:8192]]
            self.x2t = [A4[:, 8192:12288], A4[:, 12288:16384]]
            self.x22 = [A4[:, 0:8192], A4[:, 8192:16384]]
            self.vtm = B2v[:, 0:NC * D]
            self.wres = B2v[:, 0:NC * D]
            self.xblk = [C4[:, 0:2048], C4[:, 2048:4096]]
            self.xbb = [C2[:, 8192:10240], C2[:, 10240:12288]]
            self.sqb = [C2[:, 12288:14336], C2[:, 14336:16384]]
            self.wvsl = [C2[:, 0:8192], C2[:, 8192:16384]]
            self.qraw = C2[:, 0:2048]
            self.qdech = C2[:, 2048:4096]
            self.kfm = C2[:, 4096:6144]
            self.kkd = C2[:, 6144:8192]
            self.attnh = C2[:, 8192:10240]
            self.qkT = C2[:, 10240:10752]
            self.kkdT = C2[:, 10752:11008]
            self.wqh_r = [C2[:, 11264:13312], C2[:, 15360:17408]]
            self.wkh_r = [C2[:, 13312:15360], C2[:, 17408:19456]]
            self.attnt = [C2[:, 0:4096], C2[:, 4096:8192]]
            self.gatet = [C2[:, 8192:12288], C2[:, 12288:16384]]
            self.qdect = C2[:, 16384:20480]
            self.sqt = C2[:, 20480:24576]
            self.xn2 = [C2[:, 0:8192], C2[:, 8192:16384]]
            self.wfst = [C4[:, 8192:10240], C4[:, 10240:12288]]
            self.qdecs = D2v[:, 0:GRP * BLK]
            self.kdpats = D2v[:, GRP * BLK:2 * GRP * BLK]
            self.diags = D2v[:, 2 * GRP * BLK:4 * GRP * BLK]
            D4 = arD.bitcast(F32)
            self.rowsA = D4[0:1, 0:512]
            self.rowsE = D4[0:1, 512:1536]
            self.gatesb = D2v[:, 0:2048]

            self.cvec = sb("cvec", [128, 5 * NC], F32)[:]
            self.gscale = sb("gscale", [128, NC], F32)[:]
            self.bprime = sb("bprime", [128, NC], F32)[:]
            self.lnbb = sb("lnbb", [128, NC], BF16)[:]
            self.nsl = sb("nsl", [128, H], F32)[:]
            self.bdcol = sb("bdcol", [128, H], F32)[:]
            self.bdpow = sb("bdpow", [128, NBLK * H], F32)[:]
            self.onesb = sb("onesb", [128, 1], BF16)[:]
            self.onesf = sb("onesf", [1, 128], F32)[:]
            self.identb = sb("identb", [128, 128], BF16)[:]
            self.identf = sb("identf", [128, 128], F32)[:]
            self.maskc = sb("maskc", [128, 1], F32)[:]
            self.arow_s = sb("arow_s", [128, BLK], F32)[:]
            self.krow_s = sb("krow_s", [128, BLK], F32)[:]
            self.idxT_s = sb("idxT_s", [128, 2 * BLK], F32)[:]
            self.slope_s = sb("slope_s", [1, H], F32)[:]
            self.kvst = sb("kvst", [128, H * 128], F32)[:]
            self.kvbf = sb("kvbf", [128, 128], BF16)[:]
            self.kvprevb = sb("kvprevb", [128, H * 128], BF16)[:]
            self.u1t = sb("u1t", [128, BLK], F32)[:]
            self.qbt = sb("qbt", [128, 2 * BLK], BF16)[:]
            self.sclrow = sb("sclrow", [1, 256], F32)[:]
            self.sclc = sb("sclc", [128, 20], F32)[:]
            self.srow = sb("srow", [1, 8], F32)[:]

            self.cpool = sb("cpool", [128, 16], F32)[:]
            self.cvals = [0.0, 1.0, 1e-5, 1e-6, -0.5, 1.0 / D,
                          1.0 + 1e-5, 1e-10] + \
                [256.0 * j for j in range(1, NBLK)]
            for i, v in enumerate(self.cvals):
                nc.const_aps.register(F32, v, self.cpool[:, i:i + 1])

            self.pb = [psb(f"pb{i}", [128, 512]) for i in range(6)]
            self.pbt = psb("pbt", [128, 256], BF16)
            self.pkv = psb("pkv", [128, 128])

            self.sbuf_left = nc.sbuf_bytes_remaining

            P = Prog(nc, ctx)
            self.P = P
            self.emit()
            P.replay()
        return nc

    def emit(self):
        P = self.P
        self._kkdT_n = 0
        self._qkT_n = 0
        self.emit_consts()
        for c in range(self.chunks):
            self.emit_ln1(c)
            self.emit_vproj(c)
            if self.dbg and c == 0:
                self.emit_dumps()
            for h in range(H):
                self.emit_head(c, h)
            self.emit_gate(c)
        if self.use_coll:
            self.emit_collective()
        else:
            pass
        for c in range(self.chunks):
            self.emit_wout_load(c)
            for t in range(NBLK):
                self.emit_e1(c, t)
        for c in range(self.chunks):
            self.emit_signw(c)
            if self.dbg and c == 0:
                self.emit_sw_dumps()
            for t2 in range(TOKS // 512):
                self.emit_e2(c, t2)
        P.wait("gpsimd", "out_st0")
        P.wait("gpsimd", "out_st1")
        P.op("gpsimd", lambda e: e.nop())

    def emit_consts(self):
        P, s = self.P, self
        P.dma("sync", s.cvec, s.vecs, inc="c_ld")
        P.dma("sync", s.slope_s, s.slope, inc="c_ld")
        P.dma("sync", s.maskc, s.mask01, inc="c_ld")
        P.dma("sync", s.arow_s, s.arow, inc="c_ld")
        P.dma("sync", s.krow_s, s.krow, inc="c_ld")
        P.dma("sync", s.idxT_s.rearrange("p (a c) -> p a c", a=2),
              s.idxT.rearrange("(a p) c -> p a c", p=128), inc="c_ld")
        P.dma("gpsimd", s.identb, s.identw, inc="c_ldg")
        P.dma("sync", s.identf, s.identw, inc="c_ld")
        P.op("gpsimd", lambda e: e.memset(s.onesb, 1.0))
        P.op("gpsimd", lambda e: e.memset(s.onesf, 1.0))
        P.op("gpsimd", lambda e: e.memset(s.kvst, 0.0))
        P.op("gpsimd", lambda e: e.memset(s.kvbf, 0.0))
        P.op("gpsimd", lambda e: e.memset(s.kvprevb, 0.0))
        P.op("gpsimd", lambda e: e.memset(s.bdpow[:, 0:16], 1.0))
        for i, v in enumerate(s.cvals):
            P.op("gpsimd", lambda e, i=i, v=v: e.memset(s.cpool[:, i:i + 1], v))
        P.op("gpsimd", lambda e: e.engine_nop(), inc="c_ms")
        P.wait("tensor", "c_ld")
        P.wait("tensor", "c_ldg")
        P.wait("tensor", "c_ms")
        P.op("tensor", lambda e: e.matmul(s.pb[5][:, 0:16], s.onesf, s.slope_s,
                                          start=True, stop=True), inc="c_pe")
        P.wait("vector", "c_pe")
        P.op("vector", lambda e: e.tensor_scalar_mul(s.nsl, s.pb[5][:, 0:16], -1.0),
             inc="c_ve")
        P.wait("scalar", "c_ve")
        P.op("scalar", lambda e: e.activation(s.bdcol, s.nsl, AF.Exp, scale=256.0),
             inc="c_se")
        for j in range(1, NBLK):
            P.op("scalar", lambda e, j=j: e.activation(
                s.bdpow[:, j * 16:(j + 1) * 16], s.nsl, AF.Exp, scale=256.0 * j),
                inc="c_se")

    def emit_ln1(self, c):
        P, s = self.P, self
        SB = 128
        nsb = TOKS // SB
        for sem in ("e2_se", "e2_pe", "e2_ve", "e1_ve", "e1_pe", "b1_pe", "b1_ve",
                    "b1_se", "b2_pe", "b2_se", "b2_sp0", "b2_sp1", "out_st0",
                    "out_st1", "e1_sp0", "e1_sp1", "c_ve", "c_se"):
            P.wait("sync", sem)
            P.wait("gpsimd", sem)
            P.wait("tensor", sem)
            P.wait("vector", sem)
            P.wait("scalar", sem)
        done_ve = {}
        done_pe = {}
        rows_done = {}
        for j in range(nsb):
            buf = j % 2
            t0 = c * TOKS + j * SB
            if j >= 2:
                P.wait("sync", "a1_ve", done_ve[j - 2])
                P.wait("gpsimd", "a1_pe", done_pe[j - 2])
            src = s.x_in[:, t0:t0 + SB].rearrange("(n p) b -> p n b", p=128)
            P.dma("sync", s.xblk[buf].rearrange("p (n b) -> p n b", b=SB), src,
                  inc=f"a1_ldx{buf}")
            nx = P.cnt[f"a1_ldx{buf}"]
            P.dma("gpsimd", s.xbb[buf].rearrange("p (n b) -> p n b", b=SB), src,
                  inc=f"a1_ldxb{buf}")
            nxb = P.cnt[f"a1_ldxb{buf}"]
            P.wait("scalar", f"a1_ldxb{buf}", nxb)
            if j >= 2:
                P.wait("scalar", "a1_pe", done_pe[j - 2])
            P.op("scalar", lambda e, buf=buf: e.activation(
                s.sqb[buf], s.xbb[buf], AF.Square), inc="a1_se")
            nse = P.cnt["a1_se"]
            P.wait("tensor", f"a1_ldxb{buf}", nxb)
            P.wait("tensor", "a1_se", nse)
            if j >= 1:
                P.wait("tensor", "a1_rows", rows_done[j - 1])
            for n in range(NC):
                P.op("tensor", lambda e, n=n, buf=buf: e.matmul(
                    s.pb[4][0:1, 0:SB], s.onesb,
                    s.xbb[buf][:, n * SB:(n + 1) * SB],
                    start=(n == 0), stop=(n == NC - 1)))
            for n in range(NC):
                P.op("tensor", lambda e, n=n, buf=buf: e.matmul(
                    s.pb[4][0:1, SB:2 * SB], s.onesb,
                    s.sqb[buf][:, n * SB:(n + 1) * SB],
                    start=(n == 0), stop=(n == NC - 1)),
                    inc=("a1_pe" if n == NC - 1 else None))
            done_pe[j] = P.cnt["a1_pe"]
            P.wait("vector", "a1_pe", done_pe[j])
            P.op("vector", lambda e: e.tensor_scalar_mul(
                s.rowsA[:, 0:SB], s.pb[4][0:1, 0:SB], 1.0 / D))
            P.op("vector", lambda e: e.tensor_tensor(
                s.rowsE[:, 0:SB], s.rowsA[:, 0:SB], s.rowsA[:, 0:SB], ALU.mult))
            P.op("vector", lambda e: e.scalar_tensor_tensor(
                s.rowsA[:, SB:2 * SB], s.pb[4][0:1, SB:2 * SB], 1.0 / D,
                s.rowsE[:, 0:SB], ALU.mult, ALU.subtract), inc="a1_ve")
            P.wait("scalar", "a1_ve", P.cnt["a1_ve"])
            P.op("scalar", lambda e: e.activation(
                s.rowsA[:, SB:2 * SB], s.rowsA[:, SB:2 * SB], AF.Ln, bias=1e-5))
            P.op("scalar", lambda e: e.activation(
                s.rowsA[:, SB:2 * SB], s.rowsA[:, SB:2 * SB], AF.Exp, scale=-0.5),
                inc="a1_se")
            P.wait("vector", "a1_se", P.cnt["a1_se"])
            P.op("vector", lambda e: e.scalar_tensor_tensor(
                s.rowsA[:, 0:SB], s.rowsA[:, 0:SB], -1.0,
                s.rowsA[:, SB:2 * SB], ALU.mult, ALU.mult), inc="a1_rows")
            rows_done[j] = P.cnt["a1_rows"]
            P.wait("tensor", "a1_rows", rows_done[j])
            if j >= 1:
                P.wait("tensor", "a1_ve", done_ve[j - 1])
            P.op("tensor", lambda e: e.matmul(
                s.pb[5][:, 0:2 * SB], s.onesf, s.rowsA[:, 0:2 * SB],
                start=True, stop=True), inc="a1_bc")
            P.wait("vector", "a1_bc", P.cnt["a1_bc"])
            P.wait("vector", f"a1_ldx{buf}", nx)
            for n in range(NC):
                dst = s.ln1[:, n * TOKS + j * SB: n * TOKS + (j + 1) * SB]
                P.op("vector", lambda e, n=n, buf=buf, dst=dst: e.tensor_tensor(
                    dst, s.xblk[buf][:, n * SB:(n + 1) * SB],
                    s.pb[5][:, SB:2 * SB], ALU.mult))
                P.op("vector", lambda e, dst=dst: e.tensor_tensor(
                    dst, dst, s.pb[5][:, 0:SB], ALU.add),
                    inc=("a1_ve" if n == NC - 1 else None))
            done_ve[j] = P.cnt["a1_ve"]
        self.ln1_done = P.cnt["a1_ve"]

    def emit_vproj(self, c):
        P, s = self.P, self
        for sem in ("a1_pe", "a1_ve", "b1_pe", "b1_ve", "b1_se", "b2_pe",
                    "e2_pe"):
            P.wait("gpsimd", sem)
        evac_pe = {}
        for vf in range(4):
            buf = vf % 2
            if vf >= 2:
                P.wait("gpsimd", "b0_pe", evac_pe[vf - 2])
            P.dma("gpsimd", s.wvsl[buf].rearrange("p (k cc) -> p k cc", cc=512),
                  s.wv[:, :, vf * 512:(vf + 1) * 512].rearrange("k p cc -> p k cc"),
                  inc=f"b0_wld{buf}")
            nw = P.cnt[f"b0_wld{buf}"]
            P.wait("tensor", f"b0_wld{buf}", nw)
            if vf == 0:
                P.wait("tensor", "a1_ve", self.ln1_done)
                P.wait("tensor", "b1_pe")
                P.wait("tensor", "b2_pe")
                P.wait("tensor", "b2_se")
                P.wait("tensor", "e1_pe")
                P.wait("tensor", "e2_pe")
                P.wait("tensor", "e2_ve")
            for tc in range(NC):
                pbank = s.pb[tc % 2]
                if tc >= 2:
                    P.wait("tensor", "b0_se", P.cnt["b0_se"] - 1)
                for k in range(NC):
                    P.op("tensor", lambda e, k=k, tc=tc, buf=buf, pbank=pbank: e.matmul(
                        pbank[:, 0:512],
                        s.ln1[:, k * TOKS + tc * 128:k * TOKS + (tc + 1) * 128],
                        s.wvsl[buf][:, k * 512:(k + 1) * 512],
                        start=(k == 0), stop=(k == NC - 1)),
                        inc=("b0_pe" if k == NC - 1 else None))
                P.wait("scalar", "b0_pe", P.cnt["b0_pe"])
                P.op("scalar", lambda e, tc=tc, vf=vf, pbank=pbank: e.activation(
                    s.vtm[:, tc * D + vf * 512: tc * D + (vf + 1) * 512],
                    pbank[:, 0:512], AF.Silu), inc="b0_se")
            evac_pe[vf] = P.cnt["b0_pe"]
        self.v_done = P.cnt["b0_se"]

    def emit_decay_group(self, g):
        P, s = self.P, self
        P.wait("scalar", "b1_pe")
        P.wait("scalar", "b1_ve")
        P.wait("scalar", "a1_bc")
        P.wait("scalar", "a1_ve")
        P.wait("scalar", "b2_sp0")
        P.wait("scalar", "b2_sp1")
        for i in range(GRP):
            h = g * GRP + i
            sc = s.nsl[:, h:h + 1]
            P.op("scalar", lambda e, i=i, sc=sc: e.activation(
                s.qdecs[:, i * BLK:(i + 1) * BLK], s.arow_s, AF.Exp, scale=sc))
            P.op("scalar", lambda e, i=i, sc=sc: e.activation(
                s.kdpats[:, i * BLK:(i + 1) * BLK], s.krow_s, AF.Exp, scale=sc))
            P.op("scalar", lambda e, i=i, sc=sc: e.activation(
                s.diags[:, i * 2 * BLK:(i + 1) * 2 * BLK], s.idxT_s, AF.Exp,
                scale=sc), inc="b1_se")

    def emit_head(self, c, h):
        P, s = self.P, self
        if h % GRP == 0:
            self.emit_decay_group((h % H) // GRP)
        hg = h % GRP

        hb = h % 2
        wqh, wkh = s.wqh_r[hb], s.wkh_r[hb]
        if h == 0:
            self._proj_pe = {}
            for sem in ("b0_pe", "b0_se", "a1_se", "e1_se", "e1_ve", "e2_pe",
                        "b1_pe"):
                P.wait("gpsimd", sem)
        else:
            P.wait("gpsimd", "b1_pe", self._proj_pe.get(h - 2, 0))
        P.dma("gpsimd", wqh, s.wq[h], inc=f"b1_wld{hb}")
        P.dma("gpsimd", wkh, s.wk[h], inc=f"b1_wld{hb}")
        nw = P.cnt[f"b1_wld{hb}"]

        P.wait("tensor", f"b1_wld{hb}", nw)
        if h == 0:
            P.wait("tensor", "b0_se", self.v_done)
        evac = {}
        for proj in range(2):
            banks = (s.pb[0], s.pb[1]) if proj == 0 else (s.pb[2], s.pb[3])
            dstr = s.qraw if proj == 0 else s.kfm
            wt = wqh if proj == 0 else wkh
            for qt in range(4):
                bank = banks[qt % 2]
                P.wait("tensor", "b1_se", evac.get((proj, qt - 2),
                                                   self._b1_proj_guard(proj)))
                if qt == 0 and proj == 0:
                    P.wait("tensor", "b1_sp")
                for k in range(NC):
                    P.op("tensor", lambda e, k=k, qt=qt, wt=wt, bank=bank: e.matmul(
                        bank[:, 0:512], wt[:, k * 128:(k + 1) * 128],
                        s.ln1[:, k * TOKS + qt * 512:k * TOKS + (qt + 1) * 512],
                        start=(k == 0), stop=(k == NC - 1)),
                        inc=("b1_pe" if k == NC - 1 else None))
                npe = P.cnt["b1_pe"]
                P.wait("scalar", "b1_pe", npe)
                if qt == 0 and proj == 0:
                    P.wait("scalar", "b1_sp")
                    P.wait("scalar", "b1_ve")
                n_ev = P.op("scalar", lambda e, qt=qt, bank=bank, dstr=dstr:
                            e.activation(dstr[:, qt * 512:(qt + 1) * 512],
                                         bank[:, 0:512], AF.Silu), inc="b1_se")
                evac[(proj, qt)] = n_ev
        self._b1_q_evac = evac[(0, 3)]
        self._b1_k_evac = evac[(1, 3)]
        self._proj_pe[h] = P.cnt["b1_pe"]

        P.wait("vector", "b1_sp")
        P.wait("vector", "b1_se", self._b1_q_evac)
        rep = lambda ap: ap.unsqueeze(1).broadcast_to([128, NBLK, BLK])
        P.op("vector", lambda e: e.tensor_tensor(
            s.qdech.rearrange("p (r b) -> p r b", b=BLK),
            s.qraw.rearrange("p (r b) -> p r b", b=BLK),
            rep(s.qdecs[:, hg * BLK:(hg + 1) * BLK]), ALU.mult), inc="b1_ve")
        P.wait("vector", "b1_se", self._b1_k_evac)
        P.op("vector", lambda e: e.tensor_tensor(
            s.kkd.rearrange("p (r b) -> p r b", b=BLK),
            s.kfm.rearrange("p (r b) -> p r b", b=BLK),
            rep(s.kdpats[:, hg * BLK:(hg + 1) * BLK]), ALU.mult), inc="b1_ve")
        n_dec = P.cnt["b1_ve"]

        hs = h * 128
        for j in range(NBLK):
            t0 = j * BLK
            P.wait("scalar", "b1_ve")
            P.wait("scalar", "b1_pe")
            nbf = P.op("scalar", lambda e: e.copy(
                s.kvbf, s.kvst[:, hs:hs + 128]), inc="b1_se")
            P.wait("tensor", "b1_ve", n_dec)
            P.wait("tensor", "b1_ve", self._kkdT_n)
            P.op("tensor", lambda e, t0=t0: e.transpose(
                self.pbt[:][:, 0:128], s.kkd[:, t0:t0 + 128], s.identb))
            ntr = P.op("tensor", lambda e, t0=t0: e.transpose(
                self.pbt[:][:, 128:256], s.kkd[:, t0 + 128:t0 + 256], s.identb),
                inc="b1_pe")
            P.wait("tensor", "b1_ve", self._qkT_n)
            P.op("tensor", lambda e, t0=t0: e.matmul(
                s.pb[4][:, 0:256], s.kfm[:, t0:t0 + 128],
                s.qraw[:, t0:t0 + 256], start=True, stop=True))
            nqk = P.op("tensor", lambda e, t0=t0: e.matmul(
                s.pb[4][:, 256:512], s.kfm[:, t0 + 128:t0 + 256],
                s.qraw[:, t0:t0 + 256], start=True, stop=True), inc="b1_pe")
            P.wait("vector", "b1_pe", ntr)
            self._kkdT_n = P.op("vector", lambda e: e.tensor_copy(
                s.kkdT, self.pbt[:, 0:256]), inc="b1_ve")
            P.wait("vector", "b1_pe", nqk)
            P.op("vector", lambda e: e.tensor_tensor(
                s.qkT[:, 0:256], s.pb[4][:, 0:256],
                s.diags[:, hg * 2 * BLK:hg * 2 * BLK + 256], ALU.mult))
            self._qkT_n = P.op("vector", lambda e: e.tensor_tensor(
                s.qkT[:, 256:512], s.pb[4][:, 256:512],
                s.diags[:, hg * 2 * BLK + 256:hg * 2 * BLK + 512], ALU.mult),
                inc="b1_ve")
            P.wait("tensor", "b1_ve", self._qkT_n)
            P.wait("tensor", "b1_se", nbf)
            if True:
                P.wait("tensor", "b1_se")
            vt0 = (2 * j) * D + hs
            vt1 = (2 * j + 1) * D + hs
            P.op("tensor", lambda e, vt0=vt0: e.matmul(
                s.pb[5][:, 0:256], s.vtm[:, vt0:vt0 + 128], s.qkT[:, 0:256],
                start=True, stop=False))
            P.op("tensor", lambda e, vt1=vt1: e.matmul(
                s.pb[5][:, 0:256], s.vtm[:, vt1:vt1 + 128], s.qkT[:, 256:512],
                start=False, stop=False))
            nat = P.op("tensor", lambda e, t0=t0: e.matmul(
                s.pb[5][:, 0:256], s.kvbf, s.qdech[:, t0:t0 + 256],
                start=False, stop=True), inc="b1_pe")
            P.wait("tensor", "b1_ve", self._kkdT_n)
            P.op("tensor", lambda e, vt0=vt0: e.matmul(
                s.pkv[:], s.kkdT[:, 0:128], s.vtm[:, vt0:vt0 + 128],
                start=True, stop=False))
            nkv = P.op("tensor", lambda e, vt1=vt1: e.matmul(
                s.pkv[:], s.kkdT[:, 128:256], s.vtm[:, vt1:vt1 + 128],
                start=False, stop=True), inc="b1_pe")
            P.wait("scalar", "b1_pe", nat)
            P.op("scalar", lambda e, t0=t0: e.copy(
                s.attnh[:, t0:t0 + 256], s.pb[5][:, 0:256]), inc="b1_se")
            P.wait("vector", "b1_pe", nkv)
            P.op("vector", lambda e: e.scalar_tensor_tensor(
                s.kvst[:, hs:hs + 128], s.kvst[:, hs:hs + 128],
                s.bdcol[:, h:h + 1], s.pkv[:], ALU.mult, ALU.add), inc="b1_ve")
        self._kv_done = P.cnt["b1_ve"]

        P.wait("sync", "b1_se")
        P.dma("sync", s.attn_sp[h * 128:(h + 1) * 128, c * TOKS:(c + 1) * TOKS],
              s.attnh, inc="b1_sp")
        if self.use_coll:
            P.wait("sync", "b1_ve", n_dec)
            P.dma("sync", s.qdec_sp[h * 128:(h + 1) * 128, c * TOKS:(c + 1) * TOKS],
                  s.qdech, inc="b1_sp")

    def _b1_proj_guard(self, proj):
        return self.P.cnt.get("b1_se", 0)

    def emit_gate(self, c):
        P, s = self.P, self
        P.wait("gpsimd", "b1_pe")
        P.wait("gpsimd", "b1_ve")
        for k in range(NC):
            P.dma("gpsimd", s.wres[:, k * D:(k + 1) * D], s.wg[k], inc="b2_wld")
        nw = P.cnt["b2_wld"]
        P.wait("tensor", "b2_wld", nw)
        P.wait("tensor", "b1_se")
        P.wait("tensor", "b1_ve")
        P.wait("scalar", "b1_ve")
        P.wait("scalar", "b1_pe")
        evacs = []
        for th in range(2):
            for f in range(NC):
                fi = th * NC + f
                b0, b1 = (s.pb[0], s.pb[1]) if f % 2 == 0 else (s.pb[2], s.pb[3])
                if fi >= 2:
                    P.wait("tensor", "b2_se", evacs[fi - 2])
                for k in range(NC):
                    P.op("tensor", lambda e, k=k, f=f, th=th, b0=b0: e.matmul(
                        b0[:, 0:512], s.wres[:, k * D + f * 128:k * D + (f + 1) * 128],
                        s.ln1[:, k * TOKS + th * 1024:k * TOKS + th * 1024 + 512],
                        start=(k == 0), stop=(k == NC - 1)))
                    P.op("tensor", lambda e, k=k, f=f, th=th, b1=b1: e.matmul(
                        b1[:, 0:512], s.wres[:, k * D + f * 128:k * D + (f + 1) * 128],
                        s.ln1[:, k * TOKS + th * 1024 + 512:k * TOKS + th * 1024 + 1024],
                        start=(k == 0), stop=(k == NC - 1)),
                        inc=("b2_pe" if k == NC - 1 else None))
                npe = P.cnt["b2_pe"]
                gb = s.gatesb[:, (f % 2) * 1024:(f % 2) * 1024 + 1024]
                P.wait("scalar", "b2_pe", npe)
                P.wait("scalar", f"b2_sp{f % 2}",
                       P.cnt.get(f"b2_sp{f % 2}", 0))
                P.op("scalar", lambda e, gb=gb, b0=b0: e.activation(
                    gb[:, 0:512], b0[:, 0:512], AF.Sigmoid))
                nse = P.op("scalar", lambda e, gb=gb, b1=b1: e.activation(
                    gb[:, 512:1024], b1[:, 0:512], AF.Sigmoid), inc="b2_se")
                evacs.append(nse)
                P.wait("vector", "b2_se", nse)
                nve = P.op("vector", lambda e, gb=gb, f=f: e.tensor_scalar_mul(
                    gb, gb, s.cvec[:, f:f + 1]), inc="b2_ve")
                P.wait("sync", "b2_ve", nve)
                P.wait("sync", "b2_se", nse)
                P.dma("sync",
                      s.gate_sp[f * 128:(f + 1) * 128,
                                c * TOKS + th * 1024:c * TOKS + th * 1024 + 1024],
                      gb, inc=f"b2_sp{f % 2}")

    def emit_collective(self):
        P, s = self.P, self
        P.wait("gpsimd", "b1_ve")
        P.dma("gpsimd", s.kvf_d.rearrange("(h p) d -> p h d", p=128),
              s.kvst.rearrange("p (h d) -> p h d", d=128), inc="kv_sp")
        P.wait("gpsimd", "kv_sp")
        P.op("gpsimd", lambda e: e.collective_compute(
            "AllGather", ALU.bypass, replica_groups=REPLICA_PAIRS,
            ins=[s.kvf_d], outs=[s.kvg_d]), inc="cc_s")
        P.wait("gpsimd", "cc_s")
        P.dma("gpsimd", s.kvprevb.rearrange("p (h d) -> p h d", d=128),
              s.kvg_d[0:H * 128, :].rearrange("(h p) d -> p h d", p=128),
              inc="kv_ld")
        P.wait("vector", "kv_ld")
        P.op("vector", lambda e: e.tensor_scalar_mul(s.kvprevb, s.kvprevb, s.maskc),
             inc="kv_ve")
        if self.dbg:
            P.wait("sync", "kv_ve")
            P.dma("sync", s.kvp_dp, s.kvprevb, inc="dbg_st")
            P.wait("sync", "c_ve")
            P.wait("sync", "c_se")
            P.dma("sync", s.bd_dp[:, 0:16], s.bdcol, inc="dbg_st")
            P.dma("sync", s.bd_dp[:, 16:144], s.bdpow, inc="dbg_st")

    def emit_wout_load(self, c):
        if c > 0:
            return
        P, s = self.P, self
        P.wait("gpsimd", "b2_pe")
        for k in range(NC):
            P.dma("gpsimd", s.wres[:, k * D:(k + 1) * D], s.wo[k], inc="wo_ld")
        self._wo_n = P.cnt["wo_ld"]

    def emit_e1(self, c, t):
        P, s = self.P, self
        buf = t % 2
        off = c * TOKS + t * BLK
        use_corr = self.use_coll

        if t == 0:
            for sem in ("b1_pe", "b1_ve", "b1_se", "b2_pe", "b2_se", "a1_ve",
                        "a1_pe", "b2_sp0", "b2_sp1", "b1_sp", "e1_ve", "e1_pe",
                        "e1_se", "e1_sp0", "e1_sp1"):
                P.wait("sync", sem)
                P.wait("tensor", sem)
                P.wait("vector", sem)
                P.wait("scalar", sem)
            self._e1_free = {}
        else:
            if t >= 2:
                P.wait("sync", "e1_ve", self._e1_free[t - 2])
                P.wait("sync", f"e1_sp{buf}", self._e1_sp.get(t - 2, 0))
        fm = lambda apdram: apdram[:, off:off + BLK].rearrange(
            "(n p) b -> p n b", p=128)
        P.dma("sync", s.xt[buf].rearrange("p (n b) -> p n b", b=BLK),
              fm(s.x_in), inc=f"e1_ldx{buf}")
        nldx = P.cnt[f"e1_ldx{buf}"]
        P.dma("sync", s.attnt[buf].rearrange("p (n b) -> p n b", b=BLK),
              fm(s.attn_sp), inc=f"e1_lda{buf}")
        nlda = P.cnt[f"e1_lda{buf}"]
        if t >= 2:
            P.wait("sync", "e1_se", self._sq2_se.get(t - 2, 0))
        P.dma("sync", s.gatet[buf].rearrange("p (n b) -> p n b", b=BLK),
              fm(s.gate_sp), inc=f"e1_ldg{buf}")
        nldg = P.cnt[f"e1_ldg{buf}"]
        if use_corr:
            if t >= 1:
                P.wait("sync", "e1_ve", self._corr_done.get(t - 1, 0))
            else:
                self._corr_done = {}
            P.dma("sync", s.qdect.rearrange("p (n b) -> p n b", b=BLK),
                  fm(s.qdec_sp), inc="e1_ldq")
            nldq = P.cnt["e1_ldq"]

        if use_corr:
            P.wait("vector", f"e1_lda{buf}", nlda)
            if t == 0:
                self._corr_tt = {}
                self._corr_mm = {}
            P.wait("tensor", "e1_ldq", nldq)
            if t == 0 and self.use_coll:
                P.wait("tensor", "kv_ve")
            if t >= 1:
                P.wait("tensor", "e1_ve", self._kvscale_n)
            if t >= 1:
                P.wait("tensor", "e1_ve", self._z_done.get(t - 1, 0))
            for h in range(H):
                pc = s.pb[h % 2]
                if h >= 2:
                    P.wait("tensor", "e1_ve", self._corr_tt[h - 2])
                self._corr_mm[h] = P.op("tensor", lambda e, h=h, pc=pc: e.matmul(
                    pc[:, 0:256], s.kvprevb[:, h * 128:(h + 1) * 128],
                    s.qdect[:, h * BLK:(h + 1) * BLK],
                    start=True, stop=True), inc="e1_pe")
                P.wait("vector", "e1_pe", self._corr_mm[h])
                self._corr_tt[h] = P.op("vector", lambda e, h=h, pc=pc: e.tensor_tensor(
                    s.attnt[buf][:, h * BLK:(h + 1) * BLK],
                    s.attnt[buf][:, h * BLK:(h + 1) * BLK],
                    pc[:, 0:256], ALU.add), inc="e1_ve")
            self._corr_done[t] = P.cnt["e1_ve"]
            if t < NBLK - 1:
                P.wait("vector", "e1_pe", self._corr_mm[H - 1])
                for h in range(H):
                    P.op("vector", lambda e, h=h: e.tensor_scalar_mul(
                        s.kvprevb[:, h * 128:(h + 1) * 128],
                        s.kvprevb[:, h * 128:(h + 1) * 128],
                        s.bdcol[:, h:h + 1]),
                        inc=("e1_ve" if h == H - 1 else None))
                self._kvscale_n = P.cnt["e1_ve"]
        ncorr = P.cnt.get("e1_ve", 0)

        P.wait("scalar", f"e1_lda{buf}", nlda)
        if use_corr:
            P.wait("scalar", "e1_ve", ncorr)
        if t >= 1:
            P.wait("scalar", "e1_pe", self._sq_cons.get(t - 1, 0))
        else:
            self._sq_cons = {}
        nsq = P.op("scalar", lambda e, buf=buf: e.activation(
            s.sqt, s.attnt[buf], AF.Square), inc="e1_se")
        P.wait("tensor", "e1_se", nsq)
        if t >= 1:
            P.wait("tensor", "e1_ve", self._e1_rows.get(t - 1, 0))
        else:
            self._e1_rows = {}
        for n in range(NC):
            P.op("tensor", lambda e, n=n: e.matmul(
                s.pb[4][0:1, 0:BLK], s.onesb, s.sqt[:, n * BLK:(n + 1) * BLK],
                start=(n == 0), stop=(n == NC - 1)),
                inc=("e1_pe" if n == NC - 1 else None))
        nssq = P.cnt["e1_pe"]
        P.wait("scalar", "e1_pe", nssq)
        P.op("scalar", lambda e: e.activation(
            s.rowsE[:, 0:BLK], s.pb[4][0:1, 0:BLK], AF.Ln, scale=1.0 / D, bias=1e-6))
        nrow1 = P.op("scalar", lambda e: e.activation(
            s.rowsE[:, 0:BLK], s.rowsE[:, 0:BLK], AF.Exp, scale=-0.5), inc="e1_se")
        P.wait("tensor", "e1_se", nrow1)
        if t >= 1:
            P.wait("tensor", "e1_ve", self._z_done.get(t - 1, 0))
        else:
            self._z_done = {}
        if use_corr:
            P.wait("tensor", "e1_ve", self._corr_done[t])
        nbc1 = P.op("tensor", lambda e: e.matmul(
            s.pb[1][:, 0:BLK], s.onesf, s.rowsE[:, 0:BLK], start=True, stop=True),
            inc="e1_pe")
        if self.dbg:
            P.wait("sync", "e1_ve", ncorr)
            P.dma("sync", self.atc_dp[:, t * BLK:(t + 1) * BLK]
                  .rearrange("(n p) b -> p n b", p=128),
                  s.attnt[buf].rearrange("p (n b) -> p n b", b=BLK), inc="dbg_at")
            P.wait("vector", "dbg_at")
        P.wait("vector", "e1_pe", nbc1)
        P.wait("vector", "e1_se", nsq)
        P.wait("vector", f"e1_ldg{buf}", nldg)
        rrep = s.pb[1][:, 0:BLK].unsqueeze(1).broadcast_to([128, NC, BLK])
        P.op("vector", lambda e, buf=buf, rrep=rrep: e.tensor_tensor(
            s.attnt[buf].rearrange("p (n b) -> p n b", b=BLK),
            s.attnt[buf].rearrange("p (n b) -> p n b", b=BLK), rrep, ALU.mult))
        P.op("vector", lambda e, buf=buf: e.tensor_tensor(
            s.attnt[buf], s.attnt[buf], s.gatet[buf], ALU.mult), inc="e1_ve")
        self._z_done[t] = P.cnt["e1_ve"]
        nz = self._z_done[t]

        P.wait("tensor", "e1_ve", nz)
        P.wait("tensor", "wo_ld", self._wo_n)
        nf = {}
        for f in range(NC):
            pob = s.pb[2 + (f % 2)]
            if f >= 2:
                P.wait("tensor", "e1_ve", nf[f - 2][1])
            for k in range(NC):
                P.op("tensor", lambda e, k=k, f=f, buf=buf, pob=pob: e.matmul(
                    pob[:, 0:256], s.wres[:, k * D + f * 128:k * D + (f + 1) * 128],
                    s.attnt[buf][:, k * BLK:(k + 1) * BLK],
                    start=(k == 0), stop=(k == NC - 1)),
                    inc=("e1_pe" if k == NC - 1 else None))
            npe = P.cnt["e1_pe"]
            P.wait("vector", "e1_pe", npe)
            if f == 0:
                P.wait("vector", f"e1_ldx{buf}", nldx)
                if t >= 2:
                    P.wait("vector", f"e1_sp{buf}", self._e1_sp.get(t - 2, 0))
            nev = P.op("vector", lambda e, f=f, buf=buf, pob=pob: e.tensor_tensor(
                s.x2t[buf][:, f * BLK:(f + 1) * BLK],
                s.xt[buf][:, f * BLK:(f + 1) * BLK], pob[:, 0:256], ALU.add),
                inc="e1_ve")
            nf[f] = (npe, nev)
        nx2 = nf[NC - 1][1]

        P.wait("vector", "e1_ve", nx2)
        nx2b = P.op("vector", lambda e, buf=buf: e.tensor_copy(
            s.gatet[buf], s.x2t[buf]), inc="e1_ve")
        P.wait("scalar", "e1_ve", nx2b)
        nsq2 = P.op("scalar", lambda e, buf=buf: e.activation(
            s.sqt, s.gatet[buf], AF.Square), inc="e1_se")
        self._sq2_se = getattr(self, "_sq2_se", {})
        self._sq2_se[t] = nsq2
        P.wait("tensor", "e1_ve", nx2b)
        P.wait("tensor", "e1_se", nsq2)
        for n in range(NC):
            P.op("tensor", lambda e, n=n, buf=buf: e.matmul(
                s.pb[4][0:1, 0:BLK], s.onesb, s.gatet[buf][:, n * BLK:(n + 1) * BLK],
                start=(n == 0), stop=(n == NC - 1)))
        for n in range(NC):
            P.op("tensor", lambda e, n=n: e.matmul(
                s.pb[4][0:1, BLK:2 * BLK], s.onesb, s.sqt[:, n * BLK:(n + 1) * BLK],
                start=(n == 0), stop=(n == NC - 1)),
                inc=("e1_pe" if n == NC - 1 else None))
        nS = P.cnt["e1_pe"]
        self._sq_cons[t] = nS
        P.wait("vector", "e1_pe", nS)
        P.op("vector", lambda e: e.tensor_scalar_mul(
            s.rowsE[:, 256:512], s.pb[4][0:1, 0:BLK], 1.0 / D))
        P.op("vector", lambda e: e.tensor_tensor(
            s.rowsE[:, 0:256], s.rowsE[:, 256:512], s.rowsE[:, 256:512], ALU.mult))
        P.op("vector", lambda e: e.scalar_tensor_tensor(
            s.rowsE[:, 0:256], s.pb[4][0:1, BLK:2 * BLK], 1.0 / D,
            s.rowsE[:, 0:256], ALU.mult, ALU.subtract), inc="e1_ve")
        nvar = P.cnt["e1_ve"]
        self._e1_rows = getattr(self, "_e1_rows", {})
        P.wait("scalar", "e1_ve", nvar)
        P.op("scalar", lambda e: e.activation(
            s.rowsE[:, 512:768], s.rowsE[:, 0:256], AF.Ln,
            scale=1.0 + 1e-5, bias=1e-10))
        P.op("scalar", lambda e: e.activation(
            s.rowsE[:, 512:768], s.rowsE[:, 512:768], AF.Exp, scale=-0.5),
            inc="e1_se")
        nr = P.cnt["e1_se"]
        self._e1_rows[t] = nvar
        P.wait("tensor", "e1_se", nr)
        P.wait("tensor", "e1_ve", nvar)
        P.wait("tensor", "e1_ve", self._e1_free.get(t - 1, 0))
        nbc2 = P.op("tensor", lambda e: e.matmul(
            s.pb[5][:, 0:512], s.onesf, s.rowsE[:, 256:768],
            start=True, stop=True), inc="e1_pe")
        P.wait("vector", "e1_pe", nbc2)
        mrep = s.pb[5][:, 0:256].unsqueeze(1).broadcast_to([128, NC, BLK])
        rrep2 = s.pb[5][:, 256:512].unsqueeze(1).broadcast_to([128, NC, BLK])
        P.op("vector", lambda e, buf=buf, mrep=mrep: e.tensor_tensor(
            s.xt[buf].rearrange("p (n b) -> p n b", b=BLK),
            s.x2t[buf].rearrange("p (n b) -> p n b", b=BLK), mrep, ALU.subtract))
        P.op("vector", lambda e, buf=buf, rrep2=rrep2: e.tensor_tensor(
            s.xt[buf].rearrange("p (n b) -> p n b", b=BLK),
            s.xt[buf].rearrange("p (n b) -> p n b", b=BLK), rrep2, ALU.mult))
        for n in range(NC):
            P.op("vector", lambda e, n=n, buf=buf: e.tensor_scalar(
                s.attnt[buf][:, n * BLK:(n + 1) * BLK],
                s.xt[buf][:, n * BLK:(n + 1) * BLK],
                s.cvec[:, 3 * NC + n:3 * NC + n + 1],
                s.cvec[:, 4 * NC + n:4 * NC + n + 1],
                ALU.mult, ALU.add),
                inc=("e1_ve" if n == NC - 1 else None))
        nxn = P.cnt["e1_ve"]
        self._e1_free[t] = nxn

        P.wait("gpsimd", "e1_ve", nxn)
        P.dma("gpsimd", s.xn_sp[:, off:off + BLK].rearrange("(n p) b -> p n b", p=128),
              s.attnt[buf].rearrange("p (n b) -> p n b", b=BLK), inc=f"e1_sp{buf}")
        P.dma("gpsimd", s.x2_sp[:, off:off + BLK].rearrange("(n p) b -> p n b", p=128),
              s.x2t[buf].rearrange("p (n b) -> p n b", b=BLK), inc=f"e1_sp{buf}")
        if not hasattr(self, "_e1_sp"):
            self._e1_sp = {}
        self._e1_sp[t] = P.cnt[f"e1_sp{buf}"]

    def emit_signw(self, c):
        P, s = self.P, self
        if c > 0:
            return
        P.wait("sync", "e1_pe")
        P.wait("sync", "e1_ve")
        P.wait("sync", "e1_se")
        P.wait("gpsimd", "e1_pe")
        nred = 0
        for k in range(NC):
            buf = k % 2
            if k >= 2:
                P.wait("sync", "sw_se", P.cnt["sw_se"] - 1)
                P.wait("sync", "sw_ve", P.cnt["sw_ve"] - 1)
            P.dma("sync", s.wfst[buf], s.wf[k], inc=f"sw_ld{buf}")
            nld = P.cnt[f"sw_ld{buf}"]
            P.wait("scalar", f"sw_ld{buf}", nld)
            P.op("scalar", lambda e, k=k, buf=buf: e.activation(
                s.wres[:, k * D:(k + 1) * D], s.wfst[buf], AF.Sign), inc="sw_se")
            P.wait("vector", f"sw_ld{buf}", nld)
            P.op("vector", lambda e, k=k, buf=buf: e.tensor_reduce(
                s.sclc[:, k:k + 1], s.wfst[buf], AX.X, ALU.max,
                apply_absolute_value=True), inc="sw_ve")
            nred = P.cnt["sw_ve"]
        P.wait("scalar", "sw_ve", nred)
        nhop = P.op("scalar", lambda e: e.copy(s.srow[:, 1:2], s.srow[:, 0:1]),
                    inc="sw_se")
        P.wait("vector", "sw_se", nhop)
        ng = P.op("vector", lambda e: e.tensor_reduce(
            s.sclc[:, 16:17], s.sclc[:, 0:16], AX.X, ALU.max), inc="sw_ve")
        P.wait("tensor", "sw_ve", ng)
        P.wait("tensor", "e1_pe")
        ntr2 = P.op("tensor", lambda e: e.transpose(
            s.pb[1][0:1, 0:128], s.sclc[:, 16:17], s.identf), inc="sw_pe")
        P.wait("vector", "sw_pe", ntr2)
        P.op("vector", lambda e: e.tensor_reduce(
            s.srow[:, 0:1], s.pb[1][0:1, 0:128], AX.X, ALU.max), inc="sw_ve")
        nscl = P.cnt["sw_ve"]
        P.wait("tensor", "sw_ve", nscl)
        P.wait("tensor", "e1_pe")
        nbc = P.op("tensor", lambda e: e.matmul(
            s.pb[0][:, 0:1], s.onesf, s.srow[:, 0:1], start=True, stop=True),
            inc="sw_pe")
        P.wait("scalar", "sw_pe", nbc)
        nsc = P.op("scalar", lambda e: e.copy(s.sclc[:, 17:18], s.pb[0][:, 0:1]),
                   inc="sw_se")
        P.wait("vector", "sw_se", nsc)
        P.op("vector", lambda e: e.tensor_scalar_mul(
            s.gscale, s.cvec[:, NC:2 * NC], s.sclc[:, 17:18]))
        P.op("vector", lambda e: e.tensor_copy(s.bprime, s.cvec[:, 2 * NC:3 * NC]),
             inc="sw_ve")
        self._sw_done = P.cnt["sw_ve"]
        self._sw_se = P.cnt["sw_se"]

    def emit_e2(self, c, t2):
        P, s = self.P, self
        T2 = 512
        buf = t2 % 2
        off = c * TOKS + t2 * T2
        if t2 == 0 and c == 0:
            P.wait("sync", "e1_sp0")
            P.wait("sync", "e1_sp1")
            P.wait("sync", "sw_se", self._sw_se)
            P.wait("sync", "e1_ve")
            P.wait("sync", "e1_se")
            self._e2_hist = []
        if len(self._e2_hist) >= 2:
            pe_g, se_g, st_g = self._e2_hist[-2]
            P.wait("sync", "e2_pe", pe_g)
            P.wait("sync", "e2_se", se_g)
            P.wait("sync", f"out_st{buf}", st_g)
        P.dma("sync", s.xn2[buf].rearrange("p (n b) -> p n b", b=T2),
              s.xn_sp[:, off:off + T2].rearrange("(n p) b -> p n b", p=128),
              inc=f"e2_ldn{buf}")
        nldn = P.cnt[f"e2_ldn{buf}"]
        P.dma("sync", s.x22[buf].rearrange("p (n b) -> p n b", b=T2),
              s.x2_sp[:, off:off + T2].rearrange("(n p) b -> p n b", p=128),
              inc=f"e2_ldx{buf}")
        nldx = P.cnt[f"e2_ldx{buf}"]
        P.wait("tensor", f"e2_ldn{buf}", nldn)
        if t2 == 0 and c == 0:
            P.wait("tensor", "sw_se", self._sw_se)
            P.wait("tensor", "sw_ve", self._sw_done)
            P.wait("tensor", "e1_ve")
            P.wait("tensor", "e1_pe")
        nfs = {}
        for f in range(NC):
            pob = s.pb[2 * (f % 2)], s.pb[2 * (f % 2) + 1]
            if f >= 2:
                P.wait("tensor", "e2_se", nfs[f - 2])
            for k in range(NC):
                P.op("tensor", lambda e, k=k, f=f, buf=buf, pob=pob: e.matmul(
                    pob[0][:, 0:512], s.wres[:, k * D + f * 128:k * D + (f + 1) * 128],
                    s.xn2[buf][:, k * T2:k * T2 + 512],
                    start=(k == 0), stop=(k == NC - 1)),
                    inc=("e2_pe" if k == NC - 1 else None))
            npe = P.cnt["e2_pe"]
            P.wait("vector", "e2_pe", npe)
            if f == 0:
                P.wait("vector", f"e2_ldx{buf}", nldx)
            nev = P.op("vector", lambda e, f=f, buf=buf, pob=pob:
                       e.scalar_tensor_tensor(
                           s.x22[buf][:, f * T2:(f + 1) * T2], pob[0][:, 0:512],
                           s.gscale[:, f:f + 1], s.x22[buf][:, f * T2:(f + 1) * T2],
                           ALU.mult, ALU.add), inc="e2_ve")
            nse = P.op("vector", lambda e, f=f, buf=buf: e.tensor_scalar_add(
                s.x22[buf][:, f * T2:(f + 1) * T2],
                s.x22[buf][:, f * T2:(f + 1) * T2],
                s.bprime[:, f:f + 1]), inc="e2_se")
            nfs[f] = nse
        P.wait("gpsimd", "e2_se", P.cnt["e2_se"])
        P.dma("gpsimd", s.out_d[:, off:off + T2].rearrange("(n p) b -> p n b", p=128),
              s.x22[buf].rearrange("p (n b) -> p n b", b=T2), inc=f"out_st{buf}")
        self._e2_hist.append((P.cnt["e2_pe"], P.cnt["e2_se"],
                              P.cnt[f"out_st{buf}"]))

    def emit_sw_dumps(self):
        P, s = self.P, self
        P.wait("vector", "sw_ve", self._sw_done)
        P.wait("sync", "sw_ve", self._sw_done)
        P.dma("sync", s.sw_dp[:, 0:16], s.gscale, inc="dbg_st")
        P.dma("sync", s.sw_dp[:, 16:32], s.bprime, inc="dbg_st")
        P.dma("sync", s.sw_dp[:, 32:36], s.sclc, inc="dbg_st")
        P.wait("sync", "sw_se", self._sw_se)
        P.dma("sync", s.wres_dp, s.wres, inc="dbg_st")
        P.wait("sync", "dbg_st")

    def emit_dumps(self):
        P, s = self.P, self
        P.wait("sync", "b0_se", self.v_done)
        P.dma("sync", s.ln1_dp, s.ln1, inc="dbg_st")
        P.dma("sync", s.vtm_dp, s.vtm, inc="dbg_st")
        P.wait("sync", "dbg_st")


def _stage_weights(L):
    wqkvT = np.ascontiguousarray(L["L0_attn_w_qkv"].T)
    out = {}
    wqh = np.empty((H, 128, D), np.float32)
    wkh = np.empty((H, 128, D), np.float32)
    for h in range(H):
        wqh[h] = wqkvT[:, h * 128:(h + 1) * 128].reshape(NC, 128, 128) \
            .transpose(1, 0, 2).reshape(128, D)
        wkh[h] = wqkvT[:, D + h * 128:D + (h + 1) * 128].reshape(NC, 128, 128) \
            .transpose(1, 0, 2).reshape(128, D)
    out["wq"] = wqh
    out["wk"] = wkh
    out["wv"] = np.ascontiguousarray(wqkvT[:, 2 * D:3 * D].reshape(NC, 128, D))
    out["wg"] = np.ascontiguousarray(L["L0_attn_w_gate"].T.reshape(NC, 128, D))
    out["wo"] = np.ascontiguousarray(L["L0_attn_w_out"].T.reshape(NC, 128, D))
    out["wf"] = np.ascontiguousarray(L["L0_ffn_weight"].T.reshape(NC, 128, D))
    fm = lambda v: np.ascontiguousarray(np.asarray(v, np.float32).reshape(NC, 128).T)
    out["vecs"] = np.ascontiguousarray(np.concatenate(
        [fm(L["L0_attn_norm_w"]), fm(L["L0_ffn_gamma"]), fm(L["L0_ffn_beta"]),
         fm(L["L0_ffn_ln_weight"]), fm(L["L0_ffn_ln_bias"])], axis=1))
    out["slope"] = np.ascontiguousarray(
        np.asarray(L["L0_attn_slope_rate"], np.float32).reshape(1, H))
    ar = np.arange(1, BLK + 1, dtype=np.float32)
    out["arow"] = np.ascontiguousarray(np.broadcast_to(ar, (128, BLK)))
    out["krow"] = np.ascontiguousarray(np.broadcast_to(BLK - ar, (128, BLK)))
    kp = np.arange(2 * 128).reshape(2 * 128, 1).astype(np.float32)
    qp = np.arange(BLK).reshape(1, BLK).astype(np.float32)
    idxT = qp - kp
    idxT[idxT < 0] = BIG
    out["idxT"] = np.ascontiguousarray(idxT.astype(np.float32))
    out["identw"] = np.eye(128, dtype=np.float32)
    return out


_CACHE = {}


def _get_nc(chunks, use_coll, dbg=False):
    key = (chunks, use_coll, dbg)
    if key not in _CACHE:
        k = Ker(chunks, use_coll, dbg)
        _CACHE[key] = (k.build(), k.sbuf_left)
    return _CACHE[key][0]


def kernel(**inputs):
    x = np.asarray(inputs["L0_attn_hidden_states"], np.float32)
    B, N = x.shape[0], x.shape[1]
    staged = _stage_weights(inputs)

    use_v1 = True
    if use_v1:
        nc = _get_nc(1, True)
        in_maps = []
        for c in range(8):
            b, half = c // 2, c % 2
            m = dict(staged)
            m["x"] = np.ascontiguousarray(x[b, half * TOKS:(half + 1) * TOKS, :].T)
            m["mask01"] = np.full((128, 1), float(half), np.float32)
            in_maps.append(m)
        res = run_bass_kernel_spmd(nc, in_maps, core_ids=list(range(8)))
        out = np.empty((B, N, D), np.float32)
        for c in range(8):
            b, half = c // 2, c % 2
            out[b, half * TOKS:(half + 1) * TOKS, :] = res.results[c]["out"].T
        return out
    else:
        nc = _get_nc(2, False)
        in_maps = []
        for c in range(8):
            b = c % 4
            m = dict(staged)
            m["x"] = np.ascontiguousarray(x[b].T)
            m["mask01"] = np.zeros((128, 1), np.float32)
            in_maps.append(m)
        res = run_bass_kernel_spmd(nc, in_maps, core_ids=list(range(8)))
        out = np.empty((B, N, D), np.float32)
        for b in range(4):
            out[b] = res.results[b]["out"].T
        return out


# version 08
# speedup vs baseline: 1.4109x; 1.0937x over previous
import numpy as np
import ml_dtypes
from contextlib import ExitStack

import concourse.bass as bass
import concourse.mybir as mybir
from concourse.bass_utils import run_bass_kernel_spmd

F32 = mybir.dt.float32
BF16 = mybir.dt.bfloat16
U8 = mybir.dt.uint8
AF = mybir.ActivationFunctionType
ALU = mybir.AluOpType
AX = mybir.AxisListType
BFNP = ml_dtypes.bfloat16

D = 2048
H = 16
NC = 16
BLK = 256
TOKS = 2048
NBLK = TOKS // BLK
BIG = 1.0e9
GRP = 4
REPLICA_PAIRS = [[0, 1], [2, 3], [4, 5], [6, 7]]


class Prog:
    ENGINES = ("sync", "tensor", "vector", "scalar", "gpsimd")

    def __init__(self, nc, stack):
        self.nc = nc
        self.stack = stack
        self.ops = {e: [] for e in self.ENGINES}
        self.semh = {}
        self.cnt = {}

    def sem(self, name):
        if name not in self.semh:
            self.semh[name] = self.stack.enter_context(self.nc.semaphore(name))
            self.cnt[name] = 0
        return self.semh[name]

    def op(self, engine, fn, inc=None, by=1):
        if inc is not None:
            h = self.sem(inc)
            self.cnt[inc] += by
            self.ops[engine].append(lambda e, fn=fn, h=h, by=by: fn(e).then_inc(h, by))
            return self.cnt[inc]
        self.ops[engine].append(fn)
        return None

    def wait(self, engine, sem, value=None):
        if value is None:
            value = self.cnt.get(sem, 0)
        assert value <= self.cnt.get(sem, 0), (engine, sem, value)
        if value <= 0:
            return
        h = self.sem(sem)
        self.ops[engine].append(lambda e, h=h, v=value: e.wait_ge(h, v))

    def dma(self, engine, out, in_, inc):
        return self.op(engine, lambda e, o=out, i=in_: e.dma_start(out=o, in_=i),
                       inc=inc, by=16)

    def replay(self):
        with self.nc.Block() as block:
            for en in self.ENGINES:
                opl = self.ops[en]

                def body(eng, opl=opl):
                    for f in opl:
                        f(eng)
                getattr(block, en)(body)


class Ker:
    def __init__(self, chunks, use_coll, dbg=False):
        self.chunks = chunks
        self.use_coll = use_coll
        self.dbg = dbg
        self.TA = TOKS * chunks

    def build(self):
        nc = bass.Bass("TRN2", target_bir_lowering=False, debug=False)
        self.nc = nc
        TA = self.TA
        din = lambda n, sh: nc.dram_tensor(n, sh, F32, kind="ExternalInput").ap()
        self.x_in = din("x", [D, TA])
        self.wq = din("wq", [H, 128, D])
        self.wk = din("wk", [H, 128, D])
        self.wv = din("wv", [NC, 128, D])
        self.wg = din("wg", [NC, 128, D])
        self.wo = din("wo", [NC, 128, D])
        self.wf = din("wf", [NC, 128, D])
        self.vecs = din("vecs", [128, 5 * NC])
        self.slope = din("slope", [1, H])
        self.mask01 = din("mask01", [128, 1])
        self.arow = din("arow", [128, BLK])
        self.krow = din("krow", [128, BLK])
        self.krowt = din("krowt", [128, 2])
        self.idxT = din("idxT", [2 * 128, BLK])
        self.identw = din("identw", [128, 128])

        self.out_d = nc.dram_tensor("out", [D, TA], F32, kind="ExternalOutput").ap()
        spk = "ExternalOutput" if self.dbg else "Internal"
        self.attn_sp = nc.dram_tensor("attn_sp", [D, TA], BF16, kind=spk).ap()
        self.qdec_sp = nc.dram_tensor("qdec_sp", [D, TA], BF16, kind=spk).ap()
        self.gate_sp = nc.dram_tensor("gate_sp", [D, TA], BF16, kind=spk).ap()
        self.xn_sp = nc.dram_tensor("xn_sp", [D, TA], BF16, kind=spk).ap()
        self.x2_sp = nc.dram_tensor("x2_sp", [D, TA], F32, kind=spk).ap()
        if self.dbg:
            self.ln1_dp = nc.dram_tensor("ln1_dp", [128, NC * TOKS], BF16,
                                         kind="ExternalOutput").ap()
            self.vtm_dp = nc.dram_tensor("vtm_dp", [128, NC * D], BF16,
                                         kind="ExternalOutput").ap()
            self.sw_dp = nc.dram_tensor("sw_dp", [128, 64], F32,
                                        kind="ExternalOutput").ap()
            self.kvp_dp = nc.dram_tensor("kvp_dp", [128, H * 128], BF16,
                                         kind="ExternalOutput").ap()
            self.atc_dp = nc.dram_tensor("atc_dp", [D, TOKS], BF16,
                                         kind="ExternalOutput").ap()
            self.bd_dp = nc.dram_tensor("bd_dp", [128, 144], F32,
                                        kind="ExternalOutput").ap()
            self.wres_dp = nc.dram_tensor("wres_dp", [128, NC * D], BF16,
                                          kind="ExternalOutput").ap()
        self.kvf_d = nc.dram_tensor("kvf_d", [H * 128, 128], F32).ap()
        self.kvg_d = nc.dram_tensor("kvg_d", [2 * H * 128, 128], F32).ap()
        self.scl_d = nc.dram_tensor("scl_d", [128, 1], F32).ap()

        ctx = ExitStack()
        with ctx:
            sb = lambda n, sh, d: ctx.enter_context(nc.sbuf_tensor(n, sh, d))
            psb = lambda n, sh, d=F32: ctx.enter_context(nc.psum_tensor(n, sh, d))
            arA = sb("arA", [128, 65536], U8)[:]
            arB = sb("arB", [128, 65536], U8)[:]
            arC = sb("arC", [128, 49152], U8)[:]
            arD = sb("arD", [128, 8192], U8)[:]
            A2, B2v, C2, D2v = (a.bitcast(BF16) for a in (arA, arB, arC, arD))
            A4, C4 = arA.bitcast(F32), arC.bitcast(F32)

            self.ln1 = A2[:, 0:NC * TOKS]
            self.xt = [A4[:, 0:4096], A4[:, 4096:8192]]
            self.x2t = [A4[:, 8192:12288], A4[:, 12288:16384]]
            self.x22 = [A4[:, 0:8192], A4[:, 8192:16384]]
            self.vtm = B2v[:, 0:NC * D]
            self.wres = B2v[:, 0:NC * D]
            self.xblk = [C4[:, 0:2048], C4[:, 2048:4096]]
            self.xbb = [C2[:, 8192:10240], C2[:, 10240:12288]]
            self.sqb = [C2[:, 12288:14336], C2[:, 14336:16384]]
            self.wvsl = [C2[:, 0:8192], C2[:, 8192:16384]]
            self.qraw_r = [C2[:, 0:2048], C2[:, 2048:4096]]
            self.qdech_r = [C2[:, 4096:6144], C2[:, 6144:8192]]
            self.kfm_r = [C2[:, 8192:10240], C2[:, 10240:12288]]
            self.attnh = C2[:, 12288:14336]
            self.wqh_r = [C2[:, 14336:16384], C2[:, 16384:18432]]
            self.wkh_r = [C2[:, 18432:20480], C2[:, 20480:22528]]
            self.attnt = [C2[:, 0:4096], C2[:, 4096:8192]]
            self.gatet = [C2[:, 8192:12288], C2[:, 12288:16384]]
            self.qdect = C2[:, 16384:20480]
            self.sqt = C2[:, 20480:24576]
            self.xn2 = [C2[:, 0:8192], C2[:, 8192:16384]]
            self.wfst = [C4[:, 8192:10240], C4[:, 10240:12288]]
            self.qdec_r = [D2v[:, 0:BLK], D2v[:, BLK:2 * BLK]]
            self.diag_r = [D2v[:, 2 * BLK:4 * BLK], D2v[:, 4 * BLK:6 * BLK]]
            self.qkT = D2v[:, 6 * BLK:8 * BLK]
            self.kkdT = D2v[:, 8 * BLK:9 * BLK]
            self.gatesb = D2v[:, 0:2048]

            self.cvec = sb("cvec", [128, 5 * NC], F32)[:]
            self.gscale = sb("gscale", [128, NC], F32)[:]
            self.bprime = sb("bprime", [128, NC], F32)[:]
            self.lnbb = sb("lnbb", [128, NC], BF16)[:]
            self.nsl = sb("nsl", [128, H], F32)[:]
            self.bdcol = sb("bdcol", [128, H], F32)[:]
            self.bdpow = sb("bdpow", [128, NBLK * H], F32)[:]
            self.onesb = sb("onesb", [128, 1], BF16)[:]
            self.onesf = sb("onesf", [1, 128], F32)[:]
            self.identb = sb("identb", [128, 128], BF16)[:]
            self.identf = sb("identf", [128, 128], F32)[:]
            self.maskc = sb("maskc", [128, 1], F32)[:]
            self.arow_s = sb("arow_s", [128, BLK], F32)[:]
            self.idxT_s = sb("idxT_s", [128, 2 * BLK], F32)[:]
            self.slope_s = sb("slope_s", [1, H], F32)[:]
            self.kvst = sb("kvst", [128, H * 128], F32)[:]
            self.kvbf = sb("kvbf", [128, 128], BF16)[:]
            self.kvprevb = sb("kvprevb", [128, H * 128], BF16)[:]
            self.rowsA = sb("rowsA", [1, 512], F32)[:]
            self.rowsE = sb("rowsE", [1, 768], F32)[:]
            self.kdcolT = sb("kdcolT", [128, 4], F32)[:]
            self.krowt_s = sb("krowt_s", [128, 2], F32)[:]
            self.sclc = sb("sclc", [128, 20], F32)[:]
            self.srow = sb("srow", [1, 8], F32)[:]

            self.cpool = sb("cpool", [128, 16], F32)[:]
            self.cvals = [0.0, 1.0, 1e-5, 1e-6, -0.5, 1.0 / D,
                          1.0 + 1e-5, 1e-10] + \
                [256.0 * j for j in range(1, NBLK)]
            for i, v in enumerate(self.cvals):
                nc.const_aps.register(F32, v, self.cpool[:, i:i + 1])

            self.pb = [psb(f"pb{i}", [128, 512]) for i in range(6)]
            self.pbt = psb("pbt", [128, 256], BF16)
            self.pkv = psb("pkv", [128, 128])

            self.sbuf_left = nc.sbuf_bytes_remaining

            P = Prog(nc, ctx)
            self.P = P
            self.emit()
            P.replay()
        return nc

    def emit(self):
        P = self.P
        self._kkdT_n = 0
        self._qkT_n = 0
        self.emit_consts()
        for c in range(self.chunks):
            self.emit_ln1(c)
            self.emit_vproj(c)
            if self.dbg and c == 0:
                self.emit_dumps()
            for h in range(H):
                self.emit_head_pipe(c, h)
            self.emit_gate(c)
        if self.use_coll:
            self.emit_collective()
        else:
            pass
        for c in range(self.chunks):
            self.emit_wout_load(c)
            for t in range(NBLK):
                self.emit_e1(c, t)
        for c in range(self.chunks):
            self.emit_signw(c)
            if self.dbg and c == 0:
                self.emit_sw_dumps()
            for t2 in range(TOKS // 512):
                self.emit_e2(c, t2)
        P.wait("gpsimd", "out_st0")
        P.wait("gpsimd", "out_st1")
        P.op("gpsimd", lambda e: e.nop())

    def emit_consts(self):
        P, s = self.P, self
        P.dma("sync", s.cvec, s.vecs, inc="c_ld")
        P.dma("sync", s.slope_s, s.slope, inc="c_ld")
        P.dma("sync", s.maskc, s.mask01, inc="c_ld")
        P.dma("sync", s.arow_s, s.arow, inc="c_ld")
        P.dma("sync", s.krowt_s, s.krowt, inc="c_ld")
        P.dma("sync", s.idxT_s.rearrange("p (a c) -> p a c", a=2),
              s.idxT.rearrange("(a p) c -> p a c", p=128), inc="c_ld")
        P.dma("gpsimd", s.identb, s.identw, inc="c_ldg")
        P.dma("sync", s.identf, s.identw, inc="c_ld")
        P.op("gpsimd", lambda e: e.memset(s.onesb, 1.0))
        P.op("gpsimd", lambda e: e.memset(s.onesf, 1.0))
        P.op("gpsimd", lambda e: e.memset(s.kvst, 0.0))
        P.op("gpsimd", lambda e: e.memset(s.kvbf, 0.0))
        P.op("gpsimd", lambda e: e.memset(s.kvprevb, 0.0))
        P.op("gpsimd", lambda e: e.memset(s.bdpow[:, 0:16], 1.0))
        for i, v in enumerate(s.cvals):
            P.op("gpsimd", lambda e, i=i, v=v: e.memset(s.cpool[:, i:i + 1], v))
        P.op("gpsimd", lambda e: e.engine_nop(), inc="c_ms")
        P.wait("tensor", "c_ld")
        P.wait("tensor", "c_ldg")
        P.wait("tensor", "c_ms")
        P.op("tensor", lambda e: e.matmul(s.pb[5][:, 0:16], s.onesf, s.slope_s,
                                          start=True, stop=True), inc="c_pe")
        P.wait("vector", "c_pe")
        P.op("vector", lambda e: e.tensor_scalar_mul(s.nsl, s.pb[5][:, 0:16], -1.0),
             inc="c_ve")
        P.wait("scalar", "c_ve")
        P.op("scalar", lambda e: e.activation(s.bdcol, s.nsl, AF.Exp, scale=256.0),
             inc="c_se")
        for j in range(1, NBLK):
            P.op("scalar", lambda e, j=j: e.activation(
                s.bdpow[:, j * 16:(j + 1) * 16], s.nsl, AF.Exp, scale=256.0 * j),
                inc="c_se")

    def emit_ln1(self, c):
        P, s = self.P, self
        SB = 128
        nsb = TOKS // SB
        for sem in ("e2_se", "e2_pe", "e2_ve", "e1_ve", "e1_pe", "b1_pe", "b1_ve",
                    "b1_se", "b2_pe", "b2_se", "b2_sp0", "b2_sp1", "out_st0",
                    "out_st1", "e1_sp0", "e1_sp1", "c_ve", "c_se"):
            P.wait("sync", sem)
            P.wait("gpsimd", sem)
            P.wait("tensor", sem)
            P.wait("vector", sem)
            P.wait("scalar", sem)
        done_ve = {}
        done_pe = {}
        rows_done = {}
        for j in range(nsb):
            buf = j % 2
            t0 = c * TOKS + j * SB
            if j >= 2:
                P.wait("sync", "a1_ve", done_ve[j - 2])
                P.wait("gpsimd", "a1_pe", done_pe[j - 2])
            src = s.x_in[:, t0:t0 + SB].rearrange("(n p) b -> p n b", p=128)
            P.dma("sync", s.xblk[buf].rearrange("p (n b) -> p n b", b=SB), src,
                  inc=f"a1_ldx{buf}")
            nx = P.cnt[f"a1_ldx{buf}"]
            P.dma("gpsimd", s.xbb[buf].rearrange("p (n b) -> p n b", b=SB), src,
                  inc=f"a1_ldxb{buf}")
            nxb = P.cnt[f"a1_ldxb{buf}"]
            P.wait("scalar", f"a1_ldxb{buf}", nxb)
            if j >= 2:
                P.wait("scalar", "a1_pe", done_pe[j - 2])
            P.op("scalar", lambda e, buf=buf: e.activation(
                s.sqb[buf], s.xbb[buf], AF.Square), inc="a1_se")
            nse = P.cnt["a1_se"]
            P.wait("tensor", f"a1_ldxb{buf}", nxb)
            P.wait("tensor", "a1_se", nse)
            if j >= 1:
                P.wait("tensor", "a1_rows", rows_done[j - 1])
            for n in range(NC):
                P.op("tensor", lambda e, n=n, buf=buf: e.matmul(
                    s.pb[4][0:1, 0:SB], s.onesb,
                    s.xbb[buf][:, n * SB:(n + 1) * SB],
                    start=(n == 0), stop=(n == NC - 1)))
            for n in range(NC):
                P.op("tensor", lambda e, n=n, buf=buf: e.matmul(
                    s.pb[4][0:1, SB:2 * SB], s.onesb,
                    s.sqb[buf][:, n * SB:(n + 1) * SB],
                    start=(n == 0), stop=(n == NC - 1)),
                    inc=("a1_pe" if n == NC - 1 else None))
            done_pe[j] = P.cnt["a1_pe"]
            P.wait("vector", "a1_pe", done_pe[j])
            P.op("vector", lambda e: e.tensor_scalar_mul(
                s.rowsA[:, 0:SB], s.pb[4][0:1, 0:SB], 1.0 / D))
            P.op("vector", lambda e: e.tensor_tensor(
                s.rowsE[:, 0:SB], s.rowsA[:, 0:SB], s.rowsA[:, 0:SB], ALU.mult))
            P.op("vector", lambda e: e.scalar_tensor_tensor(
                s.rowsA[:, SB:2 * SB], s.pb[4][0:1, SB:2 * SB], 1.0 / D,
                s.rowsE[:, 0:SB], ALU.mult, ALU.subtract), inc="a1_ve")
            P.wait("scalar", "a1_ve", P.cnt["a1_ve"])
            P.op("scalar", lambda e: e.activation(
                s.rowsA[:, SB:2 * SB], s.rowsA[:, SB:2 * SB], AF.Ln, bias=1e-5))
            P.op("scalar", lambda e: e.activation(
                s.rowsA[:, SB:2 * SB], s.rowsA[:, SB:2 * SB], AF.Exp, scale=-0.5),
                inc="a1_se")
            P.wait("vector", "a1_se", P.cnt["a1_se"])
            P.op("vector", lambda e: e.scalar_tensor_tensor(
                s.rowsA[:, 0:SB], s.rowsA[:, 0:SB], -1.0,
                s.rowsA[:, SB:2 * SB], ALU.mult, ALU.mult), inc="a1_rows")
            rows_done[j] = P.cnt["a1_rows"]
            P.wait("tensor", "a1_rows", rows_done[j])
            if j >= 1:
                P.wait("tensor", "a1_ve", done_ve[j - 1])
            P.op("tensor", lambda e: e.matmul(
                s.pb[5][:, 0:2 * SB], s.onesf, s.rowsA[:, 0:2 * SB],
                start=True, stop=True), inc="a1_bc")
            P.wait("vector", "a1_bc", P.cnt["a1_bc"])
            P.wait("vector", f"a1_ldx{buf}", nx)
            for n in range(NC):
                dst = s.ln1[:, n * TOKS + j * SB: n * TOKS + (j + 1) * SB]
                P.op("vector", lambda e, n=n, buf=buf, dst=dst: e.tensor_tensor(
                    dst, s.xblk[buf][:, n * SB:(n + 1) * SB],
                    s.pb[5][:, SB:2 * SB], ALU.mult))
                P.op("vector", lambda e, dst=dst: e.tensor_tensor(
                    dst, dst, s.pb[5][:, 0:SB], ALU.add),
                    inc=("a1_ve" if n == NC - 1 else None))
            done_ve[j] = P.cnt["a1_ve"]
        self.ln1_done = P.cnt["a1_ve"]

    def emit_vproj(self, c):
        P, s = self.P, self
        for sem in ("a1_pe", "a1_ve", "b1_pe", "b1_ve", "b1_se", "b2_pe",
                    "e2_pe"):
            P.wait("gpsimd", sem)
        evac_pe = {}
        for vf in range(4):
            buf = vf % 2
            if vf >= 2:
                P.wait("gpsimd", "b0_pe", evac_pe[vf - 2])
            P.dma("gpsimd", s.wvsl[buf].rearrange("p (k cc) -> p k cc", cc=512),
                  s.wv[:, :, vf * 512:(vf + 1) * 512].rearrange("k p cc -> p k cc"),
                  inc=f"b0_wld{buf}")
            nw = P.cnt[f"b0_wld{buf}"]
            P.wait("tensor", f"b0_wld{buf}", nw)
            if vf == 0:
                P.wait("tensor", "a1_ve", self.ln1_done)
                P.wait("tensor", "b1_pe")
                P.wait("tensor", "b2_pe")
                P.wait("tensor", "b2_se")
                P.wait("tensor", "e1_pe")
                P.wait("tensor", "e2_pe")
                P.wait("tensor", "e2_ve")
            for tc in range(NC):
                pbank = s.pb[tc % 2]
                if tc >= 2:
                    P.wait("tensor", "b0_se", P.cnt["b0_se"] - 1)
                for k in range(NC):
                    P.op("tensor", lambda e, k=k, tc=tc, buf=buf, pbank=pbank: e.matmul(
                        pbank[:, 0:512],
                        s.ln1[:, k * TOKS + tc * 128:k * TOKS + (tc + 1) * 128],
                        s.wvsl[buf][:, k * 512:(k + 1) * 512],
                        start=(k == 0), stop=(k == NC - 1)),
                        inc=("b0_pe" if k == NC - 1 else None))
                P.wait("scalar", "b0_pe", P.cnt["b0_pe"])
                P.op("scalar", lambda e, tc=tc, vf=vf, pbank=pbank: e.activation(
                    s.vtm[:, tc * D + vf * 512: tc * D + (vf + 1) * 512],
                    pbank[:, 0:512], AF.Silu), inc="b0_se")
            evac_pe[vf] = P.cnt["b0_pe"]
        self.v_done = P.cnt["b0_se"]

    def emit_tables(self, c, h):
        P, s = self.P, self
        slot = h % 2
        sc = s.nsl[:, h:h + 1]
        if h < 2:
            for sem in ("b2_sp0", "b2_sp1", "b2_se", "b1_ve", "b1_se"):
                P.wait("scalar", sem)
        else:
            P.wait("scalar", "b1_ve", self._scan_ve.get(h - 2, 0))
        P.op("scalar", lambda e: e.activation(
            s.qdec_r[slot], s.arow_s, AF.Exp, scale=sc))
        P.op("scalar", lambda e: e.activation(
            s.diag_r[slot], s.idxT_s, AF.Exp, scale=sc))
        self._tbl_se[h] = P.op("scalar", lambda e: e.activation(
            s.kdcolT[:, slot * 2:slot * 2 + 2], s.krowt_s, AF.Exp, scale=sc),
            inc="b1_se")

    def emit_head_pipe(self, c, h):
        P, s = self.P, self
        hb = h % 2
        if h == 0:
            self._tbl_se = {}
            self._scan_ve = {}
            self._scan_pe = {}
            self._proj_pe = {}
            self._qevac = {}
            self._kevac = {}
            self._qdech_ve = {}
            self._qspill = {}
            self._atspill = {}
            self._bank_se = {}
        self.emit_tables(c, h)
        wqh, wkh = s.wqh_r[hb], s.wkh_r[hb]
        if h == 0:
            for sem in ("b0_pe", "b0_se", "a1_se", "e1_se", "e1_ve", "e2_pe",
                        "b1_pe"):
                P.wait("gpsimd", sem)
        else:
            P.wait("gpsimd", "b1_pe", self._proj_pe.get(h - 2, 0))
        P.dma("gpsimd", wqh, s.wq[h], inc=f"b1_wld{hb}")
        P.dma("gpsimd", wkh, s.wk[h], inc=f"b1_wld{hb}")
        nw = P.cnt[f"b1_wld{hb}"]

        first = True
        for i in range(8):
            proj, qt = divmod(i, 4)
            wt = wqh if proj == 0 else wkh
            bank_i = 2 * proj + qt % 2
            bank = s.pb[bank_i]
            dstr = s.qraw_r[hb] if proj == 0 else s.kfm_r[hb]
            if first:
                P.wait("tensor", f"b1_wld{hb}", nw)
                if h == 0:
                    P.wait("tensor", "b0_se", self.v_done)
                first = False
            P.wait("tensor", "b1_se", self._bank_se.get(bank_i, 0))
            for k in range(NC):
                P.op("tensor", lambda e, k=k, qt=qt, wt=wt, bank=bank: e.matmul(
                    bank[:, 0:512], wt[:, k * 128:(k + 1) * 128],
                    s.ln1[:, k * TOKS + qt * 512:k * TOKS + (qt + 1) * 512],
                    start=(k == 0), stop=(k == NC - 1)),
                    inc=("b1_pe" if k == NC - 1 else None))
            npe = P.cnt["b1_pe"]
            P.wait("scalar", "b1_pe", npe)
            nev = P.op("scalar", lambda e, qt=qt, bank=bank, dstr=dstr:
                       e.activation(dstr[:, qt * 512:(qt + 1) * 512],
                                    bank[:, 0:512], AF.Silu), inc="b1_se")
            self._bank_se[bank_i] = nev
            if i == 3:
                self._qevac[h] = nev
            if i == 7:
                self._kevac[h] = nev
            if h >= 1:
                self.emit_scan_block(c, h - 1, i)
        self._proj_pe[h] = P.cnt["b1_pe"]
        P.wait("vector", "b1_se", self._qevac[h])
        P.wait("vector", "b1_se", self._tbl_se[h])
        P.wait("vector", "b1_sp", self._qspill.get(h - 2, 0))
        rep = lambda ap: ap.unsqueeze(1).broadcast_to([128, NBLK, BLK])
        self._qdech_ve[h] = P.op("vector", lambda e, hb=hb, slot=h % 2:
                                 e.tensor_tensor(
            s.qdech_r[hb].rearrange("p (r b) -> p r b", b=BLK),
            s.qraw_r[hb].rearrange("p (r b) -> p r b", b=BLK),
            rep(s.qdec_r[slot]), ALU.mult), inc="b1_ve")
        if self.use_coll:
            P.wait("sync", "b1_ve", self._qdech_ve[h])
            P.dma("sync",
                  s.qdec_sp[h * 128:(h + 1) * 128, c * TOKS:(c + 1) * TOKS],
                  s.qdech_r[hb], inc="b1_sp")
            self._qspill[h] = P.cnt["b1_sp"]
        if h == H - 1:
            for j in range(8):
                self.emit_scan_block(c, h, j)

    def emit_scan_block(self, c, h, j):
        P, s = self.P, self
        hb = h % 2
        slot = h % 2
        t0 = j * BLK
        hs = h * 128
        P.wait("scalar", "b1_ve", getattr(self, "_last_kvstt", 0))
        P.wait("scalar", "b1_pe", getattr(self, "_last_nat", 0))
        nbf = P.op("scalar", lambda e, hs=hs: e.copy(
            s.kvbf, s.kvst[:, hs:hs + 128]), inc="b1_se")
        P.wait("tensor", "b1_se", self._kevac[h])
        P.wait("tensor", "b1_ve", getattr(self, "_kkdT_n", 0))
        P.op("tensor", lambda e, t0=t0, hb=hb: e.transpose(
            self.pbt[:][:, 0:128], s.kfm_r[hb][:, t0:t0 + 128], s.identb))
        ntr = P.op("tensor", lambda e, t0=t0, hb=hb: e.transpose(
            self.pbt[:][:, 128:256], s.kfm_r[hb][:, t0 + 128:t0 + 256],
            s.identb), inc="b1_pe")
        P.wait("tensor", "b1_ve", getattr(self, "_qkT_n", 0))
        P.op("tensor", lambda e, t0=t0, hb=hb: e.matmul(
            s.pb[4][:, 0:256], s.kfm_r[hb][:, t0:t0 + 128],
            s.qraw_r[hb][:, t0:t0 + 256], start=True, stop=True))
        nqk = P.op("tensor", lambda e, t0=t0, hb=hb: e.matmul(
            s.pb[4][:, 256:512], s.kfm_r[hb][:, t0 + 128:t0 + 256],
            s.qraw_r[hb][:, t0:t0 + 256], start=True, stop=True), inc="b1_pe")
        P.wait("vector", "b1_pe", ntr)
        if j == 0:
            P.wait("vector", "b1_se", self._tbl_se[h])
        for i2 in range(2):
            P.op("vector", lambda e, i2=i2, slot=slot: e.tensor_scalar_mul(
                s.kkdT[:, i2 * 128:(i2 + 1) * 128],
                self.pbt[:][:, i2 * 128:(i2 + 1) * 128],
                s.kdcolT[:, slot * 2 + i2:slot * 2 + i2 + 1]),
                inc=("b1_ve" if i2 == 1 else None))
        self._kkdT_n = P.cnt["b1_ve"]
        P.wait("vector", "b1_pe", nqk)
        P.op("vector", lambda e, slot=slot: e.tensor_tensor(
            s.qkT[:, 0:256], s.pb[4][:, 0:256],
            s.diag_r[slot][:, 0:256], ALU.mult))
        self._qkT_n = P.op("vector", lambda e, slot=slot: e.tensor_tensor(
            s.qkT[:, 256:512], s.pb[4][:, 256:512],
            s.diag_r[slot][:, 256:512], ALU.mult), inc="b1_ve")
        P.wait("tensor", "b1_ve", self._qkT_n)
        P.wait("tensor", "b1_se", nbf)
        P.wait("tensor", "b1_se", getattr(self, "_last_atevac", 0))
        vt0 = (2 * j) * D + hs
        vt1 = (2 * j + 1) * D + hs
        P.op("tensor", lambda e, vt0=vt0: e.matmul(
            s.pb[5][:, 0:256], s.vtm[:, vt0:vt0 + 128], s.qkT[:, 0:256],
            start=True, stop=False))
        P.op("tensor", lambda e, vt1=vt1: e.matmul(
            s.pb[5][:, 0:256], s.vtm[:, vt1:vt1 + 128], s.qkT[:, 256:512],
            start=False, stop=False))
        nat = P.op("tensor", lambda e, t0=t0, hb=hb: e.matmul(
            s.pb[5][:, 0:256], s.kvbf, s.qdech_r[hb][:, t0:t0 + 256],
            start=False, stop=True), inc="b1_pe")
        self._last_nat = nat
        P.wait("tensor", "b1_ve", self._kkdT_n)
        P.wait("tensor", "b1_ve", getattr(self, "_last_kvstt", 0))
        P.op("tensor", lambda e, vt0=vt0: e.matmul(
            s.pkv[:], s.kkdT[:, 0:128], s.vtm[:, vt0:vt0 + 128],
            start=True, stop=False))
        nkv = P.op("tensor", lambda e, vt1=vt1: e.matmul(
            s.pkv[:], s.kkdT[:, 128:256], s.vtm[:, vt1:vt1 + 128],
            start=False, stop=True), inc="b1_pe")
        P.wait("scalar", "b1_pe", nat)
        if j == 0:
            P.wait("scalar", "b1_sp", self._atspill.get(h - 1, 0))
        self._last_atevac = P.op("scalar", lambda e, t0=t0: e.copy(
            s.attnh[:, t0:t0 + 256], s.pb[5][:, 0:256]), inc="b1_se")
        P.wait("vector", "b1_pe", nkv)
        self._last_kvstt = P.op("vector", lambda e, hs=hs, h=h:
                                e.scalar_tensor_tensor(
            s.kvst[:, hs:hs + 128], s.kvst[:, hs:hs + 128],
            s.bdcol[:, h:h + 1], s.pkv[:], ALU.mult, ALU.add), inc="b1_ve")
        if j == NBLK - 1:
            self._scan_pe[h] = P.cnt["b1_pe"]
            self._scan_ve[h] = P.cnt["b1_ve"]
            P.wait("sync", "b1_se", self._last_atevac)
            P.dma("sync",
                  s.attn_sp[h * 128:(h + 1) * 128, c * TOKS:(c + 1) * TOKS],
                  s.attnh, inc="b1_sp")
            self._atspill[h] = P.cnt["b1_sp"]

    def emit_gate(self, c):
        P, s = self.P, self
        P.wait("gpsimd", "b1_pe")
        P.wait("gpsimd", "b1_ve")
        for k in range(NC):
            P.dma("gpsimd", s.wres[:, k * D:(k + 1) * D], s.wg[k], inc="b2_wld")
        nw = P.cnt["b2_wld"]
        P.wait("tensor", "b2_wld", nw)
        P.wait("tensor", "b1_se")
        P.wait("tensor", "b1_ve")
        P.wait("scalar", "b1_ve")
        P.wait("scalar", "b1_pe")
        evacs = []
        for th in range(2):
            for f in range(NC):
                fi = th * NC + f
                b0, b1 = (s.pb[0], s.pb[1]) if f % 2 == 0 else (s.pb[2], s.pb[3])
                if fi >= 2:
                    P.wait("tensor", "b2_se", evacs[fi - 2])
                for k in range(NC):
                    P.op("tensor", lambda e, k=k, f=f, th=th, b0=b0: e.matmul(
                        b0[:, 0:512], s.wres[:, k * D + f * 128:k * D + (f + 1) * 128],
                        s.ln1[:, k * TOKS + th * 1024:k * TOKS + th * 1024 + 512],
                        start=(k == 0), stop=(k == NC - 1)))
                    P.op("tensor", lambda e, k=k, f=f, th=th, b1=b1: e.matmul(
                        b1[:, 0:512], s.wres[:, k * D + f * 128:k * D + (f + 1) * 128],
                        s.ln1[:, k * TOKS + th * 1024 + 512:k * TOKS + th * 1024 + 1024],
                        start=(k == 0), stop=(k == NC - 1)),
                        inc=("b2_pe" if k == NC - 1 else None))
                npe = P.cnt["b2_pe"]
                gb = s.gatesb[:, (f % 2) * 1024:(f % 2) * 1024 + 1024]
                P.wait("scalar", "b2_pe", npe)
                P.wait("scalar", f"b2_sp{f % 2}",
                       P.cnt.get(f"b2_sp{f % 2}", 0))
                P.op("scalar", lambda e, gb=gb, b0=b0: e.activation(
                    gb[:, 0:512], b0[:, 0:512], AF.Sigmoid))
                nse = P.op("scalar", lambda e, gb=gb, b1=b1: e.activation(
                    gb[:, 512:1024], b1[:, 0:512], AF.Sigmoid), inc="b2_se")
                evacs.append(nse)
                P.wait("vector", "b2_se", nse)
                nve = P.op("vector", lambda e, gb=gb, f=f: e.tensor_scalar_mul(
                    gb, gb, s.cvec[:, f:f + 1]), inc="b2_ve")
                P.wait("sync", "b2_ve", nve)
                P.wait("sync", "b2_se", nse)
                P.dma("sync",
                      s.gate_sp[f * 128:(f + 1) * 128,
                                c * TOKS + th * 1024:c * TOKS + th * 1024 + 1024],
                      gb, inc=f"b2_sp{f % 2}")

    def emit_collective(self):
        P, s = self.P, self
        P.wait("gpsimd", "b1_ve")
        P.dma("gpsimd", s.kvf_d.rearrange("(h p) d -> p h d", p=128),
              s.kvst.rearrange("p (h d) -> p h d", d=128), inc="kv_sp")
        P.wait("gpsimd", "kv_sp")
        P.op("gpsimd", lambda e: e.collective_compute(
            "AllGather", ALU.bypass, replica_groups=REPLICA_PAIRS,
            ins=[s.kvf_d], outs=[s.kvg_d]), inc="cc_s")
        P.wait("gpsimd", "cc_s")
        P.dma("gpsimd", s.kvprevb.rearrange("p (h d) -> p h d", d=128),
              s.kvg_d[0:H * 128, :].rearrange("(h p) d -> p h d", p=128),
              inc="kv_ld")
        P.wait("vector", "kv_ld")
        P.op("vector", lambda e: e.tensor_scalar_mul(s.kvprevb, s.kvprevb, s.maskc),
             inc="kv_ve")
        if self.dbg:
            P.wait("sync", "kv_ve")
            P.dma("sync", s.kvp_dp, s.kvprevb, inc="dbg_st")
            P.wait("sync", "c_ve")
            P.wait("sync", "c_se")
            P.dma("sync", s.bd_dp[:, 0:16], s.bdcol, inc="dbg_st")
            P.dma("sync", s.bd_dp[:, 16:144], s.bdpow, inc="dbg_st")

    def emit_wout_load(self, c):
        if c > 0:
            return
        P, s = self.P, self
        P.wait("gpsimd", "b2_pe")
        for k in range(NC):
            P.dma("gpsimd", s.wres[:, k * D:(k + 1) * D], s.wo[k], inc="wo_ld")
        self._wo_n = P.cnt["wo_ld"]

    def emit_e1(self, c, t):
        P, s = self.P, self
        buf = t % 2
        off = c * TOKS + t * BLK
        use_corr = self.use_coll

        if t == 0:
            for sem in ("b1_pe", "b1_ve", "b1_se", "b2_pe", "b2_se", "a1_ve",
                        "a1_pe", "b2_sp0", "b2_sp1", "b1_sp", "e1_ve", "e1_pe",
                        "e1_se", "e1_sp0", "e1_sp1"):
                P.wait("sync", sem)
                P.wait("tensor", sem)
                P.wait("vector", sem)
                P.wait("scalar", sem)
            self._e1_free = {}
        else:
            if t >= 2:
                P.wait("sync", "e1_ve", self._e1_free[t - 2])
                P.wait("sync", f"e1_sp{buf}", self._e1_sp.get(t - 2, 0))
        fm = lambda apdram: apdram[:, off:off + BLK].rearrange(
            "(n p) b -> p n b", p=128)
        P.dma("sync", s.xt[buf].rearrange("p (n b) -> p n b", b=BLK),
              fm(s.x_in), inc=f"e1_ldx{buf}")
        nldx = P.cnt[f"e1_ldx{buf}"]
        P.dma("sync", s.attnt[buf].rearrange("p (n b) -> p n b", b=BLK),
              fm(s.attn_sp), inc=f"e1_lda{buf}")
        nlda = P.cnt[f"e1_lda{buf}"]
        if t >= 2:
            P.wait("sync", "e1_se", self._sq2_se.get(t - 2, 0))
        P.dma("sync", s.gatet[buf].rearrange("p (n b) -> p n b", b=BLK),
              fm(s.gate_sp), inc=f"e1_ldg{buf}")
        nldg = P.cnt[f"e1_ldg{buf}"]
        if use_corr:
            if t >= 1:
                P.wait("sync", "e1_ve", self._corr_done.get(t - 1, 0))
            else:
                self._corr_done = {}
            P.dma("sync", s.qdect.rearrange("p (n b) -> p n b", b=BLK),
                  fm(s.qdec_sp), inc="e1_ldq")
            nldq = P.cnt["e1_ldq"]

        if use_corr:
            P.wait("vector", f"e1_lda{buf}", nlda)
            if t == 0:
                self._corr_tt = {}
                self._corr_mm = {}
            P.wait("tensor", "e1_ldq", nldq)
            if t == 0 and self.use_coll:
                P.wait("tensor", "kv_ve")
            if t >= 1:
                P.wait("tensor", "e1_ve", self._kvscale_n)
            if t >= 1:
                P.wait("tensor", "e1_ve", self._z_done.get(t - 1, 0))
            for h in range(H):
                pc = s.pb[h % 2]
                if h >= 2:
                    P.wait("tensor", "e1_ve", self._corr_tt[h - 2])
                self._corr_mm[h] = P.op("tensor", lambda e, h=h, pc=pc: e.matmul(
                    pc[:, 0:256], s.kvprevb[:, h * 128:(h + 1) * 128],
                    s.qdect[:, h * BLK:(h + 1) * BLK],
                    start=True, stop=True), inc="e1_pe")
                P.wait("vector", "e1_pe", self._corr_mm[h])
                self._corr_tt[h] = P.op("vector", lambda e, h=h, pc=pc: e.tensor_tensor(
                    s.attnt[buf][:, h * BLK:(h + 1) * BLK],
                    s.attnt[buf][:, h * BLK:(h + 1) * BLK],
                    pc[:, 0:256], ALU.add), inc="e1_ve")
            self._corr_done[t] = P.cnt["e1_ve"]
            if t < NBLK - 1:
                P.wait("vector", "e1_pe", self._corr_mm[H - 1])
                for h in range(H):
                    P.op("vector", lambda e, h=h: e.tensor_scalar_mul(
                        s.kvprevb[:, h * 128:(h + 1) * 128],
                        s.kvprevb[:, h * 128:(h + 1) * 128],
                        s.bdcol[:, h:h + 1]),
                        inc=("e1_ve" if h == H - 1 else None))
                self._kvscale_n = P.cnt["e1_ve"]
        ncorr = P.cnt.get("e1_ve", 0)

        P.wait("scalar", f"e1_lda{buf}", nlda)
        if use_corr:
            P.wait("scalar", "e1_ve", ncorr)
        if t >= 1:
            P.wait("scalar", "e1_pe", self._sq_cons.get(t - 1, 0))
        else:
            self._sq_cons = {}
        nsq = P.op("scalar", lambda e, buf=buf: e.activation(
            s.sqt, s.attnt[buf], AF.Square), inc="e1_se")
        P.wait("tensor", "e1_se", nsq)
        if t >= 1:
            P.wait("tensor", "e1_ve", self._e1_rows.get(t - 1, 0))
        else:
            self._e1_rows = {}
        for n in range(NC):
            P.op("tensor", lambda e, n=n: e.matmul(
                s.pb[4][0:1, 0:BLK], s.onesb, s.sqt[:, n * BLK:(n + 1) * BLK],
                start=(n == 0), stop=(n == NC - 1)),
                inc=("e1_pe" if n == NC - 1 else None))
        nssq = P.cnt["e1_pe"]
        P.wait("scalar", "e1_pe", nssq)
        P.op("scalar", lambda e: e.activation(
            s.rowsE[:, 0:BLK], s.pb[4][0:1, 0:BLK], AF.Ln, scale=1.0 / D, bias=1e-6))
        nrow1 = P.op("scalar", lambda e: e.activation(
            s.rowsE[:, 0:BLK], s.rowsE[:, 0:BLK], AF.Exp, scale=-0.5), inc="e1_se")
        P.wait("tensor", "e1_se", nrow1)
        if t >= 1:
            P.wait("tensor", "e1_ve", self._z_done.get(t - 1, 0))
        else:
            self._z_done = {}
        if use_corr:
            P.wait("tensor", "e1_ve", self._corr_done[t])
        nbc1 = P.op("tensor", lambda e: e.matmul(
            s.pb[1][:, 0:BLK], s.onesf, s.rowsE[:, 0:BLK], start=True, stop=True),
            inc="e1_pe")
        if self.dbg:
            P.wait("sync", "e1_ve", ncorr)
            P.dma("sync", self.atc_dp[:, t * BLK:(t + 1) * BLK]
                  .rearrange("(n p) b -> p n b", p=128),
                  s.attnt[buf].rearrange("p (n b) -> p n b", b=BLK), inc="dbg_at")
            P.wait("vector", "dbg_at")
        P.wait("vector", "e1_pe", nbc1)
        P.wait("vector", "e1_se", nsq)
        P.wait("vector", f"e1_ldg{buf}", nldg)
        rrep = s.pb[1][:, 0:BLK].unsqueeze(1).broadcast_to([128, NC, BLK])
        P.op("vector", lambda e, buf=buf, rrep=rrep: e.tensor_tensor(
            s.attnt[buf].rearrange("p (n b) -> p n b", b=BLK),
            s.attnt[buf].rearrange("p (n b) -> p n b", b=BLK), rrep, ALU.mult))
        P.op("vector", lambda e, buf=buf: e.tensor_tensor(
            s.attnt[buf], s.attnt[buf], s.gatet[buf], ALU.mult), inc="e1_ve")
        self._z_done[t] = P.cnt["e1_ve"]
        nz = self._z_done[t]

        P.wait("tensor", "e1_ve", nz)
        P.wait("tensor", "wo_ld", self._wo_n)
        nf = {}
        for f in range(NC):
            pob = s.pb[2 + (f % 2)]
            if f >= 2:
                P.wait("tensor", "e1_ve", nf[f - 2][1])
            for k in range(NC):
                P.op("tensor", lambda e, k=k, f=f, buf=buf, pob=pob: e.matmul(
                    pob[:, 0:256], s.wres[:, k * D + f * 128:k * D + (f + 1) * 128],
                    s.attnt[buf][:, k * BLK:(k + 1) * BLK],
                    start=(k == 0), stop=(k == NC - 1)),
                    inc=("e1_pe" if k == NC - 1 else None))
            npe = P.cnt["e1_pe"]
            P.wait("vector", "e1_pe", npe)
            if f == 0:
                P.wait("vector", f"e1_ldx{buf}", nldx)
                if t >= 2:
                    P.wait("vector", f"e1_sp{buf}", self._e1_sp.get(t - 2, 0))
            nev = P.op("vector", lambda e, f=f, buf=buf, pob=pob: e.tensor_tensor(
                s.x2t[buf][:, f * BLK:(f + 1) * BLK],
                s.xt[buf][:, f * BLK:(f + 1) * BLK], pob[:, 0:256], ALU.add),
                inc="e1_ve")
            nf[f] = (npe, nev)
        nx2 = nf[NC - 1][1]

        P.wait("vector", "e1_ve", nx2)
        nx2b = P.op("vector", lambda e, buf=buf: e.tensor_copy(
            s.gatet[buf], s.x2t[buf]), inc="e1_ve")
        P.wait("scalar", "e1_ve", nx2b)
        nsq2 = P.op("scalar", lambda e, buf=buf: e.activation(
            s.sqt, s.gatet[buf], AF.Square), inc="e1_se")
        self._sq2_se = getattr(self, "_sq2_se", {})
        self._sq2_se[t] = nsq2
        P.wait("tensor", "e1_ve", nx2b)
        P.wait("tensor", "e1_se", nsq2)
        for n in range(NC):
            P.op("tensor", lambda e, n=n, buf=buf: e.matmul(
                s.pb[4][0:1, 0:BLK], s.onesb, s.gatet[buf][:, n * BLK:(n + 1) * BLK],
                start=(n == 0), stop=(n == NC - 1)))
        for n in range(NC):
            P.op("tensor", lambda e, n=n: e.matmul(
                s.pb[4][0:1, BLK:2 * BLK], s.onesb, s.sqt[:, n * BLK:(n + 1) * BLK],
                start=(n == 0), stop=(n == NC - 1)),
                inc=("e1_pe" if n == NC - 1 else None))
        nS = P.cnt["e1_pe"]
        self._sq_cons[t] = nS
        P.wait("vector", "e1_pe", nS)
        P.op("vector", lambda e: e.tensor_scalar_mul(
            s.rowsE[:, 256:512], s.pb[4][0:1, 0:BLK], 1.0 / D))
        P.op("vector", lambda e: e.tensor_tensor(
            s.rowsE[:, 0:256], s.rowsE[:, 256:512], s.rowsE[:, 256:512], ALU.mult))
        P.op("vector", lambda e: e.scalar_tensor_tensor(
            s.rowsE[:, 0:256], s.pb[4][0:1, BLK:2 * BLK], 1.0 / D,
            s.rowsE[:, 0:256], ALU.mult, ALU.subtract), inc="e1_ve")
        nvar = P.cnt["e1_ve"]
        self._e1_rows = getattr(self, "_e1_rows", {})
        P.wait("scalar", "e1_ve", nvar)
        P.op("scalar", lambda e: e.activation(
            s.rowsE[:, 512:768], s.rowsE[:, 0:256], AF.Ln,
            scale=1.0 + 1e-5, bias=1e-10))
        P.op("scalar", lambda e: e.activation(
            s.rowsE[:, 512:768], s.rowsE[:, 512:768], AF.Exp, scale=-0.5),
            inc="e1_se")
        nr = P.cnt["e1_se"]
        self._e1_rows[t] = nvar
        P.wait("tensor", "e1_se", nr)
        P.wait("tensor", "e1_ve", nvar)
        P.wait("tensor", "e1_ve", self._e1_free.get(t - 1, 0))
        nbc2 = P.op("tensor", lambda e: e.matmul(
            s.pb[5][:, 0:512], s.onesf, s.rowsE[:, 256:768],
            start=True, stop=True), inc="e1_pe")
        P.wait("vector", "e1_pe", nbc2)
        mrep = s.pb[5][:, 0:256].unsqueeze(1).broadcast_to([128, NC, BLK])
        rrep2 = s.pb[5][:, 256:512].unsqueeze(1).broadcast_to([128, NC, BLK])
        P.op("vector", lambda e, buf=buf, mrep=mrep: e.tensor_tensor(
            s.xt[buf].rearrange("p (n b) -> p n b", b=BLK),
            s.x2t[buf].rearrange("p (n b) -> p n b", b=BLK), mrep, ALU.subtract))
        P.op("vector", lambda e, buf=buf, rrep2=rrep2: e.tensor_tensor(
            s.xt[buf].rearrange("p (n b) -> p n b", b=BLK),
            s.xt[buf].rearrange("p (n b) -> p n b", b=BLK), rrep2, ALU.mult))
        for n in range(NC):
            P.op("vector", lambda e, n=n, buf=buf: e.tensor_scalar(
                s.attnt[buf][:, n * BLK:(n + 1) * BLK],
                s.xt[buf][:, n * BLK:(n + 1) * BLK],
                s.cvec[:, 3 * NC + n:3 * NC + n + 1],
                s.cvec[:, 4 * NC + n:4 * NC + n + 1],
                ALU.mult, ALU.add),
                inc=("e1_ve" if n == NC - 1 else None))
        nxn = P.cnt["e1_ve"]
        self._e1_free[t] = nxn

        P.wait("gpsimd", "e1_ve", nxn)
        P.dma("gpsimd", s.xn_sp[:, off:off + BLK].rearrange("(n p) b -> p n b", p=128),
              s.attnt[buf].rearrange("p (n b) -> p n b", b=BLK), inc=f"e1_sp{buf}")
        P.dma("gpsimd", s.x2_sp[:, off:off + BLK].rearrange("(n p) b -> p n b", p=128),
              s.x2t[buf].rearrange("p (n b) -> p n b", b=BLK), inc=f"e1_sp{buf}")
        if not hasattr(self, "_e1_sp"):
            self._e1_sp = {}
        self._e1_sp[t] = P.cnt[f"e1_sp{buf}"]

    def emit_signw(self, c):
        P, s = self.P, self
        if c > 0:
            return
        P.wait("sync", "e1_pe")
        P.wait("sync", "e1_ve")
        P.wait("sync", "e1_se")
        P.wait("gpsimd", "e1_pe")
        nred = 0
        for k in range(NC):
            buf = k % 2
            if k >= 2:
                P.wait("sync", "sw_se", P.cnt["sw_se"] - 1)
                P.wait("sync", "sw_ve", P.cnt["sw_ve"] - 1)
            P.dma("sync", s.wfst[buf], s.wf[k], inc=f"sw_ld{buf}")
            nld = P.cnt[f"sw_ld{buf}"]
            P.wait("scalar", f"sw_ld{buf}", nld)
            P.op("scalar", lambda e, k=k, buf=buf: e.activation(
                s.wres[:, k * D:(k + 1) * D], s.wfst[buf], AF.Sign), inc="sw_se")
            P.wait("vector", f"sw_ld{buf}", nld)
            P.op("vector", lambda e, k=k, buf=buf: e.tensor_reduce(
                s.sclc[:, k:k + 1], s.wfst[buf], AX.X, ALU.max,
                apply_absolute_value=True), inc="sw_ve")
            nred = P.cnt["sw_ve"]
        P.wait("scalar", "sw_ve", nred)
        nhop = P.op("scalar", lambda e: e.copy(s.srow[:, 1:2], s.srow[:, 0:1]),
                    inc="sw_se")
        P.wait("vector", "sw_se", nhop)
        ng = P.op("vector", lambda e: e.tensor_reduce(
            s.sclc[:, 16:17], s.sclc[:, 0:16], AX.X, ALU.max), inc="sw_ve")
        P.wait("tensor", "sw_ve", ng)
        P.wait("tensor", "e1_pe")
        ntr2 = P.op("tensor", lambda e: e.transpose(
            s.pb[1][0:1, 0:128], s.sclc[:, 16:17], s.identf), inc="sw_pe")
        P.wait("vector", "sw_pe", ntr2)
        P.op("vector", lambda e: e.tensor_reduce(
            s.srow[:, 0:1], s.pb[1][0:1, 0:128], AX.X, ALU.max), inc="sw_ve")
        nscl = P.cnt["sw_ve"]
        P.wait("tensor", "sw_ve", nscl)
        P.wait("tensor", "e1_pe")
        nbc = P.op("tensor", lambda e: e.matmul(
            s.pb[0][:, 0:1], s.onesf, s.srow[:, 0:1], start=True, stop=True),
            inc="sw_pe")
        P.wait("scalar", "sw_pe", nbc)
        nsc = P.op("scalar", lambda e: e.copy(s.sclc[:, 17:18], s.pb[0][:, 0:1]),
                   inc="sw_se")
        P.wait("vector", "sw_se", nsc)
        P.op("vector", lambda e: e.tensor_scalar_mul(
            s.gscale, s.cvec[:, NC:2 * NC], s.sclc[:, 17:18]))
        P.op("vector", lambda e: e.tensor_copy(s.bprime, s.cvec[:, 2 * NC:3 * NC]),
             inc="sw_ve")
        self._sw_done = P.cnt["sw_ve"]
        self._sw_se = P.cnt["sw_se"]

    def emit_e2(self, c, t2):
        P, s = self.P, self
        T2 = 512
        buf = t2 % 2
        off = c * TOKS + t2 * T2
        if t2 == 0 and c == 0:
            P.wait("sync", "e1_sp0")
            P.wait("sync", "e1_sp1")
            P.wait("sync", "sw_se", self._sw_se)
            P.wait("sync", "e1_ve")
            P.wait("sync", "e1_se")
            self._e2_hist = []
        if len(self._e2_hist) >= 2:
            pe_g, se_g, st_g = self._e2_hist[-2]
            P.wait("sync", "e2_pe", pe_g)
            P.wait("sync", "e2_se", se_g)
            P.wait("sync", f"out_st{buf}", st_g)
        P.dma("sync", s.xn2[buf].rearrange("p (n b) -> p n b", b=T2),
              s.xn_sp[:, off:off + T2].rearrange("(n p) b -> p n b", p=128),
              inc=f"e2_ldn{buf}")
        nldn = P.cnt[f"e2_ldn{buf}"]
        P.dma("sync", s.x22[buf].rearrange("p (n b) -> p n b", b=T2),
              s.x2_sp[:, off:off + T2].rearrange("(n p) b -> p n b", p=128),
              inc=f"e2_ldx{buf}")
        nldx = P.cnt[f"e2_ldx{buf}"]
        P.wait("tensor", f"e2_ldn{buf}", nldn)
        if t2 == 0 and c == 0:
            P.wait("tensor", "sw_se", self._sw_se)
            P.wait("tensor", "sw_ve", self._sw_done)
            P.wait("tensor", "e1_ve")
            P.wait("tensor", "e1_pe")
        nfs = {}
        for f in range(NC):
            pob = s.pb[2 * (f % 2)], s.pb[2 * (f % 2) + 1]
            if f >= 2:
                P.wait("tensor", "e2_se", nfs[f - 2])
            for k in range(NC):
                P.op("tensor", lambda e, k=k, f=f, buf=buf, pob=pob: e.matmul(
                    pob[0][:, 0:512], s.wres[:, k * D + f * 128:k * D + (f + 1) * 128],
                    s.xn2[buf][:, k * T2:k * T2 + 512],
                    start=(k == 0), stop=(k == NC - 1)),
                    inc=("e2_pe" if k == NC - 1 else None))
            npe = P.cnt["e2_pe"]
            P.wait("vector", "e2_pe", npe)
            if f == 0:
                P.wait("vector", f"e2_ldx{buf}", nldx)
            nev = P.op("vector", lambda e, f=f, buf=buf, pob=pob:
                       e.scalar_tensor_tensor(
                           s.x22[buf][:, f * T2:(f + 1) * T2], pob[0][:, 0:512],
                           s.gscale[:, f:f + 1], s.x22[buf][:, f * T2:(f + 1) * T2],
                           ALU.mult, ALU.add), inc="e2_ve")
            nse = P.op("vector", lambda e, f=f, buf=buf: e.tensor_scalar_add(
                s.x22[buf][:, f * T2:(f + 1) * T2],
                s.x22[buf][:, f * T2:(f + 1) * T2],
                s.bprime[:, f:f + 1]), inc="e2_se")
            nfs[f] = nse
        P.wait("gpsimd", "e2_se", P.cnt["e2_se"])
        P.dma("gpsimd", s.out_d[:, off:off + T2].rearrange("(n p) b -> p n b", p=128),
              s.x22[buf].rearrange("p (n b) -> p n b", b=T2), inc=f"out_st{buf}")
        self._e2_hist.append((P.cnt["e2_pe"], P.cnt["e2_se"],
                              P.cnt[f"out_st{buf}"]))

    def emit_sw_dumps(self):
        P, s = self.P, self
        P.wait("vector", "sw_ve", self._sw_done)
        P.wait("sync", "sw_ve", self._sw_done)
        P.dma("sync", s.sw_dp[:, 0:16], s.gscale, inc="dbg_st")
        P.dma("sync", s.sw_dp[:, 16:32], s.bprime, inc="dbg_st")
        P.dma("sync", s.sw_dp[:, 32:36], s.sclc, inc="dbg_st")
        P.wait("sync", "sw_se", self._sw_se)
        P.dma("sync", s.wres_dp, s.wres, inc="dbg_st")
        P.wait("sync", "dbg_st")

    def emit_dumps(self):
        P, s = self.P, self
        P.wait("sync", "b0_se", self.v_done)
        P.dma("sync", s.ln1_dp, s.ln1, inc="dbg_st")
        P.dma("sync", s.vtm_dp, s.vtm, inc="dbg_st")
        P.wait("sync", "dbg_st")


def _stage_weights(L):
    wqkvT = np.ascontiguousarray(L["L0_attn_w_qkv"].T)
    out = {}
    wqh = np.empty((H, 128, D), np.float32)
    wkh = np.empty((H, 128, D), np.float32)
    for h in range(H):
        wqh[h] = wqkvT[:, h * 128:(h + 1) * 128].reshape(NC, 128, 128) \
            .transpose(1, 0, 2).reshape(128, D)
        wkh[h] = wqkvT[:, D + h * 128:D + (h + 1) * 128].reshape(NC, 128, 128) \
            .transpose(1, 0, 2).reshape(128, D)
    out["wq"] = wqh
    out["wk"] = wkh
    out["wv"] = np.ascontiguousarray(wqkvT[:, 2 * D:3 * D].reshape(NC, 128, D))
    out["wg"] = np.ascontiguousarray(L["L0_attn_w_gate"].T.reshape(NC, 128, D))
    out["wo"] = np.ascontiguousarray(L["L0_attn_w_out"].T.reshape(NC, 128, D))
    out["wf"] = np.ascontiguousarray(L["L0_ffn_weight"].T.reshape(NC, 128, D))
    fm = lambda v: np.ascontiguousarray(np.asarray(v, np.float32).reshape(NC, 128).T)
    out["vecs"] = np.ascontiguousarray(np.concatenate(
        [fm(L["L0_attn_norm_w"]), fm(L["L0_ffn_gamma"]), fm(L["L0_ffn_beta"]),
         fm(L["L0_ffn_ln_weight"]), fm(L["L0_ffn_ln_bias"])], axis=1))
    out["slope"] = np.ascontiguousarray(
        np.asarray(L["L0_attn_slope_rate"], np.float32).reshape(1, H))
    ar = np.arange(1, BLK + 1, dtype=np.float32)
    out["arow"] = np.ascontiguousarray(np.broadcast_to(ar, (128, BLK)))
    out["krow"] = np.ascontiguousarray(np.broadcast_to(BLK - ar, (128, BLK)))
    p = np.arange(128, dtype=np.float32).reshape(128, 1)
    i2 = np.arange(2, dtype=np.float32).reshape(1, 2)
    out["krowt"] = np.ascontiguousarray(255.0 - i2 * 128.0 - p)
    kp = np.arange(2 * 128).reshape(2 * 128, 1).astype(np.float32)
    qp = np.arange(BLK).reshape(1, BLK).astype(np.float32)
    idxT = qp - kp
    idxT[idxT < 0] = BIG
    out["idxT"] = np.ascontiguousarray(idxT.astype(np.float32))
    out["identw"] = np.eye(128, dtype=np.float32)
    return out


_CACHE = {}


def _get_nc(chunks, use_coll, dbg=False):
    key = (chunks, use_coll, dbg)
    if key not in _CACHE:
        k = Ker(chunks, use_coll, dbg)
        _CACHE[key] = (k.build(), k.sbuf_left)
    return _CACHE[key][0]


def kernel(**inputs):
    x = np.asarray(inputs["L0_attn_hidden_states"], np.float32)
    B, N = x.shape[0], x.shape[1]
    staged = _stage_weights(inputs)

    use_v1 = True
    if use_v1:
        nc = _get_nc(1, True)
        in_maps = []
        for c in range(8):
            b, half = c // 2, c % 2
            m = dict(staged)
            m["x"] = np.ascontiguousarray(x[b, half * TOKS:(half + 1) * TOKS, :].T)
            m["mask01"] = np.full((128, 1), float(half), np.float32)
            in_maps.append(m)
        res = run_bass_kernel_spmd(nc, in_maps, core_ids=list(range(8)))
        out = np.empty((B, N, D), np.float32)
        for c in range(8):
            b, half = c // 2, c % 2
            out[b, half * TOKS:(half + 1) * TOKS, :] = res.results[c]["out"].T
        return out
    else:
        nc = _get_nc(2, False)
        in_maps = []
        for c in range(8):
            b = c % 4
            m = dict(staged)
            m["x"] = np.ascontiguousarray(x[b].T)
            m["mask01"] = np.zeros((128, 1), np.float32)
            in_maps.append(m)
        res = run_bass_kernel_spmd(nc, in_maps, core_ids=list(range(8)))
        out = np.empty((B, N, D), np.float32)
        for b in range(4):
            out[b] = res.results[b]["out"].T
        return out


# version 09 (final)
# speedup vs baseline: 1.4505x; 1.0281x over previous
import numpy as np
import ml_dtypes
from contextlib import ExitStack

import concourse.bass as bass
import concourse.mybir as mybir
from concourse.bass_utils import run_bass_kernel_spmd

F32 = mybir.dt.float32
BF16 = mybir.dt.bfloat16
U8 = mybir.dt.uint8
AF = mybir.ActivationFunctionType
ALU = mybir.AluOpType
AX = mybir.AxisListType
BFNP = ml_dtypes.bfloat16

D = 2048
H = 16
NC = 16
BLK = 256
TOKS = 2048
NBLK = TOKS // BLK
BIG = 1.0e9
GRP = 4
REPLICA_PAIRS = [[0, 1], [2, 3], [4, 5], [6, 7]]


class Prog:
    ENGINES = ("sync", "tensor", "vector", "scalar", "gpsimd")

    def __init__(self, nc, stack):
        self.nc = nc
        self.stack = stack
        self.ops = {e: [] for e in self.ENGINES}
        self.semh = {}
        self.cnt = {}

    def sem(self, name):
        if name not in self.semh:
            self.semh[name] = self.stack.enter_context(self.nc.semaphore(name))
            self.cnt[name] = 0
        return self.semh[name]

    def op(self, engine, fn, inc=None, by=1):
        if inc is not None:
            h = self.sem(inc)
            self.cnt[inc] += by
            self.ops[engine].append(lambda e, fn=fn, h=h, by=by: fn(e).then_inc(h, by))
            return self.cnt[inc]
        self.ops[engine].append(fn)
        return None

    def wait(self, engine, sem, value=None):
        if value is None:
            value = self.cnt.get(sem, 0)
        assert value <= self.cnt.get(sem, 0), (engine, sem, value)
        if value <= 0:
            return
        h = self.sem(sem)
        self.ops[engine].append(lambda e, h=h, v=value: e.wait_ge(h, v))

    def dma(self, engine, out, in_, inc):
        return self.op(engine, lambda e, o=out, i=in_: e.dma_start(out=o, in_=i),
                       inc=inc, by=16)

    def replay(self):
        with self.nc.Block() as block:
            for en in self.ENGINES:
                opl = self.ops[en]

                def body(eng, opl=opl):
                    for f in opl:
                        f(eng)
                getattr(block, en)(body)


class Ker:
    def __init__(self, chunks, use_coll, dbg=False):
        self.chunks = chunks
        self.use_coll = use_coll
        self.dbg = dbg
        self.TA = TOKS * chunks

    def build(self):
        nc = bass.Bass("TRN2", target_bir_lowering=False, debug=False)
        self.nc = nc
        TA = self.TA
        din = lambda n, sh: nc.dram_tensor(n, sh, F32, kind="ExternalInput").ap()
        self.x_in = din("x", [D, TA])
        self.wq = din("wq", [H, 128, D])
        self.wk = din("wk", [H, 128, D])
        self.wv = din("wv", [NC, 128, D])
        self.wg = din("wg", [NC, 128, D])
        self.wo = din("wo", [NC, 128, D])
        self.wf = din("wf", [NC, 128, D])
        self.vecs = din("vecs", [128, 5 * NC])
        self.slope = din("slope", [1, H])
        self.mask01 = din("mask01", [128, 1])
        self.arow = din("arow", [128, BLK])
        self.krow = din("krow", [128, BLK])
        self.krowt = din("krowt", [128, 2])
        self.idxT = din("idxT", [2 * 128, BLK])
        self.identw = din("identw", [128, 128])

        self.out_d = nc.dram_tensor("out", [D, TA], F32, kind="ExternalOutput").ap()
        spk = "ExternalOutput" if self.dbg else "Internal"
        self.attn_sp = nc.dram_tensor("attn_sp", [D, TA], BF16, kind=spk).ap()
        self.qdec_sp = nc.dram_tensor("qdec_sp", [D, TA], BF16, kind=spk).ap()
        self.gate_sp = nc.dram_tensor("gate_sp", [D, TA], BF16, kind=spk).ap()
        self.xn_sp = nc.dram_tensor("xn_sp", [D, TA], BF16, kind=spk).ap()
        self.x2_sp = nc.dram_tensor("x2_sp", [D, TA], F32, kind=spk).ap()
        if self.dbg:
            self.ln1_dp = nc.dram_tensor("ln1_dp", [128, NC * TOKS], BF16,
                                         kind="ExternalOutput").ap()
            self.vtm_dp = nc.dram_tensor("vtm_dp", [128, NC * D], BF16,
                                         kind="ExternalOutput").ap()
            self.sw_dp = nc.dram_tensor("sw_dp", [128, 64], F32,
                                        kind="ExternalOutput").ap()
            self.kvp_dp = nc.dram_tensor("kvp_dp", [128, H * 128], BF16,
                                         kind="ExternalOutput").ap()
            self.atc_dp = nc.dram_tensor("atc_dp", [D, TOKS], BF16,
                                         kind="ExternalOutput").ap()
            self.bd_dp = nc.dram_tensor("bd_dp", [128, 144], F32,
                                        kind="ExternalOutput").ap()
            self.wres_dp = nc.dram_tensor("wres_dp", [128, NC * D], BF16,
                                          kind="ExternalOutput").ap()
        self.kvf_d = nc.dram_tensor("kvf_d", [H * 128, 128], F32).ap()
        self.kvg_d = nc.dram_tensor("kvg_d", [2 * H * 128, 128], F32).ap()
        self.scl_d = nc.dram_tensor("scl_d", [128, 1], F32).ap()

        ctx = ExitStack()
        with ctx:
            sb = lambda n, sh, d: ctx.enter_context(nc.sbuf_tensor(n, sh, d))
            psb = lambda n, sh, d=F32: ctx.enter_context(nc.psum_tensor(n, sh, d))
            arA = sb("arA", [128, 65536], U8)[:]
            arB = sb("arB", [128, 65536], U8)[:]
            arC = sb("arC", [128, 49152], U8)[:]
            arD = sb("arD", [128, 8192], U8)[:]
            A2, B2v, C2, D2v = (a.bitcast(BF16) for a in (arA, arB, arC, arD))
            A4, C4 = arA.bitcast(F32), arC.bitcast(F32)

            self.ln1 = A2[:, 0:NC * TOKS]
            self.xt = [A4[:, 0:4096], A4[:, 4096:8192]]
            self.x2t = [A4[:, 8192:12288], A4[:, 12288:16384]]
            self.x22 = [A4[:, 0:8192], A4[:, 8192:16384]]
            self.vtm = B2v[:, 0:NC * D]
            self.wres = B2v[:, 0:NC * D]
            self.xblk = [C4[:, 0:2048], C4[:, 2048:4096]]
            self.xbb = [C2[:, 8192:10240], C2[:, 10240:12288]]
            self.sqb = [C2[:, 12288:14336], C2[:, 14336:16384]]
            self.wvsl = [C2[:, 0:8192], C2[:, 8192:16384]]
            self.qraw_r = [C2[:, 0:2048], C2[:, 2048:4096]]
            self.qdech_r = [C2[:, 4096:6144], C2[:, 6144:8192]]
            self.kfm_r = [C2[:, 8192:10240], C2[:, 10240:12288]]
            self.attnh = C2[:, 12288:14336]
            self.wqh_r = [C2[:, 14336:16384], C2[:, 16384:18432]]
            self.wkh_r = [C2[:, 18432:20480], C2[:, 20480:22528]]
            self.attnt = [C2[:, 0:4096], C2[:, 4096:8192]]
            self.gatet = [C2[:, 8192:12288], C2[:, 12288:16384]]
            self.qdect = C2[:, 16384:20480]
            self.sqt = C2[:, 20480:24576]
            self.x2bB = C2[:, 0:8192]
            self.sq2B = C2[:, 8192:16384]
            self.xnB = C2[:, 16384:24576]
            self.wfst = [C4[:, 8192:10240], C4[:, 10240:12288]]
            self.qdec_r = [D2v[:, 0:BLK], D2v[:, BLK:2 * BLK]]
            self.diag_r = [D2v[:, 2 * BLK:4 * BLK], D2v[:, 4 * BLK:6 * BLK]]
            self.qkT = D2v[:, 6 * BLK:8 * BLK]
            self.kkdT = D2v[:, 8 * BLK:9 * BLK]
            self.gatesb = D2v[:, 0:2048]

            self.cvec = sb("cvec", [128, 5 * NC], F32)[:]
            self.gscale = sb("gscale", [128, NC], F32)[:]
            self.bprime = sb("bprime", [128, NC], F32)[:]
            self.lnbb = sb("lnbb", [128, NC], BF16)[:]
            self.nsl = sb("nsl", [128, H], F32)[:]
            self.bdcol = sb("bdcol", [128, H], F32)[:]
            self.bdpow = sb("bdpow", [128, NBLK * H], F32)[:]
            self.onesb = sb("onesb", [128, 1], BF16)[:]
            self.onesf = sb("onesf", [1, 128], F32)[:]
            self.identb = sb("identb", [128, 128], BF16)[:]
            self.identf = sb("identf", [128, 128], F32)[:]
            self.maskc = sb("maskc", [128, 1], F32)[:]
            self.arow_s = sb("arow_s", [128, BLK], F32)[:]
            self.idxT_s = sb("idxT_s", [128, 2 * BLK], F32)[:]
            self.slope_s = sb("slope_s", [1, H], F32)[:]
            self.kvst = sb("kvst", [128, H * 128], F32)[:]
            self.kvbf = sb("kvbf", [128, 128], BF16)[:]
            self.kvprevb = sb("kvprevb", [128, H * 128], BF16)[:]
            self.rowsA = sb("rowsA", [1, 512], F32)[:]
            self.rowsE = sb("rowsE", [1, 768], F32)[:]
            self.kdcolT = sb("kdcolT", [128, 4], F32)[:]
            self.krowt_s = sb("krowt_s", [128, 2], F32)[:]
            self.sclc = sb("sclc", [128, 20], F32)[:]
            self.srow = sb("srow", [1, 8], F32)[:]

            self.cpool = sb("cpool", [128, 16], F32)[:]
            self.cvals = [0.0, 1.0, 1e-5, 1e-6, -0.5, 1.0 / D,
                          1.0 + 1e-5, 1e-10] + \
                [256.0 * j for j in range(1, NBLK)]
            for i, v in enumerate(self.cvals):
                nc.const_aps.register(F32, v, self.cpool[:, i:i + 1])

            self.pb = [psb(f"pb{i}", [128, 512]) for i in range(6)]
            self.pbt = psb("pbt", [128, 256], BF16)
            self.pkv = psb("pkv", [128, 128])

            self.sbuf_left = nc.sbuf_bytes_remaining

            P = Prog(nc, ctx)
            self.P = P
            self.emit()
            P.replay()
        return nc

    def emit(self):
        P = self.P
        self._kkdT_n = 0
        self._qkT_n = 0
        self.emit_consts()
        for c in range(self.chunks):
            self.emit_ln1(c)
            self.emit_vproj(c)
            if self.dbg and c == 0:
                self.emit_dumps()
            for h in range(H):
                self.emit_head_pipe(c, h)
            self.emit_gate(c)
        if self.use_coll:
            self.emit_collective()
        else:
            pass
        for c in range(self.chunks):
            self.emit_wout_load(c)
            for t in range(NBLK):
                self.emit_e1(c, t)
        for c in range(self.chunks):
            self.emit_signw(c)
            if self.dbg and c == 0:
                self.emit_sw_dumps()
            for t2 in range(TOKS // 512):
                self.emit_e2(c, t2)
        P.wait("gpsimd", "out_st0")
        P.wait("gpsimd", "out_st1")
        P.op("gpsimd", lambda e: e.nop())

    def emit_consts(self):
        P, s = self.P, self
        P.dma("sync", s.cvec, s.vecs, inc="c_ld")
        P.dma("sync", s.slope_s, s.slope, inc="c_ld")
        P.dma("sync", s.maskc, s.mask01, inc="c_ld")
        P.dma("sync", s.arow_s, s.arow, inc="c_ld")
        P.dma("sync", s.krowt_s, s.krowt, inc="c_ld")
        P.dma("sync", s.idxT_s.rearrange("p (a c) -> p a c", a=2),
              s.idxT.rearrange("(a p) c -> p a c", p=128), inc="c_ld")
        P.dma("gpsimd", s.identb, s.identw, inc="c_ldg")
        P.dma("sync", s.identf, s.identw, inc="c_ld")
        P.op("gpsimd", lambda e: e.memset(s.onesb, 1.0))
        P.op("gpsimd", lambda e: e.memset(s.onesf, 1.0))
        P.op("gpsimd", lambda e: e.memset(s.kvst, 0.0))
        P.op("gpsimd", lambda e: e.memset(s.kvbf, 0.0))
        P.op("gpsimd", lambda e: e.memset(s.kvprevb, 0.0))
        P.op("gpsimd", lambda e: e.memset(s.bdpow[:, 0:16], 1.0))
        for i, v in enumerate(s.cvals):
            P.op("gpsimd", lambda e, i=i, v=v: e.memset(s.cpool[:, i:i + 1], v))
        P.op("gpsimd", lambda e: e.engine_nop(), inc="c_ms")
        P.wait("tensor", "c_ld")
        P.wait("tensor", "c_ldg")
        P.wait("tensor", "c_ms")
        P.op("tensor", lambda e: e.matmul(s.pb[5][:, 0:16], s.onesf, s.slope_s,
                                          start=True, stop=True), inc="c_pe")
        P.wait("vector", "c_pe")
        P.op("vector", lambda e: e.tensor_scalar_mul(s.nsl, s.pb[5][:, 0:16], -1.0),
             inc="c_ve")
        P.wait("scalar", "c_ve")
        P.op("scalar", lambda e: e.activation(s.bdcol, s.nsl, AF.Exp, scale=256.0),
             inc="c_se")
        for j in range(1, NBLK):
            P.op("scalar", lambda e, j=j: e.activation(
                s.bdpow[:, j * 16:(j + 1) * 16], s.nsl, AF.Exp, scale=256.0 * j),
                inc="c_se")

    def emit_ln1(self, c):
        P, s = self.P, self
        SB = 128
        nsb = TOKS // SB
        for sem in ("e2_se", "e2_pe", "e2_ve", "e1_ve", "e1_pe", "b1_pe", "b1_ve",
                    "b1_se", "b2_pe", "b2_se", "b2_sp0", "b2_sp1", "out_st0",
                    "out_st1", "e1_sp0", "e1_sp1", "c_ve", "c_se"):
            P.wait("sync", sem)
            P.wait("gpsimd", sem)
            P.wait("tensor", sem)
            P.wait("vector", sem)
            P.wait("scalar", sem)
        done_ve = {}
        done_pe = {}
        rows_done = {}
        for j in range(nsb):
            buf = j % 2
            t0 = c * TOKS + j * SB
            if j >= 2:
                P.wait("sync", "a1_ve", done_ve[j - 2])
                P.wait("gpsimd", "a1_pe", done_pe[j - 2])
            src = s.x_in[:, t0:t0 + SB].rearrange("(n p) b -> p n b", p=128)
            P.dma("sync", s.xblk[buf].rearrange("p (n b) -> p n b", b=SB), src,
                  inc=f"a1_ldx{buf}")
            nx = P.cnt[f"a1_ldx{buf}"]
            P.dma("gpsimd", s.xbb[buf].rearrange("p (n b) -> p n b", b=SB), src,
                  inc=f"a1_ldxb{buf}")
            nxb = P.cnt[f"a1_ldxb{buf}"]
            P.wait("scalar", f"a1_ldxb{buf}", nxb)
            if j >= 2:
                P.wait("scalar", "a1_pe", done_pe[j - 2])
            P.op("scalar", lambda e, buf=buf: e.activation(
                s.sqb[buf], s.xbb[buf], AF.Square), inc="a1_se")
            nse = P.cnt["a1_se"]
            P.wait("tensor", f"a1_ldxb{buf}", nxb)
            P.wait("tensor", "a1_se", nse)
            if j >= 1:
                P.wait("tensor", "a1_rows", rows_done[j - 1])
            for n in range(NC):
                P.op("tensor", lambda e, n=n, buf=buf: e.matmul(
                    s.pb[4][0:1, 0:SB], s.onesb,
                    s.xbb[buf][:, n * SB:(n + 1) * SB],
                    start=(n == 0), stop=(n == NC - 1)))
            for n in range(NC):
                P.op("tensor", lambda e, n=n, buf=buf: e.matmul(
                    s.pb[4][0:1, SB:2 * SB], s.onesb,
                    s.sqb[buf][:, n * SB:(n + 1) * SB],
                    start=(n == 0), stop=(n == NC - 1)),
                    inc=("a1_pe" if n == NC - 1 else None))
            done_pe[j] = P.cnt["a1_pe"]
            P.wait("vector", "a1_pe", done_pe[j])
            P.op("vector", lambda e: e.tensor_scalar_mul(
                s.rowsA[:, 0:SB], s.pb[4][0:1, 0:SB], 1.0 / D))
            P.op("vector", lambda e: e.tensor_tensor(
                s.rowsE[:, 0:SB], s.rowsA[:, 0:SB], s.rowsA[:, 0:SB], ALU.mult))
            P.op("vector", lambda e: e.scalar_tensor_tensor(
                s.rowsA[:, SB:2 * SB], s.pb[4][0:1, SB:2 * SB], 1.0 / D,
                s.rowsE[:, 0:SB], ALU.mult, ALU.subtract), inc="a1_ve")
            P.wait("scalar", "a1_ve", P.cnt["a1_ve"])
            P.op("scalar", lambda e: e.activation(
                s.rowsA[:, SB:2 * SB], s.rowsA[:, SB:2 * SB], AF.Ln, bias=1e-5))
            P.op("scalar", lambda e: e.activation(
                s.rowsA[:, SB:2 * SB], s.rowsA[:, SB:2 * SB], AF.Exp, scale=-0.5),
                inc="a1_se")
            P.wait("vector", "a1_se", P.cnt["a1_se"])
            P.op("vector", lambda e: e.scalar_tensor_tensor(
                s.rowsA[:, 0:SB], s.rowsA[:, 0:SB], -1.0,
                s.rowsA[:, SB:2 * SB], ALU.mult, ALU.mult), inc="a1_rows")
            rows_done[j] = P.cnt["a1_rows"]
            P.wait("tensor", "a1_rows", rows_done[j])
            if j >= 1:
                P.wait("tensor", "a1_ve", done_ve[j - 1])
            P.op("tensor", lambda e: e.matmul(
                s.pb[5][:, 0:2 * SB], s.onesf, s.rowsA[:, 0:2 * SB],
                start=True, stop=True), inc="a1_bc")
            P.wait("vector", "a1_bc", P.cnt["a1_bc"])
            P.wait("vector", f"a1_ldx{buf}", nx)
            for n in range(NC):
                dst = s.ln1[:, n * TOKS + j * SB: n * TOKS + (j + 1) * SB]
                P.op("vector", lambda e, n=n, buf=buf, dst=dst: e.tensor_tensor(
                    dst, s.xblk[buf][:, n * SB:(n + 1) * SB],
                    s.pb[5][:, SB:2 * SB], ALU.mult))
                P.op("vector", lambda e, dst=dst: e.tensor_tensor(
                    dst, dst, s.pb[5][:, 0:SB], ALU.add),
                    inc=("a1_ve" if n == NC - 1 else None))
            done_ve[j] = P.cnt["a1_ve"]
        self.ln1_done = P.cnt["a1_ve"]

    def emit_vproj(self, c):
        P, s = self.P, self
        for sem in ("a1_pe", "a1_ve", "b1_pe", "b1_ve", "b1_se", "b2_pe",
                    "e2_pe"):
            P.wait("gpsimd", sem)
        evac_pe = {}
        for vf in range(4):
            buf = vf % 2
            if vf >= 2:
                P.wait("gpsimd", "b0_pe", evac_pe[vf - 2])
            P.dma("gpsimd", s.wvsl[buf].rearrange("p (k cc) -> p k cc", cc=512),
                  s.wv[:, :, vf * 512:(vf + 1) * 512].rearrange("k p cc -> p k cc"),
                  inc=f"b0_wld{buf}")
            nw = P.cnt[f"b0_wld{buf}"]
            P.wait("tensor", f"b0_wld{buf}", nw)
            if vf == 0:
                P.wait("tensor", "a1_ve", self.ln1_done)
                P.wait("tensor", "b1_pe")
                P.wait("tensor", "b2_pe")
                P.wait("tensor", "b2_se")
                P.wait("tensor", "e1_pe")
                P.wait("tensor", "e2_pe")
                P.wait("tensor", "e2_ve")
            for tc in range(NC):
                pbank = s.pb[tc % 2]
                if tc >= 2:
                    P.wait("tensor", "b0_se", P.cnt["b0_se"] - 1)
                for k in range(NC):
                    P.op("tensor", lambda e, k=k, tc=tc, buf=buf, pbank=pbank: e.matmul(
                        pbank[:, 0:512],
                        s.ln1[:, k * TOKS + tc * 128:k * TOKS + (tc + 1) * 128],
                        s.wvsl[buf][:, k * 512:(k + 1) * 512],
                        start=(k == 0), stop=(k == NC - 1)),
                        inc=("b0_pe" if k == NC - 1 else None))
                P.wait("scalar", "b0_pe", P.cnt["b0_pe"])
                P.op("scalar", lambda e, tc=tc, vf=vf, pbank=pbank: e.activation(
                    s.vtm[:, tc * D + vf * 512: tc * D + (vf + 1) * 512],
                    pbank[:, 0:512], AF.Silu), inc="b0_se")
            evac_pe[vf] = P.cnt["b0_pe"]
        self.v_done = P.cnt["b0_se"]

    def emit_tables(self, c, h):
        P, s = self.P, self
        slot = h % 2
        sc = s.nsl[:, h:h + 1]
        if h < 2:
            for sem in ("b2_sp0", "b2_sp1", "b2_se", "b1_ve", "b1_se"):
                P.wait("scalar", sem)
        else:
            P.wait("scalar", "b1_ve", self._scan_ve.get(h - 2, 0))
        P.op("scalar", lambda e: e.activation(
            s.qdec_r[slot], s.arow_s, AF.Exp, scale=sc))
        P.op("scalar", lambda e: e.activation(
            s.diag_r[slot], s.idxT_s, AF.Exp, scale=sc))
        self._tbl_se[h] = P.op("scalar", lambda e: e.activation(
            s.kdcolT[:, slot * 2:slot * 2 + 2], s.krowt_s, AF.Exp, scale=sc),
            inc="b1_se")

    def emit_head_pipe(self, c, h):
        P, s = self.P, self
        hb = h % 2
        if h == 0:
            self._tbl_se = {}
            self._scan_ve = {}
            self._scan_pe = {}
            self._proj_pe = {}
            self._qevac = {}
            self._kevac = {}
            self._qdech_ve = {}
            self._qspill = {}
            self._atspill = {}
            self._bank_se = {}
        self.emit_tables(c, h)
        wqh, wkh = s.wqh_r[hb], s.wkh_r[hb]
        if h == 0:
            for sem in ("b0_pe", "b0_se", "a1_se", "e1_se", "e1_ve", "e2_pe",
                        "b1_pe"):
                P.wait("gpsimd", sem)
        else:
            P.wait("gpsimd", "b1_pe", self._proj_pe.get(h - 2, 0))
        P.dma("gpsimd", wqh, s.wq[h], inc=f"b1_wld{hb}")
        P.dma("gpsimd", wkh, s.wk[h], inc=f"b1_wld{hb}")
        nw = P.cnt[f"b1_wld{hb}"]

        first = True
        for i in range(8):
            proj, qt = divmod(i, 4)
            wt = wqh if proj == 0 else wkh
            bank_i = 2 * proj + qt % 2
            bank = s.pb[bank_i]
            dstr = s.qraw_r[hb] if proj == 0 else s.kfm_r[hb]
            if first:
                P.wait("tensor", f"b1_wld{hb}", nw)
                if h == 0:
                    P.wait("tensor", "b0_se", self.v_done)
                first = False
            P.wait("tensor", "b1_se", self._bank_se.get(bank_i, 0))
            for k in range(NC):
                P.op("tensor", lambda e, k=k, qt=qt, wt=wt, bank=bank: e.matmul(
                    bank[:, 0:512], wt[:, k * 128:(k + 1) * 128],
                    s.ln1[:, k * TOKS + qt * 512:k * TOKS + (qt + 1) * 512],
                    start=(k == 0), stop=(k == NC - 1)),
                    inc=("b1_pe" if k == NC - 1 else None))
            npe = P.cnt["b1_pe"]
            P.wait("scalar", "b1_pe", npe)
            nev = P.op("scalar", lambda e, qt=qt, bank=bank, dstr=dstr:
                       e.activation(dstr[:, qt * 512:(qt + 1) * 512],
                                    bank[:, 0:512], AF.Silu), inc="b1_se")
            self._bank_se[bank_i] = nev
            if i == 3:
                self._qevac[h] = nev
            if i == 7:
                self._kevac[h] = nev
            if h >= 1:
                self.emit_scan_block(c, h - 1, i)
        self._proj_pe[h] = P.cnt["b1_pe"]
        P.wait("vector", "b1_se", self._qevac[h])
        P.wait("vector", "b1_se", self._tbl_se[h])
        P.wait("vector", "b1_sp", self._qspill.get(h - 2, 0))
        rep = lambda ap: ap.unsqueeze(1).broadcast_to([128, NBLK, BLK])
        self._qdech_ve[h] = P.op("vector", lambda e, hb=hb, slot=h % 2:
                                 e.tensor_tensor(
            s.qdech_r[hb].rearrange("p (r b) -> p r b", b=BLK),
            s.qraw_r[hb].rearrange("p (r b) -> p r b", b=BLK),
            rep(s.qdec_r[slot]), ALU.mult), inc="b1_ve")
        if self.use_coll:
            P.wait("sync", "b1_ve", self._qdech_ve[h])
            P.dma("sync",
                  s.qdec_sp[h * 128:(h + 1) * 128, c * TOKS:(c + 1) * TOKS],
                  s.qdech_r[hb], inc="b1_sp")
            self._qspill[h] = P.cnt["b1_sp"]
        if h == H - 1:
            for j in range(8):
                self.emit_scan_block(c, h, j)

    def emit_scan_block(self, c, h, j):
        P, s = self.P, self
        hb = h % 2
        slot = h % 2
        t0 = j * BLK
        hs = h * 128
        P.wait("scalar", "b1_ve", getattr(self, "_last_kvstt", 0))
        P.wait("scalar", "b1_pe", getattr(self, "_last_nat", 0))
        nbf = P.op("scalar", lambda e, hs=hs: e.copy(
            s.kvbf, s.kvst[:, hs:hs + 128]), inc="b1_se")
        P.wait("tensor", "b1_se", self._kevac[h])
        P.wait("tensor", "b1_ve", getattr(self, "_kkdT_n", 0))
        P.op("tensor", lambda e, t0=t0, hb=hb: e.transpose(
            self.pbt[:][:, 0:128], s.kfm_r[hb][:, t0:t0 + 128], s.identb))
        ntr = P.op("tensor", lambda e, t0=t0, hb=hb: e.transpose(
            self.pbt[:][:, 128:256], s.kfm_r[hb][:, t0 + 128:t0 + 256],
            s.identb), inc="b1_pe")
        P.wait("tensor", "b1_ve", getattr(self, "_qkT_n", 0))
        P.op("tensor", lambda e, t0=t0, hb=hb: e.matmul(
            s.pb[4][:, 0:256], s.kfm_r[hb][:, t0:t0 + 128],
            s.qraw_r[hb][:, t0:t0 + 256], start=True, stop=True))
        nqk = P.op("tensor", lambda e, t0=t0, hb=hb: e.matmul(
            s.pb[4][:, 256:512], s.kfm_r[hb][:, t0 + 128:t0 + 256],
            s.qraw_r[hb][:, t0:t0 + 256], start=True, stop=True), inc="b1_pe")
        P.wait("vector", "b1_pe", ntr)
        if j == 0:
            P.wait("vector", "b1_se", self._tbl_se[h])
        for i2 in range(2):
            P.op("vector", lambda e, i2=i2, slot=slot: e.tensor_scalar_mul(
                s.kkdT[:, i2 * 128:(i2 + 1) * 128],
                self.pbt[:][:, i2 * 128:(i2 + 1) * 128],
                s.kdcolT[:, slot * 2 + i2:slot * 2 + i2 + 1]),
                inc=("b1_ve" if i2 == 1 else None))
        self._kkdT_n = P.cnt["b1_ve"]
        P.wait("vector", "b1_pe", nqk)
        P.op("vector", lambda e, slot=slot: e.tensor_tensor(
            s.qkT[:, 0:256], s.pb[4][:, 0:256],
            s.diag_r[slot][:, 0:256], ALU.mult))
        self._qkT_n = P.op("vector", lambda e, slot=slot: e.tensor_tensor(
            s.qkT[:, 256:512], s.pb[4][:, 256:512],
            s.diag_r[slot][:, 256:512], ALU.mult), inc="b1_ve")
        P.wait("tensor", "b1_ve", self._qkT_n)
        P.wait("tensor", "b1_se", nbf)
        P.wait("tensor", "b1_se", getattr(self, "_last_atevac", 0))
        vt0 = (2 * j) * D + hs
        vt1 = (2 * j + 1) * D + hs
        P.op("tensor", lambda e, vt0=vt0: e.matmul(
            s.pb[5][:, 0:256], s.vtm[:, vt0:vt0 + 128], s.qkT[:, 0:256],
            start=True, stop=False))
        P.op("tensor", lambda e, vt1=vt1: e.matmul(
            s.pb[5][:, 0:256], s.vtm[:, vt1:vt1 + 128], s.qkT[:, 256:512],
            start=False, stop=False))
        nat = P.op("tensor", lambda e, t0=t0, hb=hb: e.matmul(
            s.pb[5][:, 0:256], s.kvbf, s.qdech_r[hb][:, t0:t0 + 256],
            start=False, stop=True), inc="b1_pe")
        self._last_nat = nat
        P.wait("tensor", "b1_ve", self._kkdT_n)
        P.wait("tensor", "b1_ve", getattr(self, "_last_kvstt", 0))
        P.op("tensor", lambda e, vt0=vt0: e.matmul(
            s.pkv[:], s.kkdT[:, 0:128], s.vtm[:, vt0:vt0 + 128],
            start=True, stop=False))
        nkv = P.op("tensor", lambda e, vt1=vt1: e.matmul(
            s.pkv[:], s.kkdT[:, 128:256], s.vtm[:, vt1:vt1 + 128],
            start=False, stop=True), inc="b1_pe")
        P.wait("scalar", "b1_pe", nat)
        if j == 0:
            P.wait("scalar", "b1_sp", self._atspill.get(h - 1, 0))
        self._last_atevac = P.op("scalar", lambda e, t0=t0: e.copy(
            s.attnh[:, t0:t0 + 256], s.pb[5][:, 0:256]), inc="b1_se")
        P.wait("vector", "b1_pe", nkv)
        self._last_kvstt = P.op("vector", lambda e, hs=hs, h=h:
                                e.scalar_tensor_tensor(
            s.kvst[:, hs:hs + 128], s.kvst[:, hs:hs + 128],
            s.bdcol[:, h:h + 1], s.pkv[:], ALU.mult, ALU.add), inc="b1_ve")
        if j == NBLK - 1:
            self._scan_pe[h] = P.cnt["b1_pe"]
            self._scan_ve[h] = P.cnt["b1_ve"]
            P.wait("sync", "b1_se", self._last_atevac)
            P.dma("sync",
                  s.attn_sp[h * 128:(h + 1) * 128, c * TOKS:(c + 1) * TOKS],
                  s.attnh, inc="b1_sp")
            self._atspill[h] = P.cnt["b1_sp"]

    def emit_gate(self, c):
        P, s = self.P, self
        P.wait("gpsimd", "b1_pe")
        P.wait("gpsimd", "b1_ve")
        for k in range(NC):
            P.dma("gpsimd", s.wres[:, k * D:(k + 1) * D], s.wg[k], inc="b2_wld")
        nw = P.cnt["b2_wld"]
        P.wait("tensor", "b2_wld", nw)
        P.wait("tensor", "b1_se")
        P.wait("tensor", "b1_ve")
        P.wait("scalar", "b1_ve")
        P.wait("scalar", "b1_pe")
        evacs = []
        for th in range(2):
            for f in range(NC):
                fi = th * NC + f
                b0, b1 = (s.pb[0], s.pb[1]) if f % 2 == 0 else (s.pb[2], s.pb[3])
                if fi >= 2:
                    P.wait("tensor", "b2_se", evacs[fi - 2])
                for k in range(NC):
                    P.op("tensor", lambda e, k=k, f=f, th=th, b0=b0: e.matmul(
                        b0[:, 0:512], s.wres[:, k * D + f * 128:k * D + (f + 1) * 128],
                        s.ln1[:, k * TOKS + th * 1024:k * TOKS + th * 1024 + 512],
                        start=(k == 0), stop=(k == NC - 1)))
                    P.op("tensor", lambda e, k=k, f=f, th=th, b1=b1: e.matmul(
                        b1[:, 0:512], s.wres[:, k * D + f * 128:k * D + (f + 1) * 128],
                        s.ln1[:, k * TOKS + th * 1024 + 512:k * TOKS + th * 1024 + 1024],
                        start=(k == 0), stop=(k == NC - 1)),
                        inc=("b2_pe" if k == NC - 1 else None))
                npe = P.cnt["b2_pe"]
                gb = s.gatesb[:, (f % 2) * 1024:(f % 2) * 1024 + 1024]
                P.wait("scalar", "b2_pe", npe)
                P.wait("scalar", f"b2_sp{f % 2}",
                       P.cnt.get(f"b2_sp{f % 2}", 0))
                P.op("scalar", lambda e, gb=gb, b0=b0: e.activation(
                    gb[:, 0:512], b0[:, 0:512], AF.Sigmoid))
                nse = P.op("scalar", lambda e, gb=gb, b1=b1: e.activation(
                    gb[:, 512:1024], b1[:, 0:512], AF.Sigmoid), inc="b2_se")
                evacs.append(nse)
                P.wait("vector", "b2_se", nse)
                nve = P.op("vector", lambda e, gb=gb, f=f: e.tensor_scalar_mul(
                    gb, gb, s.cvec[:, f:f + 1]), inc="b2_ve")
                P.wait("sync", "b2_ve", nve)
                P.wait("sync", "b2_se", nse)
                P.dma("sync",
                      s.gate_sp[f * 128:(f + 1) * 128,
                                c * TOKS + th * 1024:c * TOKS + th * 1024 + 1024],
                      gb, inc=f"b2_sp{f % 2}")

    def emit_collective(self):
        P, s = self.P, self
        P.wait("gpsimd", "b1_ve")
        P.dma("gpsimd", s.kvf_d.rearrange("(h p) d -> p h d", p=128),
              s.kvst.rearrange("p (h d) -> p h d", d=128), inc="kv_sp")
        P.wait("gpsimd", "kv_sp")
        P.op("gpsimd", lambda e: e.collective_compute(
            "AllGather", ALU.bypass, replica_groups=REPLICA_PAIRS,
            ins=[s.kvf_d], outs=[s.kvg_d]), inc="cc_s")
        P.wait("gpsimd", "cc_s")
        P.dma("gpsimd", s.kvprevb.rearrange("p (h d) -> p h d", d=128),
              s.kvg_d[0:H * 128, :].rearrange("(h p) d -> p h d", p=128),
              inc="kv_ld")
        P.wait("vector", "kv_ld")
        P.op("vector", lambda e: e.tensor_scalar_mul(s.kvprevb, s.kvprevb, s.maskc),
             inc="kv_ve")
        if self.dbg:
            P.wait("sync", "kv_ve")
            P.dma("sync", s.kvp_dp, s.kvprevb, inc="dbg_st")
            P.wait("sync", "c_ve")
            P.wait("sync", "c_se")
            P.dma("sync", s.bd_dp[:, 0:16], s.bdcol, inc="dbg_st")
            P.dma("sync", s.bd_dp[:, 16:144], s.bdpow, inc="dbg_st")

    def emit_wout_load(self, c):
        if c > 0:
            return
        P, s = self.P, self
        P.wait("gpsimd", "b2_pe")
        for k in range(NC):
            P.dma("gpsimd", s.wres[:, k * D:(k + 1) * D], s.wo[k], inc="wo_ld")
        self._wo_n = P.cnt["wo_ld"]

    def emit_e1(self, c, t):
        P, s = self.P, self
        buf = t % 2
        off = c * TOKS + t * BLK
        use_corr = self.use_coll

        if t == 0:
            for sem in ("b1_pe", "b1_ve", "b1_se", "b2_pe", "b2_se", "a1_ve",
                        "a1_pe", "b2_sp0", "b2_sp1", "b1_sp", "e1_ve", "e1_pe",
                        "e1_se", "e1_sp0", "e1_sp1"):
                P.wait("sync", sem)
                P.wait("tensor", sem)
                P.wait("vector", sem)
                P.wait("scalar", sem)
            self._e1_free = {}
        else:
            if t >= 2:
                P.wait("sync", "e1_ve", self._e1_free[t - 2])
                P.wait("sync", f"e1_sp{buf}", self._e1_sp.get(t - 2, 0))
        fm = lambda apdram: apdram[:, off:off + BLK].rearrange(
            "(n p) b -> p n b", p=128)
        P.dma("sync", s.xt[buf].rearrange("p (n b) -> p n b", b=BLK),
              fm(s.x_in), inc=f"e1_ldx{buf}")
        nldx = P.cnt[f"e1_ldx{buf}"]
        P.dma("sync", s.attnt[buf].rearrange("p (n b) -> p n b", b=BLK),
              fm(s.attn_sp), inc=f"e1_lda{buf}")
        nlda = P.cnt[f"e1_lda{buf}"]
        P.dma("sync", s.gatet[buf].rearrange("p (n b) -> p n b", b=BLK),
              fm(s.gate_sp), inc=f"e1_ldg{buf}")
        nldg = P.cnt[f"e1_ldg{buf}"]
        if use_corr:
            if t >= 1:
                P.wait("sync", "e1_ve", self._corr_done.get(t - 1, 0))
            else:
                self._corr_done = {}
            P.dma("sync", s.qdect.rearrange("p (n b) -> p n b", b=BLK),
                  fm(s.qdec_sp), inc="e1_ldq")
            nldq = P.cnt["e1_ldq"]

        if use_corr:
            P.wait("vector", f"e1_lda{buf}", nlda)
            if t == 0:
                self._corr_tt = {}
                self._corr_mm = {}
            P.wait("tensor", "e1_ldq", nldq)
            if t == 0 and self.use_coll:
                P.wait("tensor", "kv_ve")
            if t >= 1:
                P.wait("tensor", "e1_ve", self._kvscale_n)
            if t >= 1:
                P.wait("tensor", "e1_ve", self._z_done.get(t - 1, 0))
            for h in range(H):
                pc = s.pb[h % 2]
                if h >= 2:
                    P.wait("tensor", "e1_ve", self._corr_tt[h - 2])
                self._corr_mm[h] = P.op("tensor", lambda e, h=h, pc=pc: e.matmul(
                    pc[:, 0:256], s.kvprevb[:, h * 128:(h + 1) * 128],
                    s.qdect[:, h * BLK:(h + 1) * BLK],
                    start=True, stop=True), inc="e1_pe")
                P.wait("vector", "e1_pe", self._corr_mm[h])
                self._corr_tt[h] = P.op("vector", lambda e, h=h, pc=pc: e.tensor_tensor(
                    s.attnt[buf][:, h * BLK:(h + 1) * BLK],
                    s.attnt[buf][:, h * BLK:(h + 1) * BLK],
                    pc[:, 0:256], ALU.add), inc="e1_ve")
            self._corr_done[t] = P.cnt["e1_ve"]
            if t < NBLK - 1:
                P.wait("vector", "e1_pe", self._corr_mm[H - 1])
                for h in range(H):
                    P.op("vector", lambda e, h=h: e.tensor_scalar_mul(
                        s.kvprevb[:, h * 128:(h + 1) * 128],
                        s.kvprevb[:, h * 128:(h + 1) * 128],
                        s.bdcol[:, h:h + 1]),
                        inc=("e1_ve" if h == H - 1 else None))
                self._kvscale_n = P.cnt["e1_ve"]
        ncorr = P.cnt.get("e1_ve", 0)

        P.wait("scalar", f"e1_lda{buf}", nlda)
        if use_corr:
            P.wait("scalar", "e1_ve", ncorr)
        if t >= 1:
            P.wait("scalar", "e1_pe", self._sq_cons.get(t - 1, 0))
        else:
            self._sq_cons = {}
        nsq = P.op("scalar", lambda e, buf=buf: e.activation(
            s.sqt, s.attnt[buf], AF.Square), inc="e1_se")
        P.wait("tensor", "e1_se", nsq)
        if t >= 1:
            P.wait("tensor", "e1_ve", self._e1_rows.get(t - 1, 0))
        else:
            self._e1_rows = {}
        for n in range(NC):
            P.op("tensor", lambda e, n=n: e.matmul(
                s.pb[4][0:1, 0:BLK], s.onesb, s.sqt[:, n * BLK:(n + 1) * BLK],
                start=(n == 0), stop=(n == NC - 1)),
                inc=("e1_pe" if n == NC - 1 else None))
        nssq = P.cnt["e1_pe"]
        self._sq_cons[t] = nssq
        P.wait("scalar", "e1_pe", nssq)
        P.op("scalar", lambda e: e.activation(
            s.rowsE[:, 0:BLK], s.pb[4][0:1, 0:BLK], AF.Ln, scale=1.0 / D, bias=1e-6))
        nrow1 = P.op("scalar", lambda e: e.activation(
            s.rowsE[:, 0:BLK], s.rowsE[:, 0:BLK], AF.Exp, scale=-0.5), inc="e1_se")
        P.wait("tensor", "e1_se", nrow1)
        if t >= 1:
            P.wait("tensor", "e1_ve", self._z_done.get(t - 1, 0))
        else:
            self._z_done = {}
        if use_corr:
            P.wait("tensor", "e1_ve", self._corr_done[t])
        nbc1 = P.op("tensor", lambda e: e.matmul(
            s.pb[1][:, 0:BLK], s.onesf, s.rowsE[:, 0:BLK], start=True, stop=True),
            inc="e1_pe")
        if self.dbg:
            P.wait("sync", "e1_ve", ncorr)
            P.dma("sync", self.atc_dp[:, t * BLK:(t + 1) * BLK]
                  .rearrange("(n p) b -> p n b", p=128),
                  s.attnt[buf].rearrange("p (n b) -> p n b", b=BLK), inc="dbg_at")
            P.wait("vector", "dbg_at")
        P.wait("vector", "e1_pe", nbc1)
        P.wait("vector", "e1_se", nsq)
        P.wait("vector", f"e1_ldg{buf}", nldg)
        rrep = s.pb[1][:, 0:BLK].unsqueeze(1).broadcast_to([128, NC, BLK])
        P.op("vector", lambda e, buf=buf, rrep=rrep: e.tensor_tensor(
            s.attnt[buf].rearrange("p (n b) -> p n b", b=BLK),
            s.attnt[buf].rearrange("p (n b) -> p n b", b=BLK), rrep, ALU.mult))
        P.op("vector", lambda e, buf=buf: e.tensor_tensor(
            s.attnt[buf], s.attnt[buf], s.gatet[buf], ALU.mult), inc="e1_ve")
        self._z_done[t] = P.cnt["e1_ve"]
        nz = self._z_done[t]

        P.wait("tensor", "e1_ve", nz)
        P.wait("tensor", "wo_ld", self._wo_n)
        nf = {}
        for f in range(NC):
            pob = s.pb[2 + (f % 2)]
            if f >= 2:
                P.wait("tensor", "e1_ve", nf[f - 2][1])
            for k in range(NC):
                P.op("tensor", lambda e, k=k, f=f, buf=buf, pob=pob: e.matmul(
                    pob[:, 0:256], s.wres[:, k * D + f * 128:k * D + (f + 1) * 128],
                    s.attnt[buf][:, k * BLK:(k + 1) * BLK],
                    start=(k == 0), stop=(k == NC - 1)),
                    inc=("e1_pe" if k == NC - 1 else None))
            npe = P.cnt["e1_pe"]
            P.wait("vector", "e1_pe", npe)
            if f == 0:
                P.wait("vector", f"e1_ldx{buf}", nldx)
                if t >= 2:
                    P.wait("vector", f"e1_sp{buf}", self._e1_sp.get(t - 2, 0))
            nev = P.op("vector", lambda e, f=f, buf=buf, pob=pob: e.tensor_tensor(
                s.x2t[buf][:, f * BLK:(f + 1) * BLK],
                s.xt[buf][:, f * BLK:(f + 1) * BLK], pob[:, 0:256], ALU.add),
                inc="e1_ve")
            nf[f] = (npe, nev)
        nx2 = nf[NC - 1][1]

        self._e1_free[t] = nx2
        P.wait("gpsimd", "e1_ve", nx2)
        P.dma("gpsimd", s.x2_sp[:, off:off + BLK].rearrange("(n p) b -> p n b", p=128),
              s.x2t[buf].rearrange("p (n b) -> p n b", b=BLK), inc=f"e1_sp{buf}")
        if not hasattr(self, "_e1_sp"):
            self._e1_sp = {}
        self._e1_sp[t] = P.cnt[f"e1_sp{buf}"]

    def emit_signw(self, c):
        P, s = self.P, self
        if c > 0:
            return
        P.wait("sync", "e1_pe")
        P.wait("sync", "e1_ve")
        P.wait("sync", "e1_se")
        P.wait("gpsimd", "e1_pe")
        nred = 0
        for k in range(NC):
            buf = k % 2
            if k >= 2:
                P.wait("sync", "sw_se", P.cnt["sw_se"] - 1)
                P.wait("sync", "sw_ve", P.cnt["sw_ve"] - 1)
            P.dma("sync", s.wfst[buf], s.wf[k], inc=f"sw_ld{buf}")
            nld = P.cnt[f"sw_ld{buf}"]
            P.wait("scalar", f"sw_ld{buf}", nld)
            P.op("scalar", lambda e, k=k, buf=buf: e.activation(
                s.wres[:, k * D:(k + 1) * D], s.wfst[buf], AF.Sign), inc="sw_se")
            P.wait("vector", f"sw_ld{buf}", nld)
            P.op("vector", lambda e, k=k, buf=buf: e.tensor_reduce(
                s.sclc[:, k:k + 1], s.wfst[buf], AX.X, ALU.max,
                apply_absolute_value=True), inc="sw_ve")
            nred = P.cnt["sw_ve"]
        P.wait("scalar", "sw_ve", nred)
        nhop = P.op("scalar", lambda e: e.copy(s.srow[:, 1:2], s.srow[:, 0:1]),
                    inc="sw_se")
        P.wait("vector", "sw_se", nhop)
        ng = P.op("vector", lambda e: e.tensor_reduce(
            s.sclc[:, 16:17], s.sclc[:, 0:16], AX.X, ALU.max), inc="sw_ve")
        P.wait("tensor", "sw_ve", ng)
        P.wait("tensor", "e1_pe")
        ntr2 = P.op("tensor", lambda e: e.transpose(
            s.pb[1][0:1, 0:128], s.sclc[:, 16:17], s.identf), inc="sw_pe")
        P.wait("vector", "sw_pe", ntr2)
        P.op("vector", lambda e: e.tensor_reduce(
            s.srow[:, 0:1], s.pb[1][0:1, 0:128], AX.X, ALU.max), inc="sw_ve")
        nscl = P.cnt["sw_ve"]
        P.wait("tensor", "sw_ve", nscl)
        P.wait("tensor", "e1_pe")
        nbc = P.op("tensor", lambda e: e.matmul(
            s.pb[0][:, 0:1], s.onesf, s.srow[:, 0:1], start=True, stop=True),
            inc="sw_pe")
        P.wait("scalar", "sw_pe", nbc)
        nsc = P.op("scalar", lambda e: e.copy(s.sclc[:, 17:18], s.pb[0][:, 0:1]),
                   inc="sw_se")
        P.wait("vector", "sw_se", nsc)
        P.op("vector", lambda e: e.tensor_scalar_mul(
            s.gscale, s.cvec[:, NC:2 * NC], s.sclc[:, 17:18]))
        P.op("vector", lambda e: e.tensor_copy(s.bprime, s.cvec[:, 2 * NC:3 * NC]),
             inc="sw_ve")
        self._sw_done = P.cnt["sw_ve"]
        self._sw_se = P.cnt["sw_se"]

    def emit_e2(self, c, t2):
        P, s = self.P, self
        T2 = 512
        buf = t2 % 2
        off = c * TOKS + t2 * T2
        NCH = NC
        if t2 == 0 and c == 0:
            for sem in ("e1_sp0", "e1_sp1", "e1_ve", "e1_se", "e1_pe",
                        "sw_se", "sw_ve"):
                P.wait("sync", sem)
                P.wait("vector", sem)
                P.wait("scalar", sem)
                P.wait("tensor", sem)
            self._e2_hist = []
        if len(self._e2_hist) >= 2:
            st_g = self._e2_hist[-2]
            P.wait("sync", f"out_st{buf}", st_g)
        P.dma("sync", s.x22[buf].rearrange("p (n b) -> p n b", b=T2),
              s.x2_sp[:, off:off + T2].rearrange("(n p) b -> p n b", p=128),
              inc=f"e2_ldx{buf}")
        nldx = P.cnt[f"e2_ldx{buf}"]
        P.wait("vector", f"e2_ldx{buf}", nldx)
        if t2 >= 1:
            P.wait("vector", "e2_pe", self._eB_stats)
        nx2b = P.op("vector", lambda e, buf=buf: e.tensor_copy(
            s.x2bB, s.x22[buf]), inc="e2_ve")
        P.wait("scalar", "e2_ve", nx2b)
        nsq2 = P.op("scalar", lambda e: e.activation(
            s.sq2B, s.x2bB, AF.Square), inc="e2_se")
        P.wait("tensor", "e2_ve", nx2b)
        P.wait("tensor", "e2_se", nsq2)
        if t2 >= 1:
            P.wait("tensor", "e2_ve", self._eB_rows)
        for n in range(NCH):
            P.op("tensor", lambda e, n=n: e.matmul(
                s.pb[2][0:1, 0:T2], s.onesb, s.x2bB[:, n * T2:(n + 1) * T2],
                start=(n == 0), stop=(n == NCH - 1)))
        for n in range(NCH):
            P.op("tensor", lambda e, n=n: e.matmul(
                s.pb[3][0:1, 0:T2], s.onesb, s.sq2B[:, n * T2:(n + 1) * T2],
                start=(n == 0), stop=(n == NCH - 1)),
                inc=("e2_pe" if n == NCH - 1 else None))
        self._eB_stats = P.cnt["e2_pe"]
        P.wait("vector", "e2_pe", self._eB_stats)
        P.op("vector", lambda e: e.tensor_scalar_mul(
            s.rowsA[:, 0:T2], s.pb[2][0:1, 0:T2], 1.0 / D))
        P.op("vector", lambda e: e.tensor_tensor(
            s.rowsE[:, 0:T2], s.rowsA[:, 0:T2], s.rowsA[:, 0:T2], ALU.mult))
        P.op("vector", lambda e: e.scalar_tensor_tensor(
            s.rowsE[:, 0:T2], s.pb[3][0:1, 0:T2], 1.0 / D,
            s.rowsE[:, 0:T2], ALU.mult, ALU.subtract), inc="e2_ve")
        self._eB_rows = P.cnt["e2_ve"]
        P.wait("scalar", "e2_ve", self._eB_rows)
        P.op("scalar", lambda e: e.activation(
            s.rowsE[:, 0:T2], s.rowsE[:, 0:T2], AF.Ln,
            scale=1.0 + 1e-5, bias=1e-10))
        nr = P.op("scalar", lambda e: e.activation(
            s.rowsE[:, 0:T2], s.rowsE[:, 0:T2], AF.Exp, scale=-0.5), inc="e2_se")
        P.wait("tensor", "e2_se", nr)
        if t2 >= 1:
            P.wait("tensor", "e2_ve", self._eB_u)
        P.op("tensor", lambda e: e.matmul(
            s.pb[4][:, 0:T2], s.onesf, s.rowsA[:, 0:T2], start=True, stop=True))
        nbc = P.op("tensor", lambda e: e.matmul(
            s.pb[5][:, 0:T2], s.onesf, s.rowsE[:, 0:T2], start=True, stop=True),
            inc="e2_pe")
        P.wait("vector", "e2_pe", nbc)
        mrep = s.pb[4][:, 0:T2].unsqueeze(1).broadcast_to([128, NCH, T2])
        rrep = s.pb[5][:, 0:T2].unsqueeze(1).broadcast_to([128, NCH, T2])
        P.op("vector", lambda e, buf=buf, mrep=mrep: e.tensor_tensor(
            s.x2bB.rearrange("p (n b) -> p n b", b=T2),
            s.x22[buf].rearrange("p (n b) -> p n b", b=T2), mrep, ALU.subtract))
        P.op("vector", lambda e, rrep=rrep: e.tensor_tensor(
            s.sq2B.rearrange("p (n b) -> p n b", b=T2),
            s.x2bB.rearrange("p (n b) -> p n b", b=T2), rrep, ALU.mult),
            inc="e2_ve")
        self._eB_u = P.cnt["e2_ve"]
        if t2 >= 1:
            P.wait("vector", "e2_pe", self._eB_ffn)
        for n in range(NCH):
            P.op("vector", lambda e, n=n: e.tensor_scalar(
                s.xnB[:, n * T2:(n + 1) * T2], s.sq2B[:, n * T2:(n + 1) * T2],
                s.cvec[:, 3 * NC + n:3 * NC + n + 1],
                s.cvec[:, 4 * NC + n:4 * NC + n + 1],
                ALU.mult, ALU.add), inc=("e2_ve" if n == NCH - 1 else None))
        nxn = P.cnt["e2_ve"]
        P.wait("tensor", "e2_ve", nxn)
        nfs = {}
        for f in range(NCH):
            pob = s.pb[f % 2]
            if f >= 2:
                P.wait("tensor", "e2_ve", nfs[f - 2])
            for k in range(NCH):
                P.op("tensor", lambda e, k=k, f=f, pob=pob: e.matmul(
                    pob[:, 0:T2], s.wres[:, k * D + f * 128:k * D + (f + 1) * 128],
                    s.xnB[:, k * T2:(k + 1) * T2],
                    start=(k == 0), stop=(k == NCH - 1)),
                    inc=("e2_pe" if k == NCH - 1 else None))
            npe = P.cnt["e2_pe"]
            P.wait("vector", "e2_pe", npe)
            if f == 0:
                P.wait("vector", f"e2_ldx{buf}", nldx)
            P.op("vector", lambda e, f=f, buf=buf, pob=pob:
                 e.scalar_tensor_tensor(
                     s.x22[buf][:, f * T2:(f + 1) * T2], pob[:, 0:T2],
                     s.gscale[:, f:f + 1], s.x22[buf][:, f * T2:(f + 1) * T2],
                     ALU.mult, ALU.add))
            nev = P.op("vector", lambda e, f=f, buf=buf: e.tensor_scalar_add(
                s.x22[buf][:, f * T2:(f + 1) * T2],
                s.x22[buf][:, f * T2:(f + 1) * T2],
                s.bprime[:, f:f + 1]), inc="e2_ve")
            nfs[f] = nev
        self._eB_ffn = P.cnt["e2_pe"]
        nlast = nfs[NCH - 1]
        P.wait("gpsimd", "e2_ve", nlast)
        P.dma("gpsimd", s.out_d[:, off:off + T2].rearrange("(n p) b -> p n b", p=128),
              s.x22[buf].rearrange("p (n b) -> p n b", b=T2), inc=f"out_st{buf}")
        self._e2_hist.append(P.cnt[f"out_st{buf}"])

    def emit_sw_dumps(self):
        P, s = self.P, self
        P.wait("vector", "sw_ve", self._sw_done)
        P.wait("sync", "sw_ve", self._sw_done)
        P.dma("sync", s.sw_dp[:, 0:16], s.gscale, inc="dbg_st")
        P.dma("sync", s.sw_dp[:, 16:32], s.bprime, inc="dbg_st")
        P.dma("sync", s.sw_dp[:, 32:36], s.sclc, inc="dbg_st")
        P.wait("sync", "sw_se", self._sw_se)
        P.dma("sync", s.wres_dp, s.wres, inc="dbg_st")
        P.wait("sync", "dbg_st")

    def emit_dumps(self):
        P, s = self.P, self
        P.wait("sync", "b0_se", self.v_done)
        P.dma("sync", s.ln1_dp, s.ln1, inc="dbg_st")
        P.dma("sync", s.vtm_dp, s.vtm, inc="dbg_st")
        P.wait("sync", "dbg_st")


def _stage_weights(L):
    wqkvT = np.ascontiguousarray(L["L0_attn_w_qkv"].T)
    out = {}
    wqh = np.empty((H, 128, D), np.float32)
    wkh = np.empty((H, 128, D), np.float32)
    for h in range(H):
        wqh[h] = wqkvT[:, h * 128:(h + 1) * 128].reshape(NC, 128, 128) \
            .transpose(1, 0, 2).reshape(128, D)
        wkh[h] = wqkvT[:, D + h * 128:D + (h + 1) * 128].reshape(NC, 128, 128) \
            .transpose(1, 0, 2).reshape(128, D)
    out["wq"] = wqh
    out["wk"] = wkh
    out["wv"] = np.ascontiguousarray(wqkvT[:, 2 * D:3 * D].reshape(NC, 128, D))
    out["wg"] = np.ascontiguousarray(L["L0_attn_w_gate"].T.reshape(NC, 128, D))
    out["wo"] = np.ascontiguousarray(L["L0_attn_w_out"].T.reshape(NC, 128, D))
    out["wf"] = np.ascontiguousarray(L["L0_ffn_weight"].T.reshape(NC, 128, D))
    fm = lambda v: np.ascontiguousarray(np.asarray(v, np.float32).reshape(NC, 128).T)
    out["vecs"] = np.ascontiguousarray(np.concatenate(
        [fm(L["L0_attn_norm_w"]), fm(L["L0_ffn_gamma"]), fm(L["L0_ffn_beta"]),
         fm(L["L0_ffn_ln_weight"]), fm(L["L0_ffn_ln_bias"])], axis=1))
    out["slope"] = np.ascontiguousarray(
        np.asarray(L["L0_attn_slope_rate"], np.float32).reshape(1, H))
    ar = np.arange(1, BLK + 1, dtype=np.float32)
    out["arow"] = np.ascontiguousarray(np.broadcast_to(ar, (128, BLK)))
    out["krow"] = np.ascontiguousarray(np.broadcast_to(BLK - ar, (128, BLK)))
    p = np.arange(128, dtype=np.float32).reshape(128, 1)
    i2 = np.arange(2, dtype=np.float32).reshape(1, 2)
    out["krowt"] = np.ascontiguousarray(255.0 - i2 * 128.0 - p)
    kp = np.arange(2 * 128).reshape(2 * 128, 1).astype(np.float32)
    qp = np.arange(BLK).reshape(1, BLK).astype(np.float32)
    idxT = qp - kp
    idxT[idxT < 0] = BIG
    out["idxT"] = np.ascontiguousarray(idxT.astype(np.float32))
    out["identw"] = np.eye(128, dtype=np.float32)
    return out


_CACHE = {}


def _get_nc(chunks, use_coll, dbg=False):
    key = (chunks, use_coll, dbg)
    if key not in _CACHE:
        k = Ker(chunks, use_coll, dbg)
        _CACHE[key] = (k.build(), k.sbuf_left)
    return _CACHE[key][0]


def kernel(**inputs):
    x = np.asarray(inputs["L0_attn_hidden_states"], np.float32)
    B, N = x.shape[0], x.shape[1]
    staged = _stage_weights(inputs)

    use_v1 = True
    if use_v1:
        nc = _get_nc(1, True)
        in_maps = []
        for c in range(8):
            b, half = c // 2, c % 2
            m = dict(staged)
            m["x"] = np.ascontiguousarray(x[b, half * TOKS:(half + 1) * TOKS, :].T)
            m["mask01"] = np.full((128, 1), float(half), np.float32)
            in_maps.append(m)
        res = run_bass_kernel_spmd(nc, in_maps, core_ids=list(range(8)))
        out = np.empty((B, N, D), np.float32)
        for c in range(8):
            b, half = c // 2, c % 2
            out[b, half * TOKS:(half + 1) * TOKS, :] = res.results[c]["out"].T
        return out
    else:
        nc = _get_nc(2, False)
        in_maps = []
        for c in range(8):
            b = c % 4
            m = dict(staged)
            m["x"] = np.ascontiguousarray(x[b].T)
            m["mask01"] = np.zeros((128, 1), np.float32)
            in_maps.append(m)
        res = run_bass_kernel_spmd(nc, in_maps, core_ids=list(range(8)))
        out = np.empty((B, N, D), np.float32)
        for b in range(4):
            out[b] = res.results[b]["out"].T
        return out
